# version 12
# baseline (speedup 1.0000x reference)
import numpy as np
import ml_dtypes

import concourse.bass as bass
import concourse.mybir as mybir
from concourse.bass_utils import run_bass_kernel_spmd

F32 = mybir.dt.float32
BF16 = mybir.dt.bfloat16
I32 = mybir.dt.int32
AF = mybir.ActivationFunctionType
ALU = mybir.AluOpType

BF = ml_dtypes.bfloat16

H = 4096
S = 4096
B = 2
NTOK = 1024
NT = 8
NH, NKV, HD = 32, 8, 128
SHIFT = 40.0
PADEXT = [32, 28, 24, 20, 16, 12, 8, 4]



def core_tiles(cc):
    tiles = []
    for p in [cc, cc + 4, cc + 8, cc + 12]:
        tiles += [p, 31 - p]
    return sorted(tiles, reverse=True)


def core_rows(c):
    b, cc = divmod(c, 4)
    rows = []
    for t in core_tiles(cc):
        rows.extend(range(b * S + t * 128, b * S + t * 128 + 128))
    return np.asarray(rows, dtype=np.int64)


def batch_perm():
    pos = np.empty(S, np.int64)
    for cc in range(4):
        tiles = core_tiles(cc)
        for j, t in enumerate(tiles):
            pos[t * 128:(t + 1) * 128] = cc * NTOK + j * 128 + np.arange(128)
    return pos


def mobius_matrix():
    M = np.zeros((16, 16), np.float32)
    for s in range(16):
        for t in range(16):
            if (t & s) == t:
                M[t, s] = (-1.0) ** (bin(s).count("1") - bin(t).count("1"))
    return M


def rope_tables(rows_in_seq):
    inv = 1.0 / (1e4 ** (np.arange(0, HD, 2) / HD))
    freqs = rows_in_seq[:, None].astype(np.float64) * inv[None, :]
    emb = np.concatenate([freqs, freqs], -1)
    return (np.ascontiguousarray(np.cos(emb).astype(np.float32).T),
            np.ascontiguousarray(np.sin(emb).astype(np.float32).T))


def causal_masks(cc):
    tiles = core_tiles(cc)
    out = np.zeros((32, 128, 128), np.float32)
    for k in range(32):
        g = tiles[7 - k // 4]
        if k < g:
            out[k] = 1.0
        elif k == g:
            kpos = k * 128 + np.arange(128)
            gq = g * 128 + np.arange(128)
            out[k] = (kpos[:, None] <= gq[None, :]).astype(np.float32)
    return np.ascontiguousarray(out.astype(BF))


def tile_wT(wT, part=128, free=1024):
    C, Fd = wT.shape
    t = wT.reshape(C // part, part, Fd // free, free)
    return np.ascontiguousarray(t.transpose(2, 0, 1, 3))



class Sem:
    __slots__ = ("h", "v")

    def __init__(self, h):
        self.h = h
        self.v = 0

    def inc(self, n=1):
        self.v += n
        return self.v


def mm_split(t, out_ap, lhsT_ap, rhs_ap, n, start, stop, nmax=512):
    o = 0
    r = None
    while o < n:
        w = min(nmax, n - o)
        r = t.matmul(out_ap[:, o:o + w], lhsT_ap, rhs_ap[:, o:o + w],
                     start=start, stop=stop)
        o += w
    return r



def build_launch1():
    nc = bass.Bass("TRN2", target_bir_lowering=False)
    x_in = nc.dram_tensor("x", [NTOK, H], F32, kind="ExternalInput").ap()
    wqkv_in = nc.dram_tensor("wqkv_t", [24, 128, H], F32, kind="ExternalInput").ap()
    bqkv_in = nc.dram_tensor("bqkv", [1, 3072], F32, kind="ExternalInput").ap()
    qw_in = nc.dram_tensor("qw", [4, 4, 128, 128], I32, kind="ExternalInput").ap()
    lut4T_in = nc.dram_tensor("lut4T", [16, 512], F32, kind="ExternalInput").ap()
    mob_in = nc.dram_tensor("mob", [16, 16], F32, kind="ExternalInput").ap()
    ident_in = nc.dram_tensor("ident", [128, 128], BF16, kind="ExternalInput").ap()
    identf_in = nc.dram_tensor("identf", [128, 128], F32, kind="ExternalInput").ap()

    xnT_out = nc.dram_tensor("xnT", [32, 128, NTOK], BF16, kind="ExternalOutput").ap()
    qkvT_out = nc.dram_tensor("qkvT", [24, 128, NTOK], BF16, kind="ExternalOutput").ap()
    wlT_out = nc.dram_tensor("wlT", [32, 128, 512], BF16, kind="ExternalOutput").ap()

    from contextlib import ExitStack
    with ExitStack() as ctx:
        def sb(name, shape, dt):
            return ctx.enter_context(nc.sbuf_tensor(name, shape, dt))

        def psum(name, shape, dt=F32):
            return ctx.enter_context(nc.psum_tensor(name, shape, dt))

        def sem(name):
            return ctx.enter_context(nc.semaphore(name))

        xt = sb("xt", [128, 2 * H], F32)
        bn = sb("bn", [128, 52], F32)
        rstd = sb("rstd", [128, 1], F32)
        xnb = sb("xnb", [128, H], BF16)
        xnT = sb("xnTs", [128, 32 * NTOK], BF16)
        ident = sb("idents", [128, 128], BF16)
        identf = sb("identfs", [128, 128], F32)
        ones = sb("ones", [1, NTOK], BF16)
        consts = sb("consts", [128, 4], F32)
        wbuf = sb("wbuf", [128, 2 * 4096], BF16)
        bqb = sb("bqb", [1, 3072], BF16)
        qkvTs = sb("qkvTs", [128, 2 * NTOK], BF16)
        qws = sb("qws", [128, 16 * 128], I32)
        ibits = sb("ibits", [128, 2048], I32)
        bits = sb("bits", [128, 4 * 2048], BF16)
        lut4T = sb("lut4Ts", [16, 512], F32)
        mob = sb("mobs", [16, 16], F32)
        coef = sb("coef", [128, 64], F32)
        p01 = sb("p01", [128, 2048], F32)
        dq1 = sb("dq1", [128, 2048], F32)
        dq2 = sb("dq2", [128, 2048], F32)
        wacc = sb("wacc", [128, 2048], F32)
        wstage = sb("wstage", [128, 2048], BF16)
        ps_q0 = psum("ps_q0", [128, 1024], F32)
        ps_q1 = psum("ps_q1", [128, 1024], F32)
        ps_t0 = psum("ps_t0", [128, 1024], BF16)
        ps_t1 = psum("ps_t1", [128, 1024], BF16)
        ps_f0 = psum("ps_f0", [128, 512], F32)
        ps_f1 = psum("ps_f1", [128, 512], F32)
        h_ld = sem("ld")
        h_gw = sem("gw")
        h_mm = sem("mm")
        h_dv = sem("dv")
        h_ev = sem("ev")
        h_ac = sem("ac")
        h_st = sem("st")
        blk = ctx.enter_context(nc.Block())
        ld, gw, mm, dv, ev, ac, st = (Sem(h_ld), Sem(h_gw), Sem(h_mm), Sem(h_dv),
                                      Sem(h_ev), Sem(h_ac), Sem(h_st))
        pst = [ps_t0, ps_t1]
        pst_free = [0, 0]

        def c0(e, _dpld=ld.v):
            _cld = _dpld
            e.wait_ge(ld.h, _cld)
            e.dma_start(out=ident[:], in_=ident_in[:]).then_inc(ld.h, 16)
            _cld += 16
            e.wait_ge(ld.h, _cld)
            e.dma_start(out=identf[:], in_=identf_in[:]).then_inc(ld.h, 16)
            _cld += 16
            e.wait_ge(ld.h, _cld)
            e.dma_start(out=lut4T[:], in_=lut4T_in[:]).then_inc(ld.h, 16)
            _cld += 16
            e.wait_ge(ld.h, _cld)
            e.dma_start(out=mob[:], in_=mob_in[:]).then_inc(ld.h, 16)
            _cld += 16
        blk.sync(c0)
        LD_CONST = ld.inc(64)

        def v_ones(e):
            e.memset(ones[:], 1.0)
            e.memset(consts[:, 0:1], 1e-5)
            e.memset(consts[:, 1:2], 0.0).then_inc(dv.h, 1)
        blk.vector(v_ones)
        ONES_DV = dv.inc()

        def g0(e, _dpgw=gw.v):
            _cgw = _dpgw
            e.wait_ge(gw.h, _cgw)
            e.dma_start(out=bqb[:], in_=bqkv_in[:]).then_inc(gw.h, 16)
            _cgw += 16
            for ot in range(4):
                for pl in range(4):
                    e.wait_ge(gw.h, _cgw)
                    e.dma_start(out=qws[:, (ot * 4 + pl) * 128:(ot * 4 + pl + 1) * 128],
                                in_=qw_in[pl, ot, :, :]).then_inc(gw.h, 16)
                    _cgw += 16
        blk.gpsimd(g0)
        gw.inc(16 + 256)
        GW_BQ = GW_QW = gw.v

        ps_c_free = 0
        COEF_EV = 0
        for ot in range(4):
            def mobmm(e, ot=ot, fr=ps_c_free, need=LD_CONST):
                e.wait_ge(ld.h, need)
                if fr:
                    e.wait_ge(ev.h, fr)
                e.matmul(ps_f0[:, 0:16], lut4T[:, ot * 128:(ot + 1) * 128], mob[:],
                         start=True, stop=True).then_inc(mm.h, 1)
            t_mm = mm.inc()
            blk.tensor(mobmm)

            def mobev(e, ot=ot, t=t_mm):
                e.wait_ge(mm.h, t)
                e.tensor_copy(coef[:, ot * 16:(ot + 1) * 16], ps_f0[:, 0:16]) \
                    .then_inc(ev.h, 1)
            ps_c_free = COEF_EV = ev.inc()
            blk.vector(mobev)

        def ln_ld0(e, _dpld=ld.v):
            _cld = _dpld
            e.wait_ge(ld.h, _cld)
            e.dma_start(out=xt[:, 0:H], in_=x_in[0:128, :]).then_inc(ld.h, 16)
            _cld += 16
            e.wait_ge(ld.h, _cld)
            e.dma_start(out=xt[:, H:2 * H], in_=x_in[128:256, :]).then_inc(ld.h, 16)
            _cld += 16
        blk.sync(ln_ld0)
        x_loaded = [ld.inc(16), ld.inc(16)]
        xnb_free_mm = 0

        for i in range(NT):
            sl = (i % 2) * H

            def ln_stats(e, sl=sl, need=x_loaded[i], _sp=dv.v):
                e.wait_ge(ld.h, need)
                e.wait_ge(dv.h, _sp)
                xs = xt[:, sl:sl + H]
                r = None
                for k in range(8):
                    r = e.bn_stats(bn[:, 6 * k:6 * k + 6], xs[:, 512 * k:512 * (k + 1)])
                r.then_inc(dv.h, 1)
            t1 = dv.inc()
            blk.vector(ln_stats)

            def ln_aggr(e, t=t1):
                e.wait_ge(dv.h, t)
                e.bn_aggr(bn[:, 48:50], bn[:, 0:48]).then_inc(dv.h, 1)
            t2 = dv.inc()
            blk.vector(ln_aggr)

            def ln_rstd(e, t=t2, a0=ac.v):
                e.wait_ge(dv.h, t)
                e.activation(bn[:, 50:51], bn[:, 49:50], AF.Ln, bias=consts[:, 0:1], scale=1.0) \
                    .then_inc(ac.h, 1)
                e.wait_ge(ac.h, a0 + 1)
                e.activation(rstd[:], bn[:, 50:51], AF.Exp, bias=consts[:, 1:2], scale=-0.5) \
                    .then_inc(ac.h, 1)
            t3 = ac.inc(2)
            blk.scalar(ln_rstd)

            def ln_norm(e, sl=sl, t=t3, xf=xnb_free_mm):
                e.wait_ge(ac.h, t)
                if xf:
                    e.wait_ge(mm.h, xf)
                xs = xt[:, sl:sl + H]
                e.tensor_scalar(xnb[:], xs, bn[:, 48:49], rstd[:],
                                ALU.subtract, ALU.mult).then_inc(dv.h, 1)
            t4 = dv.inc()
            blk.vector(ln_norm)

            if i + 2 < NT:
                def ld_next(e, i=i, t=t4, _dpld=ld.v):
                    _cld = _dpld
                    e.wait_ge(dv.h, t)
                    e.wait_ge(ld.h, _cld)
                    e.dma_start(out=xt[:, (i % 2) * H:(i % 2) * H + H],
                                in_=x_in[(i + 2) * 128:(i + 3) * 128, :]) \
                        .then_inc(ld.h, 16)
                    _cld += 16
                blk.sync(ld_next)
                x_loaded.append(ld.inc(16))

            for cb in range(32):
                b = cb % 2

                def tns(e, cb=cb, b=b, fr=pst_free[b], t=t4):
                    e.wait_ge(dv.h, t)
                    if fr:
                        e.wait_ge(ev.h, fr)
                    e.transpose(pst[b][:, 0:128], xnb[:, cb * 128:(cb + 1) * 128], ident[:]) \
                        .then_inc(mm.h, 1)
                rd = mm.inc()
                blk.tensor(tns)

                def dve(e, rd=rd, b=b, i=i, cb=cb):
                    e.wait_ge(mm.h, rd)
                    e.tensor_copy(xnT[:, cb * NTOK + i * 128:cb * NTOK + (i + 1) * 128],
                                  pst[b][:, 0:128]).then_inc(ev.h, 1)
                pst_free[b] = ev.inc()
                blk.vector(dve)
            xnb_free_mm = mm.v
        XNT_EV = ev.v

        def xnT_store(e, t=XNT_EV, _dpst=st.v):
            _cst = _dpst
            e.wait_ge(ev.h, t)
            for cb in range(32):
                e.wait_ge(st.h, _cst)
                e.dma_start(out=xnT_out[cb, :, :],
                            in_=xnT[:, cb * NTOK:(cb + 1) * NTOK]).then_inc(st.h, 16)
                _cst += 16
        blk.sync(xnT_store)
        st.inc(16 * 32)

        psf = [ps_f0, ps_f1]
        psf_free = [COEF_EV, 0]
        wstage_free_st = 0
        qkv_mm_done = []
        qkv_ac_done = []
        qkv_st_done = []
        deq_slot = {8 + 2 * k: k for k in range(8)}

        def emit_qkv(ft):
            def wload(e, ft=ft, prev=(qkv_mm_done[ft - 2] if ft >= 2 else 0), _dpgw=gw.v):
                _cgw = _dpgw
                if prev:
                    e.wait_ge(mm.h, prev)
                e.wait_ge(gw.h, _cgw)
                e.dma_start(out=wbuf[:, (ft % 2) * 4096:(ft % 2 + 1) * 4096],
                            in_=wqkv_in[ft, :, :]).then_inc(gw.h, 16)
                _cgw += 16
            blk.gpsimd(wload)
            wv = gw.inc(16)
            pso = [ps_q0, ps_q1][ft % 2]

            def qmm(e, ft=ft, wv=wv, pso=pso, xv=XNT_EV, dvo=ONES_DV, acb=GW_BQ,
                    evprev=(qkv_ac_done[ft - 2] if ft >= 2 else 0)):
                e.wait_ge(gw.h, wv)
                e.wait_ge(ev.h, xv)
                e.wait_ge(dv.h, dvo)
                e.wait_ge(gw.h, acb)
                if evprev:
                    e.wait_ge(ac.h, evprev)
                e.matmul(pso[:, 0:512], bqb[:, ft * 128:(ft + 1) * 128],
                         ones[:, 0:512], start=True, stop=False)
                e.matmul(pso[:, 512:1024], bqb[:, ft * 128:(ft + 1) * 128],
                         ones[:, 512:1024], start=True, stop=False)
                for c in range(32):
                    lhsT = wbuf[:, (ft % 2) * 4096 + c * 128:(ft % 2) * 4096 + (c + 1) * 128]
                    rhs = xnT[:, c * NTOK:(c + 1) * NTOK]
                    r = mm_split(e, pso, lhsT, rhs, NTOK, start=False, stop=(c == 31))
                r.then_inc(mm.h, 1)
            qkv_mm_done.append(mm.inc())
            blk.tensor(qmm)

            def qev(e, ft=ft, pso=pso, t=qkv_mm_done[ft],
                    stp=(qkv_st_done[ft - 2] if ft >= 2 else 0)):
                e.wait_ge(mm.h, t)
                if stp:
                    e.wait_ge(st.h, stp)
                e.copy(qkvTs[:, (ft % 2) * NTOK:(ft % 2 + 1) * NTOK], pso[:]) \
                    .then_inc(ac.h, 1)
            qkv_ac_done.append(ac.inc())
            blk.scalar(qev)

            def qst(e, ft=ft, t=qkv_ac_done[ft], _dpst=st.v):
                _cst = _dpst
                e.wait_ge(ac.h, t)
                e.wait_ge(st.h, _cst)
                e.dma_start(out=qkvT_out[ft, :, :],
                            in_=qkvTs[:, (ft % 2) * NTOK:(ft % 2 + 1) * NTOK]) \
                    .then_inc(st.h, 16)
                _cst += 16
            blk.sync(qst)
            qkv_st_done.append(st.inc(16))

        def emit_deq_block(k):
            nonlocal wstage_free_st
            ot, ih = k // 2, k % 2

            if True:
                def extract(e, ot=ot, ih=ih, need=GW_QW, ce=COEF_EV, base=dv.v):
                    e.wait_ge(gw.h, need)
                    e.wait_ge(ev.h, ce)
                    t = base
                    e.wait_ge(dv.h, t)
                    for pl in range(4):
                        qv = qws[:, (ot * 4 + pl) * 128 + ih * 64:
                                 (ot * 4 + pl) * 128 + ih * 64 + 64]
                        r = None
                        for j in range(32):
                            r = e.tensor_scalar(
                                ibits.ap().rearrange("p (w j) -> p w j", j=32)[:, :, j],
                                qv, j, 1, ALU.logical_shift_right, ALU.bitwise_and)
                        r.then_inc(dv.h, 1)
                        t += 1
                        e.wait_ge(dv.h, t)
                        e.tensor_copy(bits[:, pl * 2048:(pl + 1) * 2048], ibits[:]) \
                            .then_inc(dv.h, 1)
                        t += 1
                        e.wait_ge(dv.h, t)
                dv.inc(8)
                blk.vector(extract)

                def multilin(e, ot=ot, base=dv.v):
                    t = base
                    e.wait_ge(dv.h, t)
                    bb = [bits[:, pl * 2048:(pl + 1) * 2048] for pl in range(4)]

                    def step(r):
                        nonlocal t
                        t += 1
                        r.then_inc(dv.h, 1)
                        e.wait_ge(dv.h, t)

                    def a(s):
                        return coef[:, ot * 16 + s:ot * 16 + s + 1]

                    def emit_G(out, T):
                        step(e.tensor_scalar(out, bb[0], a(T + 1), a(T),
                                             ALU.mult, ALU.add))
                        step(e.scalar_tensor_tensor(out, bb[1], a(T + 2), out,
                                                    ALU.mult, ALU.add))
                        step(e.scalar_tensor_tensor(out, p01[:], a(T + 3), out,
                                                    ALU.mult, ALU.add))

                    step(e.tensor_tensor(p01[:], bb[0], bb[1], ALU.mult))
                    emit_G(dq2[:], 12)
                    step(e.tensor_tensor(dq2[:], bb[2], dq2[:], ALU.mult))
                    emit_G(dq1[:], 8)
                    step(e.tensor_tensor(dq1[:], dq1[:], dq2[:], ALU.add))
                    step(e.tensor_tensor(dq1[:], bb[3], dq1[:], ALU.mult))
                    emit_G(dq2[:], 4)
                    step(e.tensor_tensor(dq2[:], bb[2], dq2[:], ALU.mult))
                    emit_G(wacc[:], 0)
                    step(e.tensor_tensor(wacc[:], wacc[:], dq2[:], ALU.add))
                    step(e.tensor_tensor(wacc[:], wacc[:], dq1[:], ALU.add))
                n_steps = 19
                blk.vector(multilin)
                ML_DONE = dv.inc(n_steps)

                for cb in range(16):
                    b = cb % 2

                    def tns(e, cb=cb, b=b, fr=psf_free[b], t=ML_DONE, ws=wstage_free_st):
                        e.wait_ge(dv.h, t)
                        if ws:
                            e.wait_ge(st.h, ws)
                        if fr:
                            e.wait_ge(ev.h, fr)
                        e.transpose(psf[b][:, 0:128], wacc[:, cb * 128:(cb + 1) * 128], identf[:]) \
                            .then_inc(mm.h, 1)
                    rd = mm.inc()
                    blk.tensor(tns)

                    def dve(e, rd=rd, b=b, cb=cb):
                        e.wait_ge(mm.h, rd)
                        e.tensor_copy(wstage[:, cb * 128:(cb + 1) * 128], psf[b][:, 0:128]) \
                            .then_inc(ev.h, 1)
                    psf_free[b] = ev.inc()
                    blk.vector(dve)
                WS_EV = ev.v

                def wl_store(e, ot=ot, ih=ih, t=WS_EV, _dpst=st.v):
                    _cst = _dpst
                    e.wait_ge(ev.h, t)
                    for cb in range(16):
                        ic = ih * 16 + cb
                        e.wait_ge(st.h, _cst)
                        e.dma_start(out=wlT_out[ic, :, ot * 128:(ot + 1) * 128],
                                    in_=wstage[:, cb * 128:(cb + 1) * 128]) \
                            .then_inc(st.h, 16)
                        _cst += 16
                blk.sync(wl_store)
                wstage_free_st = st.inc(16 * 16)

        for ft in range(24):
            emit_qkv(ft)
            if ft in deq_slot:
                emit_deq_block(deq_slot[ft])

        def fin(e, t=st.v):
            e.wait_ge(st.h, t)
        blk.gpsimd(fin)

    return nc




def build_launch2():
    nc = bass.Bass("TRN2", target_bir_lowering=False)
    x_in = nc.dram_tensor("x", [NTOK, H], F32, kind="ExternalInput").ap()
    xnT_in = nc.dram_tensor("xnT", [32, 128, NTOK], BF16, kind="ExternalInput").ap()
    qT_in = nc.dram_tensor("qT", [8, 128, NTOK], BF16, kind="ExternalInput").ap()
    kt_in = nc.dram_tensor("kt", [32, 128, 1024], BF16, kind="ExternalInput").ap()
    v_in = nc.dram_tensor("v", [128, 32768], BF16, kind="ExternalInput").ap()
    wl_in = nc.dram_tensor("wl", [8, 128, 16384], BF16, kind="ExternalInput").ap()
    wo_in = nc.dram_tensor("wo", [4, 8, 128, 1024], F32, kind="ExternalInput").ap()
    bo_in = nc.dram_tensor("bo", [1, H], F32, kind="ExternalInput").ap()
    bffn_in = nc.dram_tensor("bffn", [1, H], F32, kind="ExternalInput").ap()
    w1_in = nc.dram_tensor("w1", [48, 128, H], F32, kind="ExternalInput").ap()
    cos_in = nc.dram_tensor("cosT", [128, NTOK], F32, kind="ExternalInput").ap()
    sin_in = nc.dram_tensor("sinT", [128, NTOK], F32, kind="ExternalInput").ap()
    wr_in = nc.dram_tensor("wrT", [128, 256], F32, kind="ExternalInput").ap()
    at_in = nc.dram_tensor("at2", [128, H], F32, kind="ExternalInput").ap()
    brr_in = nc.dram_tensor("brr", [128, H], F32, kind="ExternalInput").ap()
    wb_in = nc.dram_tensor("wb", [8, 128, 16384], F32, kind="ExternalInput").ap()
    bb_in = nc.dram_tensor("bb", [1, H], F32, kind="ExternalInput").ap()
    sel_in = nc.dram_tensor("sel", [8, 128], F32, kind="ExternalInput").ap()
    ident_in = nc.dram_tensor("ident", [128, 128], BF16, kind="ExternalInput").ap()

    x1_out = nc.dram_tensor("x1", [NTOK, H], F32, kind="ExternalOutput").ap()
    g1_out = nc.dram_tensor("g1", [NTOK, H], BF16, kind="ExternalOutput").ap()
    q1T_out = nc.dram_tensor("q1T", [32, 128, NTOK], BF16, kind="ExternalOutput").ap()
    k1T_out = nc.dram_tensor("k1T", [8, 128, NTOK], BF16, kind="ExternalOutput").ap()
    v1T_out = nc.dram_tensor("v1T", [8, 128, NTOK], BF16, kind="ExternalOutput").ap()
    s0_dram = nc.dram_tensor("s0scr", [8, 4, 128, 1024], F32, kind="Internal").ap()

    from contextlib import ExitStack
    with ExitStack() as ctx:
        def sb(name, shape, dt):
            return ctx.enter_context(nc.sbuf_tensor(name, shape, dt))

        def psum(name, shape, dt=F32):
            return ctx.enter_context(nc.psum_tensor(name, shape, dt))

        AR0 = sb("AR0", [128, 32768], BF16)
        AR1 = sb("AR1", [128, 32768], BF16)
        AR2 = sb("AR2", [128, 8192], BF16)
        arena = sb("arena", [128, 8192], F32)
        scr_b = sb("scr_b", [128, 4096], BF16)
        bias = sb("bias", [1, H], BF16)
        rbuf = sb("rbuf", [128, NTOK], F32)
        onesf = sb("onesf", [1, 128], F32)
        ones = sb("ones2", [1, NTOK], BF16)
        ones_col = sb("ones_col", [128, 1], BF16)
        consts = sb("consts2", [128, 4], F32)
        bn = sb("bn2", [128, 52], F32)
        rstd = sb("rstd2", [128, 1], F32)
        wr2 = sb("wr2", [128, 256], BF16)
        sel2 = sb("sel2", [8, 128], BF16)
        rout = sb("rout", [128, 32], F32)
        routb = sb("routb", [128, 8], BF16)
        combT = sb("combT", [8, NTOK], BF16)
        ident = sb("ident2", [128, 128], BF16)
        ps_a = psum("ps2_a", [128, 1024], F32)
        ps_b = psum("ps2_b", [128, 1024], F32)
        ps_c1 = psum("ps2_c1", [128, 512], F32)
        ps_c2 = psum("ps2_c2", [128, 512], F32)
        ps_t0 = psum("ps2_t0", [128, 1024], BF16)
        ps_t1 = psum("ps2_t1", [128, 1024], BF16)
        xtile = arena[:, 0:4096]
        ktb = scr_b[:, 0:2048]
        cosb = rbuf[:]
        sinb = arena[:, 4096:5120]
        rvec = arena[0:1, 7168:8192]
        xnb = scr_b[:, 0:4096]
        qkb = scr_b[:, 0:2048]
        hbuf = scr_b[:, 2048:3072]
        cmat = scr_b[:, 3072:4096]
        at2 = AR2[:, 0:4096]
        brr2 = AR2[:, 4096:8192]
        ld = Sem(ctx.enter_context(nc.semaphore("ld2")))
        gw = Sem(ctx.enter_context(nc.semaphore("gw2")))
        mm = Sem(ctx.enter_context(nc.semaphore("mm2")))
        dv = Sem(ctx.enter_context(nc.semaphore("dv2")))
        ev = Sem(ctx.enter_context(nc.semaphore("ev2")))
        ac = Sem(ctx.enter_context(nc.semaphore("ac2")))
        st = Sem(ctx.enter_context(nc.semaphore("st2")))
        blk = ctx.enter_context(nc.Block())

        def c0(e, _dpld=ld.v):
            _cld = _dpld
            e.wait_ge(ld.h, _cld)
            e.dma_start(out=ident[:], in_=ident_in[:]).then_inc(ld.h, 16)
            _cld += 16
            e.wait_ge(ld.h, _cld)
            e.dma_start(out=AR0[:], in_=v_in[:]).then_inc(ld.h, 16)
            _cld += 16
            e.wait_ge(ld.h, _cld)
            e.dma_start(out=AR2[:].rearrange("p (dc n) -> p dc n", n=NTOK),
                        in_=qT_in[:].rearrange("dc p n -> p dc n")).then_inc(ld.h, 16)
            _cld += 16
        blk.sync(c0)
        LD_A = ld.inc(48)

        def g_c(e, _dpgw=gw.v):
            _cgw = _dpgw
            e.wait_ge(gw.h, _cgw)
            e.dma_start(out=bias[:], in_=bo_in[:]).then_inc(gw.h, 16)
            _cgw += 16
            e.wait_ge(gw.h, _cgw)
            e.dma_start(out=sel2[:], in_=sel_in[:]).then_inc(gw.h, 16)
            _cgw += 16
            e.wait_ge(gw.h, _cgw)
            e.dma_start(out=wr2[:], in_=wr_in[:]).then_inc(gw.h, 16)
            _cgw += 16
        blk.gpsimd(g_c)
        gw.inc(48)
        GW_BO = GW_EMISC = gw.v

        def v_c(e):
            e.memset(ones[:], 1.0)
            e.memset(ones_col[:], 1.0)
            e.memset(onesf[:], 1.0)
            e.memset(consts[:, 0:1], -SHIFT)
            e.memset(consts[:, 1:2], 1e-5)
            e.memset(consts[:, 2:3], 0.0).then_inc(dv.h, 1)
        blk.vector(v_c)
        DV_C = dv.inc()

        MM_A0 = mm.v
        AC_A0 = ac.v

        def a_exps(e, mm0=MM_A0, dvc=DV_C):
            e.wait_ge(dv.h, dvc)
            for kt in range(32):
                e.wait_ge(mm.h, mm0 + 2 * kt + 1)
                pso = [ps_a, ps_b][kt % 2]
                e.activation(AR1[:, kt * 1024:(kt + 1) * 1024], pso[:],
                             AF.Exp, bias=consts[:, 0:1], scale=1.0 / 32.0) \
                    .then_inc(ac.h, 1)
        blk.scalar(a_exps)
        exp_ac = [ac.inc() for _ in range(32)]
        assert exp_ac == [AC_A0 + kt + 1 for kt in range(32)]

        s_mm_done = []
        SUM_MM = 0
        for kt in range(32):
            def k_ld(e, kt=kt, prev=(s_mm_done[kt - 2] if kt >= 2 else 0), _dpld=ld.v):
                _cld = _dpld
                if prev:
                    e.wait_ge(mm.h, prev)
                e.wait_ge(ld.h, _cld)
                e.dma_start(out=ktb[:, (kt % 2) * 1024:(kt % 2 + 1) * 1024],
                            in_=kt_in[kt, :, :]).then_inc(ld.h, 16)
                _cld += 16
            blk.sync(k_ld)
            kv = ld.inc(16)

            pso = [ps_a, ps_b][kt % 2]

            def s_mm(e, kt=kt, kv=kv, need=LD_A, dvc=DV_C, pso=pso,
                     acprev=(exp_ac[kt - 2] if kt >= 2 else 0)):
                e.wait_ge(ld.h, max(kv, need))
                e.wait_ge(dv.h, dvc)
                if acprev:
                    e.wait_ge(ac.h, acprev)
                r = None
                for dc in range(8):
                    lhsT = ktb[:, (kt % 2) * 1024 + dc * 128:(kt % 2) * 1024 + (dc + 1) * 128]
                    rhs = AR2[:, dc * NTOK:(dc + 1) * NTOK]
                    r = mm_split(e, pso, lhsT, rhs, NTOK, start=(dc == 0), stop=(dc == 7))
                r.then_inc(mm.h, 1)
            s_mm_done.append(mm.inc())
            assert s_mm_done[kt] == MM_A0 + 2 * kt + 1
            blk.tensor(s_mm)

            def s_sum(e, kt=kt, t=exp_ac[kt]):
                e.wait_ge(ac.h, t)
                e.matmul(ps_c1[0:1, 0:512], ones_col[:],
                         AR1[:, kt * 1024:kt * 1024 + 512],
                         start=(kt == 0), stop=(kt == 31))
                e.matmul(ps_c2[0:1, 0:512], ones_col[:],
                         AR1[:, kt * 1024 + 512:(kt + 1) * 1024],
                         start=(kt == 0), stop=(kt == 31)).then_inc(mm.h, 1)
            SUM_MM = mm.inc()
            blk.tensor(s_sum)
        AEXP_AC = ac.v

        def recip(e, t=SUM_MM, base=dv.v):
            e.wait_ge(mm.h, t)
            e.wait_ge(dv.h, base)
            e.reciprocal(rvec[:, 0:512], ps_c1[0:1, 0:512]).then_inc(dv.h, 1)
            e.wait_ge(dv.h, base + 1)
            e.reciprocal(rvec[:, 512:1024], ps_c2[0:1, 0:512]).then_inc(dv.h, 1)
        RECIP_DV = dv.inc(2)
        blk.vector(recip)

        def rb_mm(e, t=RECIP_DV):
            e.wait_ge(dv.h, t)
            e.matmul(ps_b[:, 0:512], onesf[:], rvec[:, 0:512], start=True, stop=True)
            e.matmul(ps_b[:, 512:1024], onesf[:], rvec[:, 512:1024],
                     start=True, stop=True).then_inc(mm.h, 1)
        RB_MM = mm.inc()
        blk.tensor(rb_mm)

        def rb_ev(e, t=RB_MM):
            e.wait_ge(mm.h, t)
            e.tensor_copy(rbuf[:], ps_b[:]).then_inc(dv.h, 1)
        RB_DV = dv.inc()
        blk.vector(rb_ev)

        av_ev = []
        for dt in range(8):
            def av(e, dt=dt, t=AEXP_AC, rb=RB_DV,
                   evprev=(av_ev[dt - 1] if dt >= 1 else 0)):
                e.wait_ge(ac.h, t)
                e.wait_ge(dv.h, rb)
                if evprev:
                    e.wait_ge(dv.h, evprev)
                r = None
                for kt in range(32):
                    lhsT = AR0[:, (kt * 8 + dt) * 128:(kt * 8 + dt + 1) * 128]
                    rhs = AR1[:, kt * 1024:(kt + 1) * 1024]
                    r = mm_split(e, ps_b, lhsT, rhs, NTOK,
                                 start=(kt == 0), stop=(kt == 31))
                r.then_inc(mm.h, 1)
            t_av = mm.inc()
            blk.tensor(av)

            def av_e(e, dt=dt, t=t_av, rb=RB_DV):
                e.wait_ge(mm.h, t)
                e.wait_ge(dv.h, rb)
                e.tensor_tensor(AR2[:, dt * 1024:(dt + 1) * 1024], ps_b[:], rbuf[:],
                                ALU.mult).then_inc(dv.h, 1)
            av_ev.append(dv.inc())
            blk.vector(av_e)
        AV_DV = dv.v
        AV_MM = mm.v

        wo_ld = []
        wo_mm_done = []
        wo_ac_done = []
        wo_st_done = []
        for og in range(4):
            def wo_l(e, og=og, avt=AV_MM,
                     prev=(wo_mm_done[8 * (og - 2) + 7] if og >= 2 else 0),
                     _dpgw=gw.v):
                _cgw = _dpgw
                e.wait_ge(mm.h, max(avt, prev))
                for dc in range(8):
                    e.wait_ge(gw.h, _cgw)
                    e.dma_start(out=AR1[:, (og % 2) * 8192 + dc * 1024:
                                        (og % 2) * 8192 + (dc + 1) * 1024],
                                in_=wo_in[og, dc, :, :]).then_inc(gw.h, 16)
                    _cgw += 16
            blk.gpsimd(wo_l)
            wo_ld.append(gw.inc(128))

            for tok in range(8):
                pso = [ps_a, ps_b][tok % 2]

                def wo_mm(e, og=og, tok=tok, pso=pso, wv=wo_ld[og], avt=AV_DV,
                          bo=GW_BO,
                          acp=(wo_ac_done[-2] if len(wo_ac_done) >= 2 else 0)):
                    e.wait_ge(gw.h, max(wv, bo))
                    e.wait_ge(dv.h, avt)
                    if acp:
                        e.wait_ge(ac.h, acp)
                    e.matmul(pso[:, 0:512], ones[0:1, 0:128],
                             bias[0:1, og * 1024:og * 1024 + 512],
                             start=True, stop=False)
                    e.matmul(pso[:, 512:1024], ones[0:1, 0:128],
                             bias[0:1, og * 1024 + 512:(og + 1) * 1024],
                             start=True, stop=False)
                    r = None
                    for dc in range(8):
                        lhsT = AR2[:, dc * 1024 + tok * 128:dc * 1024 + (tok + 1) * 128]
                        rhs = AR1[:, (og % 2) * 8192 + dc * 1024:
                                  (og % 2) * 8192 + (dc + 1) * 1024]
                        r = mm_split(e, pso, lhsT, rhs, NTOK,
                                     start=False, stop=(dc == 7))
                    r.then_inc(mm.h, 1)
                wo_mm_done.append(mm.inc())
                blk.tensor(wo_mm)

                def wo_ev(e, og=og, tok=tok, pso=pso, t=wo_mm_done[-1],
                          stp=(wo_st_done[-2] if len(wo_st_done) >= 2 else 0)):
                    e.wait_ge(mm.h, t)
                    if stp:
                        e.wait_ge(st.h, stp)
                    e.copy(arena[:, (tok % 2) * 1024:(tok % 2 + 1) * 1024], pso[:]) \
                        .then_inc(ac.h, 1)
                wo_ac_done.append(ac.inc())
                blk.scalar(wo_ev)

                def wo_s(e, og=og, tok=tok, t=wo_ac_done[-1], _dpst=st.v):
                    _cst = _dpst
                    e.wait_ge(ac.h, t)
                    e.wait_ge(st.h, _cst)
                    e.dma_start(out=s0_dram[tok, og, :, :],
                                in_=arena[:, (tok % 2) * 1024:(tok % 2 + 1) * 1024]) \
                        .then_inc(st.h, 16)
                    _cst += 16
                blk.sync(wo_s)
                wo_st_done.append(st.inc(16))
        PA_ST = st.v
        PA_MM = mm.v
        PA_AC = ac.v

        def b_ld(e, t=PA_MM, _dpld=ld.v):
            _cld = _dpld
            e.wait_ge(mm.h, t)
            e.wait_ge(ld.h, _cld)
            e.dma_start(out=AR0[:].rearrange("p (cb n) -> p cb n", n=NTOK),
                        in_=xnT_in[:].rearrange("cb p n -> p cb n")).then_inc(ld.h, 16)
            _cld += 16
        blk.sync(b_ld)
        B_XNT = ld.inc(16)

        def b_bias(e, t=PA_AC, _dpgw=gw.v):
            _cgw = _dpgw
            e.wait_ge(ac.h, t)
            e.wait_ge(gw.h, _cgw)
            e.dma_start(out=bias[:], in_=bffn_in[:]).then_inc(gw.h, 16)
            _cgw += 16
        blk.gpsimd(b_bias)
        B_BIAS = gw.inc(16)

        wl_ld = []
        yl_mm_done = []
        yl_ac_done = []
        b_st_done = []
        b_gate_dvs = []
        for og in range(8):
            def wl_l(e, og=og, pa=PA_MM,
                     prev=(yl_mm_done[8 * (og - 2) + 7] if og >= 2 else 0),
                     _dpgw=gw.v):
                _cgw = _dpgw
                e.wait_ge(mm.h, max(pa, prev))
                e.wait_ge(gw.h, _cgw)
                e.dma_start(out=AR1[:, (og % 2) * 16384:(og % 2 + 1) * 16384],
                            in_=wl_in[og, :, :]).then_inc(gw.h, 16)
                _cgw += 16
            blk.gpsimd(wl_l)
            wl_ld.append(gw.inc(16))

            for tok in range(8):
                pso = [ps_a, ps_b][tok % 2]
                aslot = ((og * 8 + tok) % 4) * 512

                def yl_mm(e, og=og, tok=tok, pso=pso, wv=wl_ld[og], xv=B_XNT,
                          bi=B_BIAS, pa=PA_AC,
                          acp=(yl_ac_done[-2] if len(yl_ac_done) >= 2 else 0)):
                    e.wait_ge(ld.h, xv)
                    e.wait_ge(gw.h, max(bi, wv))
                    e.wait_ge(ac.h, max(acp, pa))
                    e.matmul(pso[:, 0:512], ones[0:1, 0:128],
                             bias[0:1, og * 512:(og + 1) * 512],
                             start=True, stop=False)
                    r = None
                    for ic in range(32):
                        lhsT = AR0[:, ic * 1024 + tok * 128:ic * 1024 + (tok + 1) * 128]
                        rhs = AR1[:, (og % 2) * 16384 + ic * 512:
                                  (og % 2) * 16384 + (ic + 1) * 512]
                        r = e.matmul(pso[:, 0:512], lhsT, rhs,
                                     start=False, stop=(ic == 31))
                    r.then_inc(mm.h, 1)
                yl_mm_done.append(mm.inc())
                blk.tensor(yl_mm)

                def yl_ev(e, og=og, tok=tok, pso=pso, aslot=aslot,
                          t=yl_mm_done[-1], pa=PA_ST,
                          gp=(b_gate_dvs[-4] if len(b_gate_dvs) >= 4 else 0)):
                    e.wait_ge(mm.h, t)
                    e.wait_ge(st.h, pa)
                    if gp:
                        e.wait_ge(dv.h, gp)
                    e.activation(arena[:, aslot:aslot + 512],
                                 pso[:, 0:512], AF.Silu, bias=consts[:, 2:3],
                                 scale=1.0).then_inc(ac.h, 1)
                yl_ac_done.append(ac.inc())
                blk.scalar(yl_ev)

                def b_gld(e, og=og, tok=tok, aslot=aslot, sw=PA_ST,
                          gp=(b_gate_dvs[-4] if len(b_gate_dvs) >= 4 else 0),
                          _dpld=ld.v):
                    _cld = _dpld
                    e.wait_ge(st.h, sw)
                    if gp:
                        e.wait_ge(dv.h, gp)
                    e.wait_ge(ld.h, _cld)
                    e.dma_start(out=arena[:, 2048 + aslot:2048 + aslot + 512],
                                in_=s0_dram[tok, og // 2, :,
                                            (og % 2) * 512:(og % 2) * 512 + 512]) \
                        .then_inc(ld.h, 16)
                    _cld += 16
                    e.wait_ge(ld.h, _cld)
                    e.dma_start(out=arena[:, 4096 + aslot:4096 + aslot + 512],
                                in_=x_in[tok * 128:(tok + 1) * 128,
                                         og * 512:(og + 1) * 512]).then_inc(ld.h, 16)
                    _cld += 16
                blk.sync(b_gld)
                gv = ld.inc(32)

                def b_gate(e, og=og, tok=tok, aslot=aslot, t=yl_ac_done[-1],
                           gv=gv, base=dv.v,
                           xp=(b_st_done[-2] if len(b_st_done) >= 2 else 0)):
                    e.wait_ge(ac.h, t)
                    e.wait_ge(ld.h, gv)
                    if xp:
                        e.wait_ge(st.h, xp)
                    gsl = 6144 + (tok % 2) * 512
                    e.tensor_tensor(arena[:, gsl:gsl + 512],
                                    arena[:, 2048 + aslot:2048 + aslot + 512],
                                    arena[:, aslot:aslot + 512],
                                    ALU.mult).then_inc(dv.h, 1)
                    e.wait_ge(dv.h, base + 1)
                    e.tensor_tensor(arena[:, gsl:gsl + 512],
                                    arena[:, 4096 + aslot:4096 + aslot + 512],
                                    arena[:, gsl:gsl + 512], ALU.add) \
                        .then_inc(dv.h, 1)
                b_gate_dv = dv.inc(2)
                b_gate_dvs.append(b_gate_dv)
                blk.vector(b_gate)

                def b_x1s(e, og=og, tok=tok, t=b_gate_dv, _dpst=st.v):
                    _cst = _dpst
                    e.wait_ge(dv.h, t)
                    e.wait_ge(st.h, _cst)
                    gsl = 6144 + (tok % 2) * 512
                    e.dma_start(out=x1_out[tok * 128:(tok + 1) * 128,
                                           og * 512:(og + 1) * 512],
                                in_=arena[:, gsl:gsl + 512]).then_inc(st.h, 16)
                    _cst += 16
                blk.sync(b_x1s)
                b_st_done.append(st.inc(16))
        PB_ST = st.v
        PB_MM = mm.v

        pst = [ps_t0, ps_t1]
        pst_free = [0, 0]
        c_prev_norm = 0
        for i in range(NT):
            def c_l(e, i=i, t=PB_ST, pn=c_prev_norm, _dpld=ld.v):
                _cld = _dpld
                e.wait_ge(st.h, t)
                if pn:
                    e.wait_ge(dv.h, pn)
                e.wait_ge(ld.h, _cld)
                e.dma_start(out=xtile, in_=x1_out[i * 128:(i + 1) * 128, :]) \
                    .then_inc(ld.h, 16)
                _cld += 16
            blk.sync(c_l)
            xl = ld.inc(16)

            def c_stats(e, i=i, xl=xl, prevdv=dv.v):
                e.wait_ge(ld.h, xl)
                e.wait_ge(dv.h, prevdv)
                r = None
                for k in range(8):
                    r = e.bn_stats(bn[:, 6 * k:6 * k + 6], xtile[:, 512 * k:512 * (k + 1)])
                r.then_inc(dv.h, 1)
            t1 = dv.inc()
            blk.vector(c_stats)

            def c_aggr(e, t=t1):
                e.wait_ge(dv.h, t)
                e.bn_aggr(bn[:, 48:50], bn[:, 0:48]).then_inc(dv.h, 1)
            t2 = dv.inc()
            blk.vector(c_aggr)

            def c_rstd(e, t=t2, a0=ac.v):
                e.wait_ge(dv.h, t)
                e.activation(bn[:, 50:51], bn[:, 49:50], AF.Ln, bias=consts[:, 1:2],
                             scale=1.0).then_inc(ac.h, 1)
                e.wait_ge(ac.h, a0 + 1)
                e.activation(rstd[:], bn[:, 50:51], AF.Exp, bias=consts[:, 2:3],
                             scale=-0.5).then_inc(ac.h, 1)
            t3 = ac.inc(2)
            blk.scalar(c_rstd)

            def c_norm(e, t=t3, pm=PB_MM):
                e.wait_ge(ac.h, t)
                e.wait_ge(mm.h, pm)
                e.tensor_scalar(xnb, xtile, bn[:, 48:49], rstd[:],
                                ALU.subtract, ALU.mult).then_inc(dv.h, 1)
            t4 = dv.inc()
            c_prev_norm = t4
            blk.vector(c_norm)

            for cb in range(32):
                b = cb % 2

                def tns(e, cb=cb, b=b, fr=pst_free[b], t=t4):
                    e.wait_ge(dv.h, t)
                    if fr:
                        e.wait_ge(ev.h, fr)
                    e.transpose(pst[b][:, 0:128], xnb[:, cb * 128:(cb + 1) * 128],
                                ident[:]).then_inc(mm.h, 1)
                rd = mm.inc()
                blk.tensor(tns)

                def dve(e, rd=rd, b=b, i=i, cb=cb):
                    e.wait_ge(mm.h, rd)
                    e.tensor_copy(AR0[:, cb * NTOK + i * 128:cb * NTOK + (i + 1) * 128],
                                  pst[b][:, 0:128]).then_inc(ev.h, 1)
                pst_free[b] = ev.inc()
                blk.vector(dve)
            PB_MM = mm.v
        PC_EV = ev.v

        def d_cs(e, t=PB_MM, avt=AV_DV, _dpld=ld.v):
            _cld = _dpld
            e.wait_ge(mm.h, t)
            e.wait_ge(dv.h, avt)
            e.wait_ge(ld.h, _cld)
            e.dma_start(out=cosb, in_=cos_in[:]).then_inc(ld.h, 16)
            _cld += 16
            e.wait_ge(ld.h, _cld)
            e.dma_start(out=sinb, in_=sin_in[:]).then_inc(ld.h, 16)
            _cld += 16
        blk.sync(d_cs)
        D_CS = ld.inc(32)
        d_mm_done = []
        d_ac_done = []
        d_dv_done = []
        qkb_st = []
        for ft in range(48):
            def d_wl(e, ft=ft, prev=(d_mm_done[ft - 2] if ft >= 2 else 0),
                     pa=PB_MM, _dpgw=gw.v):
                _cgw = _dpgw
                e.wait_ge(mm.h, max(prev, pa))
                e.wait_ge(gw.h, _cgw)
                e.dma_start(out=AR2[:, (ft % 2) * 4096:(ft % 2 + 1) * 4096],
                            in_=w1_in[ft, :, :]).then_inc(gw.h, 16)
                _cgw += 16
            blk.gpsimd(d_wl)
            wv = gw.inc(16)

            pso = [ps_a, ps_b][ft % 2]

            def d_mm(e, ft=ft, wv=wv, xv=PC_EV, pso=pso,
                     acp=(d_ac_done[ft - 2] if ft >= 2 else 0)):
                e.wait_ge(gw.h, wv)
                e.wait_ge(ev.h, xv)
                if acp:
                    e.wait_ge(ac.h, acp)
                r = None
                for ic in range(32):
                    lhsT = AR2[:, (ft % 2) * 4096 + ic * 128:(ft % 2) * 4096 + (ic + 1) * 128]
                    rhs = AR0[:, ic * NTOK:(ic + 1) * NTOK]
                    r = mm_split(e, pso, lhsT, rhs, NTOK, start=(ic == 0), stop=(ic == 31))
                r.then_inc(mm.h, 1)
            d_mm_done.append(mm.inc())
            blk.tensor(d_mm)

            if ft < 40:
                def d_ev(e, ft=ft, t=d_mm_done[-1], pso=pso,
                         dvp=(d_dv_done[ft - 2] if ft >= 2 else 0)):
                    e.wait_ge(mm.h, t)
                    if dvp:
                        e.wait_ge(dv.h, dvp)
                    e.copy(arena[:, (ft % 2) * 1024:(ft % 2 + 1) * 1024], pso[:]) \
                        .then_inc(ac.h, 1)
                d_ac_done.append(ac.inc())
                blk.scalar(d_ev)

                def d_rope(e, ft=ft, t=d_ac_done[-1], base=dv.v, cs=D_CS,
                           qp=(qkb_st[ft - 2] if ft >= 2 else 0)):
                    e.wait_ge(ac.h, t)
                    e.wait_ge(ld.h, cs)
                    e.wait_ge(dv.h, base)
                    if qp:
                        e.wait_ge(st.h, qp)
                    sl = (ft % 2) * 1024
                    qk = arena[:, sl:sl + 1024]
                    rt = arena[:, 2048:3072]
                    t1 = arena[:, 3072:4096]
                    tt = base
                    e.tensor_scalar(rt[0:64, :], qk[64:128, :], -1.0, None,
                                    ALU.mult, ALU.bypass).then_inc(dv.h, 1)
                    tt += 1
                    e.wait_ge(dv.h, tt)
                    e.tensor_copy(rt[64:128, :], qk[0:64, :]).then_inc(dv.h, 1)
                    tt += 1
                    e.wait_ge(dv.h, tt)
                    e.tensor_tensor(t1, qk, cosb, ALU.mult).then_inc(dv.h, 1)
                    tt += 1
                    e.wait_ge(dv.h, tt)
                    e.tensor_tensor(rt, rt, sinb, ALU.mult).then_inc(dv.h, 1)
                    tt += 1
                    e.wait_ge(dv.h, tt)
                    e.tensor_tensor(qkb[:, (ft % 2) * 1024:(ft % 2 + 1) * 1024],
                                    t1, rt, ALU.add).then_inc(dv.h, 1)
                d_dv_done.append(dv.inc(5))
                blk.vector(d_rope)

                def d_st(e, ft=ft, t=d_dv_done[-1], _dpst=st.v):
                    _cst = _dpst
                    e.wait_ge(dv.h, t)
                    if ft < 32:
                        e.wait_ge(st.h, _cst)
                        e.dma_start(out=q1T_out[ft, :, :],
                                    in_=qkb[:, (ft % 2) * 1024:(ft % 2 + 1) * 1024]) \
                            .then_inc(st.h, 16)
                        _cst += 16
                    else:
                        e.wait_ge(st.h, _cst)
                        e.dma_start(out=k1T_out[ft - 32, :, :],
                                    in_=qkb[:, (ft % 2) * 1024:(ft % 2 + 1) * 1024]) \
                            .then_inc(st.h, 16)
                        _cst += 16
                blk.sync(d_st)
                qkb_st.append(st.inc(16))
            else:
                def d_evv(e, ft=ft, t=d_mm_done[-1], pso=pso,
                          qp=(qkb_st[ft - 2] if ft >= 2 else 0)):
                    e.wait_ge(mm.h, t)
                    if qp:
                        e.wait_ge(st.h, qp)
                    e.copy(qkb[:, (ft % 2) * 1024:(ft % 2 + 1) * 1024], pso[:]) \
                        .then_inc(ac.h, 1)
                d_ac_done.append(ac.inc())
                blk.scalar(d_evv)

                def d_stv(e, ft=ft, t=d_ac_done[-1], _dpst=st.v):
                    _cst = _dpst
                    e.wait_ge(ac.h, t)
                    e.wait_ge(st.h, _cst)
                    e.dma_start(out=v1T_out[ft - 40, :, :],
                                in_=qkb[:, (ft % 2) * 1024:(ft % 2 + 1) * 1024]) \
                        .then_inc(st.h, 16)
                    _cst += 16
                blk.sync(d_stv)
                qkb_st.append(st.inc(16))
        PD_MM = mm.v
        PD_AC = ac.v
        PD_DV = dv.v
        PD_ST = st.v

        def e_bias(e, t=PD_MM, _dpgw=gw.v):
            _cgw = _dpgw
            e.wait_ge(mm.h, t)
            e.wait_ge(gw.h, _cgw)
            e.dma_start(out=bias[:], in_=bb_in[:]).then_inc(gw.h, 16)
            _cgw += 16
            e.wait_ge(gw.h, _cgw)
            e.dma_start(out=at2, in_=at_in[:]).then_inc(gw.h, 16)
            _cgw += 16
            e.wait_ge(gw.h, _cgw)
            e.dma_start(out=brr2, in_=brr_in[:]).then_inc(gw.h, 16)
            _cgw += 16
        blk.gpsimd(e_bias)
        gw.inc(48)
        E_BIAS = E_AT = gw.v
        comb_dv = []
        combT_ev = []
        for tok in range(8):
            def e_rmm(e, tok=tok, t=PD_AC, xv=PC_EV, wv=GW_EMISC,
                      prev=(comb_dv[tok - 1] if tok >= 1 else 0)):
                e.wait_ge(ac.h, t)
                e.wait_ge(ev.h, xv)
                e.wait_ge(gw.h, wv)
                if prev:
                    e.wait_ge(dv.h, prev)
                r = None
                for ic in range(32):
                    lhsT = AR0[:, ic * 1024 + tok * 128:ic * 1024 + (tok + 1) * 128]
                    rhs = wr2[:, ic * 8:(ic + 1) * 8]
                    r = e.matmul(ps_a[:, 0:8], lhsT, rhs,
                                 start=(ic == 0), stop=(ic == 31))
                r.then_inc(mm.h, 1)
            t_rmm = mm.inc()
            blk.tensor(e_rmm)

            def e_rmax(e, tok=tok, t=t_rmm, base=dv.v):
                e.wait_ge(mm.h, t)
                e.wait_ge(dv.h, base)
                e.reduce_max(rout[:, 24:25], ps_a[:, 0:8], axis=mybir.AxisListType.X) \
                    .then_inc(dv.h, 1)
                e.wait_ge(dv.h, base + 1)
                e.tensor_scalar(rout[:, 25:26], rout[:, 24:25], -1.0, None,
                                ALU.mult, ALU.bypass).then_inc(dv.h, 1)
            t_rmax = dv.inc(2)
            blk.vector(e_rmax)

            def e_rexp(e, tok=tok, t=t_rmax):
                e.wait_ge(dv.h, t)
                e.activation(rout[:, 0:8], ps_a[:, 0:8], AF.Exp,
                             bias=rout[:, 25:26], scale=1.0).then_inc(ac.h, 1)
            t_rexp = ac.inc()
            blk.scalar(e_rexp)

            def e_rsel(e, tok=tok, t=t_rexp, base=dv.v):
                e.wait_ge(ac.h, t)
                tt = base
                e.wait_ge(dv.h, tt)
                el = rout[:, 0:8]

                def step(r):
                    nonlocal tt
                    tt += 1
                    r.then_inc(dv.h, 1)
                    e.wait_ge(dv.h, tt)

                step(e.tensor_scalar(rout[:, 8:16], el, 1.0, None,
                                     ALU.is_ge, ALU.bypass))
                step(e.scalar_tensor_tensor(rout[:, 16:24], rout[:, 8:16], -2.0,
                                            el, ALU.mult, ALU.add))
                step(e.reduce_max(rout[:, 26:27], rout[:, 16:24],
                                  axis=mybir.AxisListType.X))
                step(e.tensor_scalar(rout[:, 8:16], el, rout[:, 26:27], None,
                                     ALU.is_ge, ALU.bypass))
                step(e.tensor_tensor(rout[:, 16:24], el, rout[:, 8:16], ALU.mult))
                step(e.tensor_scalar(rout[:, 27:28], rout[:, 26:27], 1.0, None,
                                     ALU.add, ALU.bypass))
                step(e.reciprocal(rout[:, 28:29], rout[:, 27:28]))
                step(e.tensor_scalar(routb[:], rout[:, 16:24], rout[:, 28:29], None,
                                     ALU.mult, ALU.bypass))
            comb_dv.append(dv.inc(8))
            blk.vector(e_rsel)

            def e_ctr(e, tok=tok, t=comb_dv[-1], pc=PC_EV,
                      fr=(combT_ev[tok - 1] if tok >= 1 else 0)):
                e.wait_ge(dv.h, t)
                e.wait_ge(ev.h, max(pc, fr))
                e.transpose(ps_t0[0:8, 0:128], routb[:], ident[:]).then_inc(mm.h, 1)
            t_ctr = mm.inc()
            blk.tensor(e_ctr)

            def e_cev(e, tok=tok, t=t_ctr):
                e.wait_ge(mm.h, t)
                e.tensor_copy(combT[:, tok * 128:(tok + 1) * 128], ps_t0[0:8, 0:128]) \
                    .then_inc(ev.h, 1)
            combT_ev.append(ev.inc())
            blk.vector(e_cev)
        E_COMBT = ev.v

        def e_cmm(e, t=E_COMBT, wv=GW_EMISC):
            e.wait_ge(ev.h, t)
            e.wait_ge(gw.h, wv)
            r = mm_split(e, ps_b, sel2[:], combT[:], NTOK, start=True, stop=True)
            r.then_inc(mm.h, 1)
        t_cmm = mm.inc()
        blk.tensor(e_cmm)

        def e_cmev(e, t=t_cmm):
            e.wait_ge(mm.h, t)
            e.tensor_copy(cmat, ps_b[:]).then_inc(dv.h, 1)
        E_CMAT = dv.inc()
        blk.vector(e_cmev)

        def e_hmm(e, t=E_CMAT, wv=E_AT, xv=PC_EV):
            e.wait_ge(dv.h, t)
            e.wait_ge(gw.h, wv)
            e.wait_ge(ev.h, xv)
            r = None
            for ic in range(32):
                lhsT = at2[:, ic * 128:(ic + 1) * 128]
                rhs = AR0[:, ic * NTOK:(ic + 1) * NTOK]
                r = mm_split(e, ps_a, lhsT, rhs, NTOK, start=(ic == 0), stop=(ic == 31))
            r.then_inc(mm.h, 1)
        t_hmm = mm.inc()
        blk.tensor(e_hmm)

        def e_hev(e, t=t_hmm):
            e.wait_ge(mm.h, t)
            e.tensor_tensor(hbuf, ps_a[:], cmat, ALU.mult).then_inc(dv.h, 1)
        E_HBUF = dv.inc()
        blk.vector(e_hev)


        e_wld = []
        e_mm_done = []
        e_ac_done = []
        g1_st = []
        for og in range(8):
            def e_wl(e, og=og, pd=PD_MM,
                     prev=(e_mm_done[8 * (og - 2) + 7] if og >= 2 else 0),
                     _dpgw=gw.v):
                _cgw = _dpgw
                e.wait_ge(mm.h, max(pd, prev))
                e.wait_ge(gw.h, _cgw)
                e.dma_start(out=AR1[:, (og % 2) * 16384:(og % 2 + 1) * 16384],
                            in_=wb_in[og, :, :]).then_inc(gw.h, 16)
                _cgw += 16
            blk.gpsimd(e_wl)
            e_wld.append(gw.inc(16))

            for tok in range(8):
                pso = [ps_b, ps_a][tok % 2]
                qslot = ((og * 8 + tok) % 4) * 512

                def eb_mm(e, og=og, tok=tok, pso=pso, wv=max(e_wld[og], E_AT),
                          hb=E_HBUF, bi=E_BIAS,
                          acp=(e_ac_done[-2] if len(e_ac_done) >= 2 else 0)):
                    e.wait_ge(gw.h, max(wv, bi))
                    e.wait_ge(dv.h, hb)
                    if acp:
                        e.wait_ge(ac.h, acp)
                    e.matmul(pso[:, 0:512], ones[0:1, 0:128],
                             bias[0:1, og * 512:(og + 1) * 512],
                             start=True, stop=False)
                    lhsT = hbuf[:, tok * 128:(tok + 1) * 128]
                    e.matmul(pso[:, 0:512], lhsT,
                             brr2[:, og * 512:(og + 1) * 512],
                             start=False, stop=False)
                    r = None
                    for ic in range(32):
                        lh = AR0[:, ic * 1024 + tok * 128:ic * 1024 + (tok + 1) * 128]
                        rhs = AR1[:, (og % 2) * 16384 + ic * 512:
                                  (og % 2) * 16384 + (ic + 1) * 512]
                        r = e.matmul(pso[:, 0:512], lh, rhs,
                                     start=False, stop=(ic == 31))
                    r.then_inc(mm.h, 1)
                e_mm_done.append(mm.inc())
                blk.tensor(eb_mm)

                def eb_ev(e, og=og, tok=tok, pso=pso, qslot=qslot,
                          t=e_mm_done[-1], pdst=PD_ST,
                          gp=(g1_st[-4] if len(g1_st) >= 4 else 0)):
                    e.wait_ge(mm.h, t)
                    e.wait_ge(st.h, max(pdst, gp))
                    e.activation(qkb[:, qslot:qslot + 512],
                                 pso[:, 0:512], AF.Silu, bias=consts[:, 2:3],
                                 scale=1.0).then_inc(ac.h, 1)
                e_ac_done.append(ac.inc())
                blk.scalar(eb_ev)

                def eb_st(e, og=og, tok=tok, qslot=qslot, t=e_ac_done[-1],
                          _dpst=st.v):
                    _cst = _dpst
                    e.wait_ge(ac.h, t)
                    e.wait_ge(st.h, _cst)
                    e.dma_start(out=g1_out[tok * 128:(tok + 1) * 128,
                                           og * 512:(og + 1) * 512],
                                in_=qkb[:, qslot:qslot + 512]).then_inc(st.h, 16)
                    _cst += 16
                blk.sync(eb_st)
                g1_st.append(st.inc(16))

        def fin(e, t=st.v):
            e.wait_ge(st.h, t)
        blk.gpsimd(fin)

    return nc



_CACHE = {}


def _prep_launch1(inputs):
    Wqkv = np.asarray(inputs["L0_attn_Wqkv"], np.float32)
    wqkv_t = np.ascontiguousarray(
        Wqkv.reshape(32, 128, 24, 128).transpose(2, 1, 0, 3).reshape(24, 128, H))
    bqkv = np.ascontiguousarray(
        np.asarray(inputs["L0_attn_bqkv"], np.float32).reshape(1, 3072))
    qw = np.asarray(inputs["L0_ffn_qweight"])
    lut4 = np.asarray(inputs["L0_ffn_lut4"], np.float32)
    mob = mobius_matrix()
    ident = np.eye(128, dtype=BF)
    x = np.asarray(inputs["L0_attn_x"], np.float32).reshape(B * S, H)

    in_maps = []
    for c in range(NCORE := 8):
        rows = core_rows(c)
        o0 = 512 * c
        in_maps.append({
            "x": np.ascontiguousarray(x[rows]),
            "wqkv_t": wqkv_t,
            "bqkv": bqkv,
            "qw": np.ascontiguousarray(
                qw[0:4, o0:o0 + 512, :].reshape(4, 4, 128, 128)),
            "lut4T": np.ascontiguousarray(lut4[o0:o0 + 512, :].T),
            "mob": mob,
            "ident": ident,
            "identf": np.eye(128, dtype=np.float32),
        })
    return in_maps




def build_launch3():
    nc = bass.Bass("TRN2", target_bir_lowering=False)
    x1_in = nc.dram_tensor("x1", [NTOK, H], F32, kind="ExternalInput").ap()
    g1_in = nc.dram_tensor("g1", [NTOK, H], BF16, kind="ExternalInput").ap()
    q1T_in = nc.dram_tensor("q1T", [32, 128, NTOK], BF16, kind="ExternalInput").ap()
    k1_in = nc.dram_tensor("k1", [8, 128, S], BF16, kind="ExternalInput").ap()
    v1_in = nc.dram_tensor("v1", [128, 32768], BF16, kind="ExternalInput").ap()
    wo1_in = nc.dram_tensor("wo1", [8, 32, 128, 512], F32, kind="ExternalInput").ap()
    mask_in = nc.dram_tensor("mask", [128, 4096], BF16, kind="ExternalInput").ap()

    out_out = nc.dram_tensor("out", [NTOK, H], F32, kind="ExternalOutput").ap()

    SC1 = 0.08838834764831843

    from contextlib import ExitStack
    with ExitStack() as ctx:
        def sb(name, shape, dt):
            return ctx.enter_context(nc.sbuf_tensor(name, shape, dt))

        def psum(name, shape, dt=F32):
            return ctx.enter_context(nc.psum_tensor(name, shape, dt))

        AR0 = sb("V1", [128, 32768], BF16)
        AR1 = sb("OT", [128, 32768], BF16)
        k1b = sb("k1b", [128, 2 * 4096], BF16)
        q1h = sb("q1h", [128, 2 * 1024], BF16)
        exps = sb("exps", [128, 4 * 1024], BF16)
        maskb = sb("maskb", [128, 4096], BF16)
        rvec = sb("rvec3", [1, NTOK], F32)
        rrow = sb("rrow3", [1, NTOK], F32)
        onesf = sb("onesf3", [1, 128], F32)
        ones_col = sb("ones_col3", [128, 1], BF16)
        consts = sb("consts3", [128, 2], F32)
        x1t = sb("x1t", [128, 2 * 1024], F32)
        g1t = sb("g1t", [128, 2 * 1024], BF16)
        ot1 = sb("ot1", [128, 2 * 1024], F32)
        ps_s0 = psum("ps3_s0", [128, 1024], F32)
        ps_s1 = psum("ps3_s1", [128, 1024], F32)
        ps_o = psum("ps3_o", [128, 1024], F32)
        ps_r1 = psum("ps3_r1", [128, 512], F32)
        ps_r2 = psum("ps3_r2", [128, 512], F32)
        ld = Sem(ctx.enter_context(nc.semaphore("ld3")))
        gw = Sem(ctx.enter_context(nc.semaphore("gw3")))
        mm = Sem(ctx.enter_context(nc.semaphore("mm3")))
        dv = Sem(ctx.enter_context(nc.semaphore("dv3")))
        ac = Sem(ctx.enter_context(nc.semaphore("ac3")))
        st = Sem(ctx.enter_context(nc.semaphore("st3")))
        blk = ctx.enter_context(nc.Block())
        pss = [ps_s0, ps_s1]

        def c0(e, _dpld=ld.v):
            _cld = _dpld
            e.wait_ge(ld.h, _cld)
            e.dma_start(out=AR0[:], in_=v1_in[:]).then_inc(ld.h, 16)
            _cld += 16
            e.wait_ge(ld.h, _cld)
            e.dma_start(out=maskb[:], in_=mask_in[:]).then_inc(ld.h, 16)
            _cld += 16
        blk.sync(c0)
        LD_C = ld.inc(32)

        def v_c(e):
            e.memset(ones_col[:], 1.0)
            e.memset(onesf[:], 1.0)
            e.memset(consts[:, 0:1], -SHIFT)
            e.memset(consts[:, 1:2], 0.0).then_inc(dv.h, 1)
        blk.vector(v_c)
        DV_C = dv.inc()

        def nq_of(ka):
            return (8 - ka // 4) * 128

        def pe_ops_of_head(h):
            ops = [("smm", 0)]
            if h > 0:
                ops += [("hrbP", -1)]
            for ka in range(1, 32):
                ops += [("smm", ka), ("sum", ka - 1)]
            ops += [("sum", 31)]
            return ops

        plan = {}
        p_mm, p_ac, p_dv = mm.v, ac.v, dv.v
        for h in range(NH):
            for op, ka in pe_ops_of_head(h):
                p_mm += 1
                if op == "hrbP":
                    plan[("hrb", h - 1)] = p_mm
                else:
                    plan[(op, h, ka)] = p_mm
            for ka in range(32):
                p_ac += 1
                plan[("exp", h, ka)] = p_ac
            plan[("svexp", h)] = p_ac + 3
            plan[("otev", h)] = p_ac + 4
            p_ac += 4
            for ka in range(32):
                p_dv += 1
                plan[("mask", h, ka)] = p_dv
                if ka == 0 and h > 0:
                    p_dv += 1
                    plan[("omult", h - 1)] = p_dv
        plan[("hrb", NH - 1)] = p_mm + 1
        plan[("omult", NH - 1)] = p_dv + 1

        k_ld = [0] * 8
        q_ld = [0] * 32
        for h in range(NH):
            kv = h // 4
            if h % 4 == 0:
                def kld(e, kv=kv,
                        prev=(plan[("smm", 4 * (kv - 2) + 3, 31)]
                              if kv >= 2 else 0), _dpld=ld.v):
                    _cld = _dpld
                    if prev:
                        e.wait_ge(mm.h, prev)
                    e.wait_ge(ld.h, _cld)
                    e.dma_start(out=k1b[:, (kv % 2) * 4096:(kv % 2 + 1) * 4096],
                                in_=k1_in[kv, :, :]).then_inc(ld.h, 16)
                    _cld += 16
                blk.sync(kld)
                k_ld[kv] = ld.inc(16)

            def qld(e, h=h,
                    prev=(plan[("smm", h - 2, 31)] if h >= 2 else 0), _dpgw=gw.v):
                _cgw = _dpgw
                if prev:
                    e.wait_ge(mm.h, prev)
                e.wait_ge(gw.h, _cgw)
                e.dma_start(out=q1h[:, (h % 2) * 1024:(h % 2 + 1) * 1024],
                            in_=q1T_in[h, :, :]).then_inc(gw.h, 16)
                _cgw += 16
            blk.gpsimd(qld)
            q_ld[h] = gw.inc(16)

            def pe_head(e, h=h, kv=kv, kl=k_ld[kv], ql=q_ld[h]):
                e.wait_ge(ld.h, kl)
                e.wait_ge(gw.h, ql)
                for op, ka in pe_ops_of_head(h):
                    if op == "smm":
                        b = ka % 2
                        if ka >= 2:
                            e.wait_ge(ac.h, plan[("exp", h, ka - 2)])
                        elif h > 0:
                            e.wait_ge(ac.h, plan[("exp", h - 1, 30 + ka)])
                            if ka == 1:
                                e.wait_ge(dv.h, plan[("omult", h - 1)])
                        nq = nq_of(ka)
                        lhsT = k1b[:, (kv % 2) * 4096 + ka * 128:
                                   (kv % 2) * 4096 + (ka + 1) * 128]
                        rhs = q1h[:, (h % 2) * 1024:(h % 2) * 1024 + nq]
                        r = mm_split(e, pss[b], lhsT, rhs, nq,
                                     start=True, stop=True)
                        r.then_inc(mm.h, 1)
                    elif op == "sum":
                        e.wait_ge(dv.h, plan[("mask", h, ka)])
                        nq = nq_of(ka)
                        sl = (ka % 4) * 1024
                        st0 = (ka == 0)
                        n1 = min(nq, 512)
                        lhsT = AR0[:, (kv * 32 + ka) * 128:
                                   (kv * 32 + ka + 1) * 128]
                        r = e.matmul(ps_r1[0:1, 0:n1], ones_col[:],
                                     exps[:, sl:sl + n1],
                                     start=st0, stop=(ka == 31))
                        r = e.matmul(ps_o[:, 0:n1], lhsT, exps[:, sl:sl + n1],
                                     start=st0, stop=(ka == 31))
                        if nq > 512:
                            r = e.matmul(ps_r2[0:1, 0:nq - 512], ones_col[:],
                                         exps[:, sl + 512:sl + nq],
                                         start=st0, stop=(ka == 15))
                            r = e.matmul(ps_o[:, 512:nq], lhsT,
                                         exps[:, sl + 512:sl + nq],
                                         start=st0, stop=(ka == 15))
                        r.then_inc(mm.h, 1)
                    else:
                        e.wait_ge(ac.h, plan[("svexp", h - 1)])
                        e.matmul(ps_s1[:, 0:512], onesf[:], rrow[:, 0:512],
                                 start=True, stop=True)
                        e.matmul(ps_s1[:, 512:1024], onesf[:],
                                 rrow[:, 512:1024],
                                 start=True, stop=True).then_inc(mm.h, 1)
            for op, ka in pe_ops_of_head(h):
                v = mm.inc()
                key = ("hrb", h - 1) if op == "hrbP" else (op, h, ka)
                assert v == plan[key], (h, op, ka)
            blk.tensor(pe_head)

            def act_head(e, h=h, dc=DV_C):
                if h == 0:
                    e.wait_ge(dv.h, dc)
                for ka in range(32):
                    e.wait_ge(mm.h, plan[("smm", h, ka)])
                    nq = nq_of(ka)
                    sl = (ka % 4) * 1024
                    e.activation(exps[:, sl:sl + nq],
                                 pss[ka % 2][:, 0:nq], AF.Exp,
                                 bias=consts[:, 0:1], scale=SC1) \
                        .then_inc(ac.h, 1)
                e.wait_ge(mm.h, plan[("sum", h, 31)])
                e.activation(rvec[:, 0:512], ps_r1[0:1, 0:512], AF.Ln,
                             bias=consts[0:1, 1:2], scale=1.0).then_inc(ac.h, 1)
                e.activation(rvec[:, 512:1024], ps_r2[0:1, 0:512], AF.Ln,
                             bias=consts[0:1, 1:2], scale=1.0).then_inc(ac.h, 1)
                e.wait_ge(ac.h, plan[("svexp", h)] - 1)
                e.activation(rrow[:, 0:1024], rvec[:, 0:1024], AF.Exp,
                             bias=consts[0:1, 1:2], scale=-1.0).then_inc(ac.h, 1)
                e.copy(AR1[:, h * 1024:(h + 1) * 1024], ps_o[:]) \
                    .then_inc(ac.h, 1)
            for ka in range(32):
                v = ac.inc()
                assert v == plan[("exp", h, ka)], (h, ka)
            assert ac.inc(4) == plan[("otev", h)], h
            blk.scalar(act_head)

            def dve_head(e, h=h):
                for ka in range(32):
                    e.wait_ge(ac.h, plan[("exp", h, ka)])
                    nq = nq_of(ka)
                    sl = (ka % 4) * 1024
                    e.tensor_tensor(exps[:, sl + nq - 128:sl + nq],
                                    exps[:, sl + nq - 128:sl + nq],
                                    maskb[:, ka * 128:(ka + 1) * 128],
                                    ALU.mult).then_inc(dv.h, 1)
                    if ka == 0 and h > 0:
                        e.wait_ge(mm.h, plan[("hrb", h - 1)])
                        e.wait_ge(ac.h, plan[("otev", h - 1)])
                        e.tensor_tensor(AR1[:, (h - 1) * 1024:h * 1024],
                                        AR1[:, (h - 1) * 1024:h * 1024],
                                        ps_s1[:], ALU.mult).then_inc(dv.h, 1)
            for ka in range(32):
                v = dv.inc()
                assert v == plan[("mask", h, ka)], (h, ka)
                if ka == 0 and h > 0:
                    assert dv.inc() == plan[("omult", h - 1)], h
            blk.vector(dve_head)

        def pe_fin(e, hl=NH - 1):
            e.wait_ge(ac.h, plan[("svexp", hl)])
            e.matmul(ps_s1[:, 0:512], onesf[:], rrow[:, 0:512],
                     start=True, stop=True)
            e.matmul(ps_s1[:, 512:1024], onesf[:], rrow[:, 512:1024],
                     start=True, stop=True).then_inc(mm.h, 1)
        assert mm.inc() == plan[("hrb", NH - 1)]
        blk.tensor(pe_fin)

        def dve_fin(e, hl=NH - 1):
            e.wait_ge(mm.h, plan[("hrb", hl)])
            e.wait_ge(ac.h, plan[("otev", hl)])
            e.tensor_tensor(AR1[:, hl * 1024:(hl + 1) * 1024],
                            AR1[:, hl * 1024:(hl + 1) * 1024],
                            ps_s1[:], ALU.mult).then_inc(dv.h, 1)
        assert dv.inc() == plan[("omult", NH - 1)]
        blk.vector(dve_fin)
        ATT_DV = dv.v
        ATT_MM = mm.v
        last_exp_slot = [plan[("exp", NH - 1, 30)], plan[("exp", NH - 1, 31)]]

        psw = [ps_o, ps_s0]
        wo_win = []
        wo_mm = []
        gate_dv = []
        out_st = []
        for og in range(8):
            def wwin(e, og=og, att=ATT_MM,
                     lastmm=(wo_mm[8 * (og - 2) + 7] if og >= 2 else 0),
                     _dpgw=gw.v):
                _cgw = _dpgw
                e.wait_ge(mm.h, max(att, lastmm))
                e.wait_ge(gw.h, _cgw)
                e.dma_start(out=AR0[:, (og % 2) * 16384:(og % 2 + 1) * 16384]
                            .rearrange("p (dc f) -> p dc f", f=512),
                            in_=wo1_in[og, :, :, :].rearrange("dc p f -> p dc f")) \
                    .then_inc(gw.h, 16)
                _cgw += 16
            blk.gpsimd(wwin)
            wo_win.append(gw.inc(16))

            for tok in range(8):
                sl = ((og * 8 + tok) % 4) * 512

                def gld(e, og=og, tok=tok, sl=sl,
                        gp=(gate_dv[-4] if len(gate_dv) >= 4 else 0), _dpld=ld.v):
                    _cld = _dpld
                    if gp:
                        e.wait_ge(dv.h, gp)
                    e.wait_ge(ld.h, _cld)
                    e.dma_start(out=x1t[:, sl:sl + 512],
                                in_=x1_in[tok * 128:(tok + 1) * 128,
                                          og * 512:(og + 1) * 512]).then_inc(ld.h, 16)
                    _cld += 16
                    e.wait_ge(ld.h, _cld)
                    e.dma_start(out=g1t[:, sl:sl + 512],
                                in_=g1_in[tok * 128:(tok + 1) * 128,
                                          og * 512:(og + 1) * 512]).then_inc(ld.h, 16)
                    _cld += 16
                blk.sync(gld)
                gv = ld.inc(32)

                def wmm(e, og=og, tok=tok, t=ATT_DV, wv=wo_win[og],
                        ae=(last_exp_slot[0] if tok % 2 == 1 else 0),
                        gp=(gate_dv[-2] if len(gate_dv) >= 2 else 0)):
                    e.wait_ge(dv.h, t)
                    e.wait_ge(gw.h, wv)
                    if ae:
                        e.wait_ge(ac.h, ae)
                    if gp:
                        e.wait_ge(dv.h, gp)
                    r = None
                    for dc in range(32):
                        lhsT = AR1[:, dc * 1024 + tok * 128:dc * 1024 + (tok + 1) * 128]
                        rhs = AR0[:, (og % 2) * 16384 + dc * 512:
                                  (og % 2) * 16384 + (dc + 1) * 512]
                        r = e.matmul(psw[tok % 2][:, 0:512], lhsT, rhs,
                                     start=(dc == 0), stop=(dc == 31))
                    r.then_inc(mm.h, 1)
                wo_mm.append(mm.inc())
                blk.tensor(wmm)

                def gate(e, og=og, tok=tok, sl=sl, t=wo_mm[-1], gv=gv, base=dv.v,
                         sp=(out_st[-4] if len(out_st) >= 4 else 0)):
                    e.wait_ge(mm.h, t)
                    e.wait_ge(ld.h, gv)
                    if sp:
                        e.wait_ge(st.h, sp)
                    e.tensor_tensor(ot1[:, sl:sl + 512], psw[tok % 2][:, 0:512],
                                    g1t[:, sl:sl + 512], ALU.mult).then_inc(dv.h, 1)
                    e.wait_ge(dv.h, base + 1)
                    e.tensor_tensor(ot1[:, sl:sl + 512], ot1[:, sl:sl + 512],
                                    x1t[:, sl:sl + 512], ALU.add).then_inc(dv.h, 1)
                gate_dv.append(dv.inc(2))
                blk.vector(gate)

                def ost(e, og=og, tok=tok, sl=sl, t=gate_dv[-1], _dpst=st.v):
                    _cst = _dpst
                    e.wait_ge(dv.h, t)
                    e.wait_ge(st.h, _cst)
                    e.dma_start(out=out_out[tok * 128:(tok + 1) * 128,
                                            og * 512:(og + 1) * 512],
                                in_=ot1[:, sl:sl + 512]).then_inc(st.h, 16)
                    _cst += 16
                blk.sync(ost)
                out_st.append(st.inc(16))

        def fin(e, t=st.v):
            e.wait_ge(st.h, t)
        blk.gpsimd(fin)

    return nc


def _prep_launch3(inputs, r2):
    perm = batch_perm()
    Wo1T = np.asarray(inputs["L1_attn_Wo"], np.float32).T
    wo1t = np.ascontiguousarray(
        Wo1T.reshape(32, 128, 8, 512).transpose(2, 0, 1, 3))
    k1b_b, v1_b = [], []
    for b in range(B):
        K1_loc = np.concatenate(
            [np.asarray(r2[4 * b + cc]["k1T"]).reshape(1024, NTOK)
             for cc in range(4)], axis=1)
        K1_full = np.ascontiguousarray(K1_loc[:, perm])
        k1b_b.append(np.ascontiguousarray(K1_full.reshape(8, 128, S)))
        V1_loc = np.concatenate(
            [np.asarray(r2[4 * b + cc]["v1T"]).reshape(1024, NTOK)
             for cc in range(4)], axis=1)
        V1_full = np.ascontiguousarray(V1_loc[:, perm].T)
        v1_b.append(np.ascontiguousarray(
            V1_full.reshape(32, 128, 8, 128).transpose(1, 2, 0, 3).reshape(128, 32768)))
    in_maps = []
    for c in range(8):
        b, cc = divmod(c, 4)
        masks = causal_masks(cc)
        maskb = np.ascontiguousarray(masks.transpose(1, 0, 2).reshape(128, 4096))
        in_maps.append({
            "x1": np.asarray(r2[c]["x1"]),
            "g1": np.asarray(r2[c]["g1"]),
            "q1T": np.asarray(r2[c]["q1T"]),
            "k1": k1b_b[b],
            "v1": v1_b[b],
            "wo1": wo1t,
            "mask": maskb,
        })
    return in_maps


def _prep_launch2(inputs, r1):
    perm = batch_perm()
    x = np.asarray(inputs["L0_attn_x"], np.float32).reshape(B * S, H)
    Wo = np.asarray(inputs["L0_attn_Wo"], np.float32)
    wo2 = np.ascontiguousarray(Wo.reshape(8, 128, 4, 1024).transpose(2, 0, 1, 3))
    bo = np.ascontiguousarray(np.asarray(inputs["L0_attn_bo"], np.float32).reshape(1, H))
    bffn = np.ascontiguousarray(np.asarray(inputs["L0_ffn_bias"], np.float32).reshape(1, H))
    Wq = np.asarray(inputs["L1_attn_Wq"], np.float32)
    Wk = np.asarray(inputs["L1_attn_Wk"], np.float32)
    Wv = np.asarray(inputs["L1_attn_Wv"], np.float32)
    W1T = np.concatenate([Wq.T, Wk.T, Wv.T], axis=1)
    w1t = np.ascontiguousarray(
        W1T.reshape(32, 128, 48, 128).transpose(2, 1, 0, 3).reshape(48, 128, H))
    Wr = np.asarray(inputs["L1_ffn_W_router"], np.float32)
    wr2 = np.ascontiguousarray(Wr.T.reshape(32, 128, 8).transpose(1, 0, 2).reshape(128, 256))
    A_stack = np.asarray(inputs["L1_ffn_A_stack"], np.float32)
    AT = A_stack.transpose(2, 0, 1).reshape(H, 128)
    at2 = np.ascontiguousarray(AT.reshape(32, 128, 128).transpose(1, 0, 2).reshape(128, H))
    B_stack = np.asarray(inputs["L1_ffn_B_stack"], np.float32)
    brr = np.ascontiguousarray(B_stack.transpose(0, 2, 1).reshape(128, H))
    WbT = np.asarray(inputs["L1_ffn_W_base"], np.float32).T
    wb2 = np.ascontiguousarray(
        WbT.reshape(32, 128, 8, 512).transpose(2, 1, 0, 3).reshape(8, 128, 16384))
    bb = np.ascontiguousarray(np.asarray(inputs["L1_ffn_b_base"], np.float32).reshape(1, H))
    sel = np.zeros((8, 128), np.float32)
    for er in range(128):
        sel[er // 16, er] = 2.0
    ident = np.eye(128, dtype=BF)

    kt2b, v2b = [], []
    for b in range(B):
        KT_loc = np.concatenate(
            [np.asarray(r1[4 * b + cc]["qkvT"])[8:16].reshape(1024, NTOK)
             for cc in range(4)], axis=1)
        KT_full = KT_loc[:, perm]
        kt2b.append(np.ascontiguousarray(
            KT_full.reshape(8, 128, 32, 128).transpose(2, 1, 0, 3).reshape(32, 128, 1024)))
        VT_loc = np.concatenate(
            [np.asarray(r1[4 * b + cc]["qkvT"])[16:24].reshape(1024, NTOK)
             for cc in range(4)], axis=1)
        V_full = np.ascontiguousarray(VT_loc[:, perm].T)
        v2b.append(np.ascontiguousarray(
            V_full.reshape(32, 128, 8, 128).transpose(1, 0, 2, 3).reshape(128, 32768)))

    wl_full = np.concatenate(
        [np.asarray(r1[c]["wlT"]).reshape(H, 512) for c in range(8)], axis=1)
    wl2 = np.ascontiguousarray(
        wl_full.reshape(32, 128, 8, 512).transpose(2, 1, 0, 3).reshape(8, 128, 16384))

    in_maps = []
    for c in range(8):
        b = c // 4
        rows = core_rows(c)
        cosT, sinT = rope_tables(rows % S)
        in_maps.append({
            "x": np.ascontiguousarray(x[rows]),
            "xnT": np.asarray(r1[c]["xnT"]),
            "qT": np.ascontiguousarray(np.asarray(r1[c]["qkvT"])[0:8]),
            "kt": kt2b[b],
            "v": v2b[b],
            "wl": wl2,
            "wo": wo2,
            "bo": bo,
            "bffn": bffn,
            "w1": w1t,
            "cosT": cosT,
            "sinT": sinT,
            "wrT": wr2,
            "at2": at2,
            "brr": brr,
            "wb": wb2,
            "bb": bb,
            "sel": sel,
            "ident": ident,
        })
    return in_maps

def kernel(**inputs):
    if "nc1" not in _CACHE:
        _CACHE["nc1"] = build_launch1()
        _CACHE["nc2"] = build_launch2()
        _CACHE["nc3"] = build_launch3()
    im1 = _prep_launch1(inputs)
    r1 = run_bass_kernel_spmd(_CACHE["nc1"], im1, core_ids=list(range(8))).results
    im2 = _prep_launch2(inputs, r1)
    r2 = run_bass_kernel_spmd(_CACHE["nc2"], im2, core_ids=list(range(8))).results
    im3 = _prep_launch3(inputs, r2)
    r3 = run_bass_kernel_spmd(_CACHE["nc3"], im3, core_ids=list(range(8))).results
    out = np.empty((B * S, H), np.float32)
    for c in range(8):
        out[core_rows(c)] = np.asarray(r3[c]["out"], np.float32)
    return out.reshape(B, S, H)


# version 13
# speedup vs baseline: 1.0097x; 1.0097x over previous
import numpy as np
import ml_dtypes

import concourse.bass as bass
import concourse.mybir as mybir
from concourse.bass_utils import run_bass_kernel_spmd

F32 = mybir.dt.float32
BF16 = mybir.dt.bfloat16
I32 = mybir.dt.int32
AF = mybir.ActivationFunctionType
ALU = mybir.AluOpType

BF = ml_dtypes.bfloat16

H = 4096
S = 4096
B = 2
NTOK = 1024
NT = 8
NH, NKV, HD = 32, 8, 128
SHIFT = 40.0
PADEXT = [32, 28, 24, 20, 16, 12, 8, 4]



def core_tiles(cc):
    tiles = []
    for p in [cc, cc + 4, cc + 8, cc + 12]:
        tiles += [p, 31 - p]
    return sorted(tiles, reverse=True)


def core_rows(c):
    b, cc = divmod(c, 4)
    rows = []
    for t in core_tiles(cc):
        rows.extend(range(b * S + t * 128, b * S + t * 128 + 128))
    return np.asarray(rows, dtype=np.int64)


def batch_perm():
    pos = np.empty(S, np.int64)
    for cc in range(4):
        tiles = core_tiles(cc)
        for j, t in enumerate(tiles):
            pos[t * 128:(t + 1) * 128] = cc * NTOK + j * 128 + np.arange(128)
    return pos


def mobius_matrix():
    M = np.zeros((16, 16), np.float32)
    for s in range(16):
        for t in range(16):
            if (t & s) == t:
                M[t, s] = (-1.0) ** (bin(s).count("1") - bin(t).count("1"))
    return M


def rope_tables(rows_in_seq):
    inv = 1.0 / (1e4 ** (np.arange(0, HD, 2) / HD))
    freqs = rows_in_seq[:, None].astype(np.float64) * inv[None, :]
    emb = np.concatenate([freqs, freqs], -1)
    return (np.ascontiguousarray(np.cos(emb).astype(np.float32).T),
            np.ascontiguousarray(np.sin(emb).astype(np.float32).T))


def causal_masks(cc):
    tiles = core_tiles(cc)
    out = np.zeros((32, 128, 128), np.float32)
    for k in range(32):
        g = tiles[7 - k // 4]
        if k < g:
            out[k] = 1.0
        elif k == g:
            kpos = k * 128 + np.arange(128)
            gq = g * 128 + np.arange(128)
            out[k] = (kpos[:, None] <= gq[None, :]).astype(np.float32)
    return np.ascontiguousarray(out.astype(BF))


def tile_wT(wT, part=128, free=1024):
    C, Fd = wT.shape
    t = wT.reshape(C // part, part, Fd // free, free)
    return np.ascontiguousarray(t.transpose(2, 0, 1, 3))



class Sem:
    __slots__ = ("h", "v")

    def __init__(self, h):
        self.h = h
        self.v = 0

    def inc(self, n=1):
        self.v += n
        return self.v


def mm_split(t, out_ap, lhsT_ap, rhs_ap, n, start, stop, nmax=512):
    o = 0
    r = None
    while o < n:
        w = min(nmax, n - o)
        r = t.matmul(out_ap[:, o:o + w], lhsT_ap, rhs_ap[:, o:o + w],
                     start=start, stop=stop)
        o += w
    return r



def build_launch1():
    nc = bass.Bass("TRN2", target_bir_lowering=False)
    x_in = nc.dram_tensor("x", [NTOK, H], F32, kind="ExternalInput").ap()
    wqkv_in = nc.dram_tensor("wqkv_t", [24, 128, H], F32, kind="ExternalInput").ap()
    bqkv_in = nc.dram_tensor("bqkv", [1, 3072], F32, kind="ExternalInput").ap()
    qw_in = nc.dram_tensor("qw", [4, 4, 128, 128], I32, kind="ExternalInput").ap()
    lut4T_in = nc.dram_tensor("lut4T", [16, 512], F32, kind="ExternalInput").ap()
    mob_in = nc.dram_tensor("mob", [16, 16], F32, kind="ExternalInput").ap()
    ident_in = nc.dram_tensor("ident", [128, 128], BF16, kind="ExternalInput").ap()
    identf_in = nc.dram_tensor("identf", [128, 128], F32, kind="ExternalInput").ap()

    xnT_out = nc.dram_tensor("xnT", [32, 128, NTOK], BF16, kind="ExternalOutput").ap()
    qkvT_out = nc.dram_tensor("qkvT", [24, 128, NTOK], BF16, kind="ExternalOutput").ap()
    wlT_out = nc.dram_tensor("wlT", [32, 128, 512], BF16, kind="ExternalOutput").ap()

    from contextlib import ExitStack
    with ExitStack() as ctx:
        def sb(name, shape, dt):
            return ctx.enter_context(nc.sbuf_tensor(name, shape, dt))

        def psum(name, shape, dt=F32):
            return ctx.enter_context(nc.psum_tensor(name, shape, dt))

        def sem(name):
            return ctx.enter_context(nc.semaphore(name))

        xt = sb("xt", [128, 2 * H], F32)
        bn = sb("bn", [128, 52], F32)
        rstd = sb("rstd", [128, 1], F32)
        xnb = sb("xnb", [128, H], BF16)
        xnT = sb("xnTs", [128, 32 * NTOK], BF16)
        ident = sb("idents", [128, 128], BF16)
        identf = sb("identfs", [128, 128], F32)
        ones = sb("ones", [1, NTOK], BF16)
        consts = sb("consts", [128, 4], F32)
        wbuf = sb("wbuf", [128, 2 * 4096], BF16)
        bqb = sb("bqb", [1, 3072], BF16)
        qkvTs = sb("qkvTs", [128, 2 * NTOK], BF16)
        qws = sb("qws", [128, 16 * 128], I32)
        ibits = sb("ibits", [128, 2048], I32)
        bits = sb("bits", [128, 4 * 2048], BF16)
        lut4T = sb("lut4Ts", [16, 512], F32)
        mob = sb("mobs", [16, 16], F32)
        coef = sb("coef", [128, 64], F32)
        p01 = sb("p01", [128, 2048], F32)
        dq1 = sb("dq1", [128, 2048], F32)
        dq2 = sb("dq2", [128, 2048], F32)
        wacc = sb("wacc", [128, 2048], F32)
        wstage = sb("wstage", [128, 2048], BF16)
        ps_q0 = psum("ps_q0", [128, 1024], F32)
        ps_q1 = psum("ps_q1", [128, 1024], F32)
        ps_t0 = psum("ps_t0", [128, 1024], BF16)
        ps_t1 = psum("ps_t1", [128, 1024], BF16)
        ps_f0 = psum("ps_f0", [128, 512], F32)
        ps_f1 = psum("ps_f1", [128, 512], F32)
        h_ld = sem("ld")
        h_gw = sem("gw")
        h_mm = sem("mm")
        h_dv = sem("dv")
        h_ev = sem("ev")
        h_ac = sem("ac")
        h_st = sem("st")
        blk = ctx.enter_context(nc.Block())
        ld, gw, mm, dv, ev, ac, st = (Sem(h_ld), Sem(h_gw), Sem(h_mm), Sem(h_dv),
                                      Sem(h_ev), Sem(h_ac), Sem(h_st))
        pst = [ps_t0, ps_t1]
        pst_free = [0, 0]

        def c0(e, _dpld=ld.v):
            _cld = _dpld
            e.wait_ge(ld.h, _cld)
            e.dma_start(out=ident[:], in_=ident_in[:]).then_inc(ld.h, 16)
            _cld += 16
            e.wait_ge(ld.h, _cld)
            e.dma_start(out=identf[:], in_=identf_in[:]).then_inc(ld.h, 16)
            _cld += 16
            e.wait_ge(ld.h, _cld)
            e.dma_start(out=lut4T[:], in_=lut4T_in[:]).then_inc(ld.h, 16)
            _cld += 16
            e.wait_ge(ld.h, _cld)
            e.dma_start(out=mob[:], in_=mob_in[:]).then_inc(ld.h, 16)
            _cld += 16
        blk.sync(c0)
        LD_CONST = ld.inc(64)

        def v_ones(e):
            e.memset(ones[:], 1.0)
            e.memset(consts[:, 0:1], 1e-5)
            e.memset(consts[:, 1:2], 0.0).then_inc(dv.h, 1)
        blk.vector(v_ones)
        ONES_DV = dv.inc()

        def g0(e, _dpgw=gw.v):
            _cgw = _dpgw
            e.wait_ge(gw.h, _cgw)
            e.dma_start(out=bqb[:], in_=bqkv_in[:]).then_inc(gw.h, 16)
            _cgw += 16
            for ot in range(4):
                for pl in range(4):
                    e.wait_ge(gw.h, _cgw)
                    e.dma_start(out=qws[:, (ot * 4 + pl) * 128:(ot * 4 + pl + 1) * 128],
                                in_=qw_in[pl, ot, :, :]).then_inc(gw.h, 16)
                    _cgw += 16
        blk.gpsimd(g0)
        gw.inc(16 + 256)
        GW_BQ = GW_QW = gw.v

        ps_c_free = 0
        COEF_EV = 0
        for ot in range(4):
            def mobmm(e, ot=ot, fr=ps_c_free, need=LD_CONST):
                e.wait_ge(ld.h, need)
                if fr:
                    e.wait_ge(ev.h, fr)
                e.matmul(ps_f0[:, 0:16], lut4T[:, ot * 128:(ot + 1) * 128], mob[:],
                         start=True, stop=True).then_inc(mm.h, 1)
            t_mm = mm.inc()
            blk.tensor(mobmm)

            def mobev(e, ot=ot, t=t_mm):
                e.wait_ge(mm.h, t)
                e.tensor_copy(coef[:, ot * 16:(ot + 1) * 16], ps_f0[:, 0:16]) \
                    .then_inc(ev.h, 1)
            ps_c_free = COEF_EV = ev.inc()
            blk.vector(mobev)

        def ln_ld0(e, _dpld=ld.v):
            _cld = _dpld
            e.wait_ge(ld.h, _cld)
            e.dma_start(out=xt[:, 0:H], in_=x_in[0:128, :]).then_inc(ld.h, 16)
            _cld += 16
            e.wait_ge(ld.h, _cld)
            e.dma_start(out=xt[:, H:2 * H], in_=x_in[128:256, :]).then_inc(ld.h, 16)
            _cld += 16
        blk.sync(ln_ld0)
        x_loaded = [ld.inc(16), ld.inc(16)]
        xnb_free_mm = 0

        for i in range(NT):
            sl = (i % 2) * H

            def ln_stats(e, sl=sl, need=x_loaded[i], _sp=dv.v):
                e.wait_ge(ld.h, need)
                e.wait_ge(dv.h, _sp)
                xs = xt[:, sl:sl + H]
                r = None
                for k in range(8):
                    r = e.bn_stats(bn[:, 6 * k:6 * k + 6], xs[:, 512 * k:512 * (k + 1)])
                r.then_inc(dv.h, 1)
            t1 = dv.inc()
            blk.vector(ln_stats)

            def ln_aggr(e, t=t1):
                e.wait_ge(dv.h, t)
                e.bn_aggr(bn[:, 48:50], bn[:, 0:48]).then_inc(dv.h, 1)
            t2 = dv.inc()
            blk.vector(ln_aggr)

            def ln_rstd(e, t=t2, a0=ac.v):
                e.wait_ge(dv.h, t)
                e.activation(bn[:, 50:51], bn[:, 49:50], AF.Ln, bias=consts[:, 0:1], scale=1.0) \
                    .then_inc(ac.h, 1)
                e.wait_ge(ac.h, a0 + 1)
                e.activation(rstd[:], bn[:, 50:51], AF.Exp, bias=consts[:, 1:2], scale=-0.5) \
                    .then_inc(ac.h, 1)
            t3 = ac.inc(2)
            blk.scalar(ln_rstd)

            def ln_norm(e, sl=sl, t=t3, xf=xnb_free_mm):
                e.wait_ge(ac.h, t)
                if xf:
                    e.wait_ge(mm.h, xf)
                xs = xt[:, sl:sl + H]
                e.tensor_scalar(xnb[:], xs, bn[:, 48:49], rstd[:],
                                ALU.subtract, ALU.mult).then_inc(dv.h, 1)
            t4 = dv.inc()
            blk.vector(ln_norm)

            if i + 2 < NT:
                def ld_next(e, i=i, t=t4, _dpld=ld.v):
                    _cld = _dpld
                    e.wait_ge(dv.h, t)
                    e.wait_ge(ld.h, _cld)
                    e.dma_start(out=xt[:, (i % 2) * H:(i % 2) * H + H],
                                in_=x_in[(i + 2) * 128:(i + 3) * 128, :]) \
                        .then_inc(ld.h, 16)
                    _cld += 16
                blk.sync(ld_next)
                x_loaded.append(ld.inc(16))

            for cb in range(32):
                b = cb % 2

                def tns(e, cb=cb, b=b, fr=pst_free[b], t=t4):
                    e.wait_ge(dv.h, t)
                    if fr:
                        e.wait_ge(ev.h, fr)
                    e.transpose(pst[b][:, 0:128], xnb[:, cb * 128:(cb + 1) * 128], ident[:]) \
                        .then_inc(mm.h, 1)
                rd = mm.inc()
                blk.tensor(tns)

                def dve(e, rd=rd, b=b, i=i, cb=cb):
                    e.wait_ge(mm.h, rd)
                    e.tensor_copy(xnT[:, cb * NTOK + i * 128:cb * NTOK + (i + 1) * 128],
                                  pst[b][:, 0:128]).then_inc(ev.h, 1)
                pst_free[b] = ev.inc()
                blk.vector(dve)
            xnb_free_mm = mm.v
        XNT_EV = ev.v

        def xnT_store(e, t=XNT_EV, _dpst=st.v):
            _cst = _dpst
            e.wait_ge(ev.h, t)
            for cb in range(32):
                e.wait_ge(st.h, _cst)
                e.dma_start(out=xnT_out[cb, :, :],
                            in_=xnT[:, cb * NTOK:(cb + 1) * NTOK]).then_inc(st.h, 16)
                _cst += 16
        blk.sync(xnT_store)
        st.inc(16 * 32)

        psf = [ps_f0, ps_f1]
        psf_free = [COEF_EV, 0]
        wstage_free_st = 0
        qkv_mm_done = []
        qkv_ac_done = []
        qkv_st_done = []
        deq_slot = {8 + 2 * k: k for k in range(8)}

        def emit_qkv(ft):
            def wload(e, ft=ft, prev=(qkv_mm_done[ft - 2] if ft >= 2 else 0), _dpgw=gw.v):
                _cgw = _dpgw
                if prev:
                    e.wait_ge(mm.h, prev)
                e.wait_ge(gw.h, _cgw)
                e.dma_start(out=wbuf[:, (ft % 2) * 4096:(ft % 2 + 1) * 4096],
                            in_=wqkv_in[ft, :, :]).then_inc(gw.h, 16)
                _cgw += 16
            blk.gpsimd(wload)
            wv = gw.inc(16)
            pso = [ps_q0, ps_q1][ft % 2]

            def qmm(e, ft=ft, wv=wv, pso=pso, xv=XNT_EV, dvo=ONES_DV, acb=GW_BQ,
                    evprev=(qkv_ac_done[ft - 2] if ft >= 2 else 0)):
                e.wait_ge(gw.h, wv)
                e.wait_ge(ev.h, xv)
                e.wait_ge(dv.h, dvo)
                e.wait_ge(gw.h, acb)
                if evprev:
                    e.wait_ge(ac.h, evprev)
                e.matmul(pso[:, 0:512], bqb[:, ft * 128:(ft + 1) * 128],
                         ones[:, 0:512], start=True, stop=False)
                e.matmul(pso[:, 512:1024], bqb[:, ft * 128:(ft + 1) * 128],
                         ones[:, 512:1024], start=True, stop=False)
                for c in range(32):
                    lhsT = wbuf[:, (ft % 2) * 4096 + c * 128:(ft % 2) * 4096 + (c + 1) * 128]
                    rhs = xnT[:, c * NTOK:(c + 1) * NTOK]
                    r = mm_split(e, pso, lhsT, rhs, NTOK, start=False, stop=(c == 31))
                r.then_inc(mm.h, 1)
            qkv_mm_done.append(mm.inc())
            blk.tensor(qmm)

            def qev(e, ft=ft, pso=pso, t=qkv_mm_done[ft],
                    stp=(qkv_st_done[ft - 2] if ft >= 2 else 0)):
                e.wait_ge(mm.h, t)
                if stp:
                    e.wait_ge(st.h, stp)
                e.copy(qkvTs[:, (ft % 2) * NTOK:(ft % 2 + 1) * NTOK], pso[:]) \
                    .then_inc(ac.h, 1)
            qkv_ac_done.append(ac.inc())
            blk.scalar(qev)

            def qst(e, ft=ft, t=qkv_ac_done[ft], _dpst=st.v):
                _cst = _dpst
                e.wait_ge(ac.h, t)
                e.wait_ge(st.h, _cst)
                e.dma_start(out=qkvT_out[ft, :, :],
                            in_=qkvTs[:, (ft % 2) * NTOK:(ft % 2 + 1) * NTOK]) \
                    .then_inc(st.h, 16)
                _cst += 16
            blk.sync(qst)
            qkv_st_done.append(st.inc(16))

        def emit_deq_block(k):
            nonlocal wstage_free_st
            ot, ih = k // 2, k % 2

            if True:
                def extract(e, ot=ot, ih=ih, need=GW_QW, ce=COEF_EV, base=dv.v):
                    e.wait_ge(gw.h, need)
                    e.wait_ge(ev.h, ce)
                    t = base
                    e.wait_ge(dv.h, t)
                    for pl in range(4):
                        qv = qws[:, (ot * 4 + pl) * 128 + ih * 64:
                                 (ot * 4 + pl) * 128 + ih * 64 + 64]
                        r = None
                        for j in range(32):
                            r = e.tensor_scalar(
                                ibits.ap().rearrange("p (w j) -> p w j", j=32)[:, :, j],
                                qv, j, 1, ALU.logical_shift_right, ALU.bitwise_and)
                        r.then_inc(dv.h, 1)
                        t += 1
                        e.wait_ge(dv.h, t)
                        e.tensor_copy(bits[:, pl * 2048:(pl + 1) * 2048], ibits[:]) \
                            .then_inc(dv.h, 1)
                        t += 1
                        e.wait_ge(dv.h, t)
                dv.inc(8)
                blk.vector(extract)

                def multilin(e, ot=ot, base=dv.v):
                    t = base
                    e.wait_ge(dv.h, t)
                    bb = [bits[:, pl * 2048:(pl + 1) * 2048] for pl in range(4)]

                    def step(r):
                        nonlocal t
                        t += 1
                        r.then_inc(dv.h, 1)
                        e.wait_ge(dv.h, t)

                    def a(s):
                        return coef[:, ot * 16 + s:ot * 16 + s + 1]

                    def emit_G(out, T):
                        step(e.tensor_scalar(out, bb[0], a(T + 1), a(T),
                                             ALU.mult, ALU.add))
                        step(e.scalar_tensor_tensor(out, bb[1], a(T + 2), out,
                                                    ALU.mult, ALU.add))
                        step(e.scalar_tensor_tensor(out, p01[:], a(T + 3), out,
                                                    ALU.mult, ALU.add))

                    step(e.tensor_tensor(p01[:], bb[0], bb[1], ALU.mult))
                    emit_G(dq2[:], 12)
                    step(e.tensor_tensor(dq2[:], bb[2], dq2[:], ALU.mult))
                    emit_G(dq1[:], 8)
                    step(e.tensor_tensor(dq1[:], dq1[:], dq2[:], ALU.add))
                    step(e.tensor_tensor(dq1[:], bb[3], dq1[:], ALU.mult))
                    emit_G(dq2[:], 4)
                    step(e.tensor_tensor(dq2[:], bb[2], dq2[:], ALU.mult))
                    emit_G(wacc[:], 0)
                    step(e.tensor_tensor(wacc[:], wacc[:], dq2[:], ALU.add))
                    step(e.tensor_tensor(wacc[:], wacc[:], dq1[:], ALU.add))
                n_steps = 19
                blk.vector(multilin)
                ML_DONE = dv.inc(n_steps)

                for cb in range(16):
                    b = cb % 2

                    def tns(e, cb=cb, b=b, fr=psf_free[b], t=ML_DONE, ws=wstage_free_st):
                        e.wait_ge(dv.h, t)
                        if ws:
                            e.wait_ge(st.h, ws)
                        if fr:
                            e.wait_ge(ev.h, fr)
                        e.transpose(psf[b][:, 0:128], wacc[:, cb * 128:(cb + 1) * 128], identf[:]) \
                            .then_inc(mm.h, 1)
                    rd = mm.inc()
                    blk.tensor(tns)

                    def dve(e, rd=rd, b=b, cb=cb):
                        e.wait_ge(mm.h, rd)
                        e.tensor_copy(wstage[:, cb * 128:(cb + 1) * 128], psf[b][:, 0:128]) \
                            .then_inc(ev.h, 1)
                    psf_free[b] = ev.inc()
                    blk.vector(dve)
                WS_EV = ev.v

                def wl_store(e, ot=ot, ih=ih, t=WS_EV, _dpst=st.v):
                    _cst = _dpst
                    e.wait_ge(ev.h, t)
                    for cb in range(16):
                        ic = ih * 16 + cb
                        e.wait_ge(st.h, _cst)
                        e.dma_start(out=wlT_out[ic, :, ot * 128:(ot + 1) * 128],
                                    in_=wstage[:, cb * 128:(cb + 1) * 128]) \
                            .then_inc(st.h, 16)
                        _cst += 16
                blk.sync(wl_store)
                wstage_free_st = st.inc(16 * 16)

        for ft in range(24):
            emit_qkv(ft)
            if ft in deq_slot:
                emit_deq_block(deq_slot[ft])

        def fin(e, t=st.v):
            e.wait_ge(st.h, t)
        blk.gpsimd(fin)

    return nc




def build_launch2():
    nc = bass.Bass("TRN2", target_bir_lowering=False)
    x_in = nc.dram_tensor("x", [NTOK, H], F32, kind="ExternalInput").ap()
    xnT_in = nc.dram_tensor("xnT", [32, 128, NTOK], BF16, kind="ExternalInput").ap()
    qT_in = nc.dram_tensor("qT", [8, 128, NTOK], BF16, kind="ExternalInput").ap()
    kt_in = nc.dram_tensor("kt", [32, 128, 1024], BF16, kind="ExternalInput").ap()
    v_in = nc.dram_tensor("v", [128, 32768], BF16, kind="ExternalInput").ap()
    wl_in = nc.dram_tensor("wl", [8, 128, 16384], BF16, kind="ExternalInput").ap()
    wo_in = nc.dram_tensor("wo", [4, 8, 128, 1024], F32, kind="ExternalInput").ap()
    bo_in = nc.dram_tensor("bo", [1, H], F32, kind="ExternalInput").ap()
    bffn_in = nc.dram_tensor("bffn", [1, H], F32, kind="ExternalInput").ap()
    w1_in = nc.dram_tensor("w1", [48, 128, H], F32, kind="ExternalInput").ap()
    cos_in = nc.dram_tensor("cosT", [128, NTOK], F32, kind="ExternalInput").ap()
    sin_in = nc.dram_tensor("sinT", [128, NTOK], F32, kind="ExternalInput").ap()
    wr_in = nc.dram_tensor("wrT", [128, 256], F32, kind="ExternalInput").ap()
    at_in = nc.dram_tensor("at2", [128, H], F32, kind="ExternalInput").ap()
    brr_in = nc.dram_tensor("brr", [128, H], F32, kind="ExternalInput").ap()
    wb_in = nc.dram_tensor("wb", [8, 128, 16384], F32, kind="ExternalInput").ap()
    bb_in = nc.dram_tensor("bb", [1, H], F32, kind="ExternalInput").ap()
    sel_in = nc.dram_tensor("sel", [8, 128], F32, kind="ExternalInput").ap()
    ident_in = nc.dram_tensor("ident", [128, 128], BF16, kind="ExternalInput").ap()

    x1_out = nc.dram_tensor("x1", [NTOK, H], F32, kind="ExternalOutput").ap()
    g1_out = nc.dram_tensor("g1", [NTOK, H], BF16, kind="ExternalOutput").ap()
    q1T_out = nc.dram_tensor("q1T", [32, 128, NTOK], BF16, kind="ExternalOutput").ap()
    k1T_out = nc.dram_tensor("k1T", [8, 128, NTOK], BF16, kind="ExternalOutput").ap()
    v1T_out = nc.dram_tensor("v1T", [8, 128, NTOK], BF16, kind="ExternalOutput").ap()
    s0_dram = nc.dram_tensor("s0scr", [8, 4, 128, 1024], F32, kind="Internal").ap()

    from contextlib import ExitStack
    with ExitStack() as ctx:
        def sb(name, shape, dt):
            return ctx.enter_context(nc.sbuf_tensor(name, shape, dt))

        def psum(name, shape, dt=F32):
            return ctx.enter_context(nc.psum_tensor(name, shape, dt))

        AR0 = sb("AR0", [128, 32768], BF16)
        AR1 = sb("AR1", [128, 32768], BF16)
        AR2 = sb("AR2", [128, 8192], BF16)
        arena = sb("arena", [128, 8192], F32)
        scr_b = sb("scr_b", [128, 4096], BF16)
        bias = sb("bias", [1, H], BF16)
        rbuf = sb("rbuf", [128, NTOK], F32)
        onesf = sb("onesf", [1, 128], F32)
        ones = sb("ones2", [1, NTOK], BF16)
        ones_col = sb("ones_col", [128, 1], BF16)
        consts = sb("consts2", [128, 4], F32)
        bn = sb("bn2", [128, 104], F32)
        rstd = sb("rstd2", [128, 2], F32)
        wr2 = sb("wr2", [128, 256], BF16)
        sel2 = sb("sel2", [8, 128], BF16)
        rout = sb("rout", [128, 32], F32)
        routb = sb("routb", [128, 8], BF16)
        combT = sb("combT", [8, NTOK], BF16)
        ident = sb("ident2", [128, 128], BF16)
        ps_a = psum("ps2_a", [128, 1024], F32)
        ps_b = psum("ps2_b", [128, 1024], F32)
        ps_c1 = psum("ps2_c1", [128, 512], F32)
        ps_c2 = psum("ps2_c2", [128, 512], F32)
        ps_t0 = psum("ps2_t0", [128, 1024], BF16)
        ps_t1 = psum("ps2_t1", [128, 1024], BF16)
        xtile = arena[:, 0:4096]
        ktb = scr_b[:, 0:2048]
        cosb = rbuf[:]
        sinb = arena[:, 4096:5120]
        rvec = arena[0:1, 7168:8192]
        xnb = scr_b[:, 0:4096]
        qkb = scr_b[:, 0:2048]
        hbuf = scr_b[:, 2048:3072]
        cmat = scr_b[:, 3072:4096]
        at2 = AR2[:, 0:4096]
        brr2 = AR2[:, 4096:8192]
        ld = Sem(ctx.enter_context(nc.semaphore("ld2")))
        gw = Sem(ctx.enter_context(nc.semaphore("gw2")))
        mm = Sem(ctx.enter_context(nc.semaphore("mm2")))
        dv = Sem(ctx.enter_context(nc.semaphore("dv2")))
        ev = Sem(ctx.enter_context(nc.semaphore("ev2")))
        ac = Sem(ctx.enter_context(nc.semaphore("ac2")))
        st = Sem(ctx.enter_context(nc.semaphore("st2")))
        blk = ctx.enter_context(nc.Block())

        def c0(e, _dpld=ld.v):
            _cld = _dpld
            e.wait_ge(ld.h, _cld)
            e.dma_start(out=ident[:], in_=ident_in[:]).then_inc(ld.h, 16)
            _cld += 16
            e.wait_ge(ld.h, _cld)
            e.dma_start(out=AR0[:], in_=v_in[:]).then_inc(ld.h, 16)
            _cld += 16
            e.wait_ge(ld.h, _cld)
            e.dma_start(out=AR2[:].rearrange("p (dc n) -> p dc n", n=NTOK),
                        in_=qT_in[:].rearrange("dc p n -> p dc n")).then_inc(ld.h, 16)
            _cld += 16
        blk.sync(c0)
        LD_A = ld.inc(48)

        def g_c(e, _dpgw=gw.v):
            _cgw = _dpgw
            e.wait_ge(gw.h, _cgw)
            e.dma_start(out=bias[:], in_=bo_in[:]).then_inc(gw.h, 16)
            _cgw += 16
            e.wait_ge(gw.h, _cgw)
            e.dma_start(out=sel2[:], in_=sel_in[:]).then_inc(gw.h, 16)
            _cgw += 16
            e.wait_ge(gw.h, _cgw)
            e.dma_start(out=wr2[:], in_=wr_in[:]).then_inc(gw.h, 16)
            _cgw += 16
        blk.gpsimd(g_c)
        gw.inc(48)
        GW_BO = GW_EMISC = gw.v

        def v_c(e):
            e.memset(ones[:], 1.0)
            e.memset(ones_col[:], 1.0)
            e.memset(onesf[:], 1.0)
            e.memset(consts[:, 0:1], -SHIFT)
            e.memset(consts[:, 1:2], 1e-5)
            e.memset(consts[:, 2:3], 0.0).then_inc(dv.h, 1)
        blk.vector(v_c)
        DV_C = dv.inc()

        MM_A0 = mm.v
        AC_A0 = ac.v

        def a_exps(e, mm0=MM_A0, dvc=DV_C):
            e.wait_ge(dv.h, dvc)
            for kt in range(32):
                e.wait_ge(mm.h, mm0 + 2 * kt + 1)
                pso = [ps_a, ps_b][kt % 2]
                e.activation(AR1[:, kt * 1024:(kt + 1) * 1024], pso[:],
                             AF.Exp, bias=consts[:, 0:1], scale=1.0 / 32.0) \
                    .then_inc(ac.h, 1)
        blk.scalar(a_exps)
        exp_ac = [ac.inc() for _ in range(32)]
        assert exp_ac == [AC_A0 + kt + 1 for kt in range(32)]

        s_mm_done = []
        SUM_MM = 0
        for kt in range(32):
            def k_ld(e, kt=kt, prev=(s_mm_done[kt - 2] if kt >= 2 else 0), _dpld=ld.v):
                _cld = _dpld
                if prev:
                    e.wait_ge(mm.h, prev)
                e.wait_ge(ld.h, _cld)
                e.dma_start(out=ktb[:, (kt % 2) * 1024:(kt % 2 + 1) * 1024],
                            in_=kt_in[kt, :, :]).then_inc(ld.h, 16)
                _cld += 16
            blk.sync(k_ld)
            kv = ld.inc(16)

            pso = [ps_a, ps_b][kt % 2]

            def s_mm(e, kt=kt, kv=kv, need=LD_A, dvc=DV_C, pso=pso,
                     acprev=(exp_ac[kt - 2] if kt >= 2 else 0)):
                e.wait_ge(ld.h, max(kv, need))
                e.wait_ge(dv.h, dvc)
                if acprev:
                    e.wait_ge(ac.h, acprev)
                r = None
                for dc in range(8):
                    lhsT = ktb[:, (kt % 2) * 1024 + dc * 128:(kt % 2) * 1024 + (dc + 1) * 128]
                    rhs = AR2[:, dc * NTOK:(dc + 1) * NTOK]
                    r = mm_split(e, pso, lhsT, rhs, NTOK, start=(dc == 0), stop=(dc == 7))
                r.then_inc(mm.h, 1)
            s_mm_done.append(mm.inc())
            assert s_mm_done[kt] == MM_A0 + 2 * kt + 1
            blk.tensor(s_mm)

            def s_sum(e, kt=kt, t=exp_ac[kt]):
                e.wait_ge(ac.h, t)
                e.matmul(ps_c1[0:1, 0:512], ones_col[:],
                         AR1[:, kt * 1024:kt * 1024 + 512],
                         start=(kt == 0), stop=(kt == 31))
                e.matmul(ps_c2[0:1, 0:512], ones_col[:],
                         AR1[:, kt * 1024 + 512:(kt + 1) * 1024],
                         start=(kt == 0), stop=(kt == 31)).then_inc(mm.h, 1)
            SUM_MM = mm.inc()
            blk.tensor(s_sum)
        AEXP_AC = ac.v

        def recip(e, t=SUM_MM, base=dv.v):
            e.wait_ge(mm.h, t)
            e.wait_ge(dv.h, base)
            e.reciprocal(rvec[:, 0:512], ps_c1[0:1, 0:512]).then_inc(dv.h, 1)
            e.wait_ge(dv.h, base + 1)
            e.reciprocal(rvec[:, 512:1024], ps_c2[0:1, 0:512]).then_inc(dv.h, 1)
        RECIP_DV = dv.inc(2)
        blk.vector(recip)

        def rb_mm(e, t=RECIP_DV):
            e.wait_ge(dv.h, t)
            e.matmul(ps_b[:, 0:512], onesf[:], rvec[:, 0:512], start=True, stop=True)
            e.matmul(ps_b[:, 512:1024], onesf[:], rvec[:, 512:1024],
                     start=True, stop=True).then_inc(mm.h, 1)
        RB_MM = mm.inc()
        blk.tensor(rb_mm)

        def rb_ev(e, t=RB_MM):
            e.wait_ge(mm.h, t)
            e.tensor_copy(rbuf[:], ps_b[:]).then_inc(dv.h, 1)
        RB_DV = dv.inc()
        blk.vector(rb_ev)

        av_ev = []
        for dt in range(8):
            def av(e, dt=dt, t=AEXP_AC, rb=RB_DV,
                   evprev=(av_ev[dt - 1] if dt >= 1 else 0)):
                e.wait_ge(ac.h, t)
                e.wait_ge(dv.h, rb)
                if evprev:
                    e.wait_ge(dv.h, evprev)
                r = None
                for kt in range(32):
                    lhsT = AR0[:, (kt * 8 + dt) * 128:(kt * 8 + dt + 1) * 128]
                    rhs = AR1[:, kt * 1024:(kt + 1) * 1024]
                    r = mm_split(e, ps_b, lhsT, rhs, NTOK,
                                 start=(kt == 0), stop=(kt == 31))
                r.then_inc(mm.h, 1)
            t_av = mm.inc()
            blk.tensor(av)

            def av_e(e, dt=dt, t=t_av, rb=RB_DV):
                e.wait_ge(mm.h, t)
                e.wait_ge(dv.h, rb)
                e.tensor_tensor(AR2[:, dt * 1024:(dt + 1) * 1024], ps_b[:], rbuf[:],
                                ALU.mult).then_inc(dv.h, 1)
            av_ev.append(dv.inc())
            blk.vector(av_e)
        AV_DV = dv.v
        AV_MM = mm.v

        wo_ld = []
        wo_mm_done = []
        wo_ac_done = []
        wo_st_done = []
        for og in range(4):
            def wo_l(e, og=og, avt=AV_MM,
                     prev=(wo_mm_done[8 * (og - 2) + 7] if og >= 2 else 0),
                     _dpgw=gw.v):
                _cgw = _dpgw
                e.wait_ge(mm.h, max(avt, prev))
                for dc in range(8):
                    e.wait_ge(gw.h, _cgw)
                    e.dma_start(out=AR1[:, (og % 2) * 8192 + dc * 1024:
                                        (og % 2) * 8192 + (dc + 1) * 1024],
                                in_=wo_in[og, dc, :, :]).then_inc(gw.h, 16)
                    _cgw += 16
            blk.gpsimd(wo_l)
            wo_ld.append(gw.inc(128))

            for tok in range(8):
                pso = [ps_a, ps_b][tok % 2]

                def wo_mm(e, og=og, tok=tok, pso=pso, wv=wo_ld[og], avt=AV_DV,
                          bo=GW_BO,
                          acp=(wo_ac_done[-2] if len(wo_ac_done) >= 2 else 0)):
                    e.wait_ge(gw.h, max(wv, bo))
                    e.wait_ge(dv.h, avt)
                    if acp:
                        e.wait_ge(ac.h, acp)
                    e.matmul(pso[:, 0:512], ones[0:1, 0:128],
                             bias[0:1, og * 1024:og * 1024 + 512],
                             start=True, stop=False)
                    e.matmul(pso[:, 512:1024], ones[0:1, 0:128],
                             bias[0:1, og * 1024 + 512:(og + 1) * 1024],
                             start=True, stop=False)
                    r = None
                    for dc in range(8):
                        lhsT = AR2[:, dc * 1024 + tok * 128:dc * 1024 + (tok + 1) * 128]
                        rhs = AR1[:, (og % 2) * 8192 + dc * 1024:
                                  (og % 2) * 8192 + (dc + 1) * 1024]
                        r = mm_split(e, pso, lhsT, rhs, NTOK,
                                     start=False, stop=(dc == 7))
                    r.then_inc(mm.h, 1)
                wo_mm_done.append(mm.inc())
                blk.tensor(wo_mm)

                def wo_ev(e, og=og, tok=tok, pso=pso, t=wo_mm_done[-1],
                          stp=(wo_st_done[-2] if len(wo_st_done) >= 2 else 0)):
                    e.wait_ge(mm.h, t)
                    if stp:
                        e.wait_ge(st.h, stp)
                    e.copy(arena[:, (tok % 2) * 1024:(tok % 2 + 1) * 1024], pso[:]) \
                        .then_inc(ac.h, 1)
                wo_ac_done.append(ac.inc())
                blk.scalar(wo_ev)

                def wo_s(e, og=og, tok=tok, t=wo_ac_done[-1], _dpst=st.v):
                    _cst = _dpst
                    e.wait_ge(ac.h, t)
                    e.wait_ge(st.h, _cst)
                    e.dma_start(out=s0_dram[tok, og, :, :],
                                in_=arena[:, (tok % 2) * 1024:(tok % 2 + 1) * 1024]) \
                        .then_inc(st.h, 16)
                    _cst += 16
                blk.sync(wo_s)
                wo_st_done.append(st.inc(16))
        PA_ST = st.v
        PA_MM = mm.v
        PA_AC = ac.v

        def b_ld(e, t=PA_MM, _dpld=ld.v):
            _cld = _dpld
            e.wait_ge(mm.h, t)
            e.wait_ge(ld.h, _cld)
            e.dma_start(out=AR0[:].rearrange("p (cb n) -> p cb n", n=NTOK),
                        in_=xnT_in[:].rearrange("cb p n -> p cb n")).then_inc(ld.h, 16)
            _cld += 16
        blk.sync(b_ld)
        B_XNT = ld.inc(16)

        def b_bias(e, t=PA_AC, _dpgw=gw.v):
            _cgw = _dpgw
            e.wait_ge(ac.h, t)
            e.wait_ge(gw.h, _cgw)
            e.dma_start(out=bias[:], in_=bffn_in[:]).then_inc(gw.h, 16)
            _cgw += 16
        blk.gpsimd(b_bias)
        B_BIAS = gw.inc(16)

        wl_ld = []
        yl_mm_done = []
        yl_ac_done = []
        b_st_done = []
        b_gate_dvs = []
        for og in range(8):
            def wl_l(e, og=og, pa=PA_MM,
                     prev=(yl_mm_done[8 * (og - 2) + 7] if og >= 2 else 0),
                     _dpgw=gw.v):
                _cgw = _dpgw
                e.wait_ge(mm.h, max(pa, prev))
                e.wait_ge(gw.h, _cgw)
                e.dma_start(out=AR1[:, (og % 2) * 16384:(og % 2 + 1) * 16384],
                            in_=wl_in[og, :, :]).then_inc(gw.h, 16)
                _cgw += 16
            blk.gpsimd(wl_l)
            wl_ld.append(gw.inc(16))

            for tok in range(8):
                pso = [ps_a, ps_b][tok % 2]
                aslot = ((og * 8 + tok) % 4) * 512

                def yl_mm(e, og=og, tok=tok, pso=pso, wv=wl_ld[og], xv=B_XNT,
                          bi=B_BIAS, pa=PA_AC,
                          acp=(yl_ac_done[-2] if len(yl_ac_done) >= 2 else 0)):
                    e.wait_ge(ld.h, xv)
                    e.wait_ge(gw.h, max(bi, wv))
                    e.wait_ge(ac.h, max(acp, pa))
                    e.matmul(pso[:, 0:512], ones[0:1, 0:128],
                             bias[0:1, og * 512:(og + 1) * 512],
                             start=True, stop=False)
                    r = None
                    for ic in range(32):
                        lhsT = AR0[:, ic * 1024 + tok * 128:ic * 1024 + (tok + 1) * 128]
                        rhs = AR1[:, (og % 2) * 16384 + ic * 512:
                                  (og % 2) * 16384 + (ic + 1) * 512]
                        r = e.matmul(pso[:, 0:512], lhsT, rhs,
                                     start=False, stop=(ic == 31))
                    r.then_inc(mm.h, 1)
                yl_mm_done.append(mm.inc())
                blk.tensor(yl_mm)

                def yl_ev(e, og=og, tok=tok, pso=pso, aslot=aslot,
                          t=yl_mm_done[-1], pa=PA_ST,
                          gp=(b_gate_dvs[-4] if len(b_gate_dvs) >= 4 else 0)):
                    e.wait_ge(mm.h, t)
                    e.wait_ge(st.h, pa)
                    if gp:
                        e.wait_ge(dv.h, gp)
                    e.activation(arena[:, aslot:aslot + 512],
                                 pso[:, 0:512], AF.Silu, bias=consts[:, 2:3],
                                 scale=1.0).then_inc(ac.h, 1)
                yl_ac_done.append(ac.inc())
                blk.scalar(yl_ev)

                def b_gld(e, og=og, tok=tok, aslot=aslot, sw=PA_ST,
                          gp=(b_gate_dvs[-4] if len(b_gate_dvs) >= 4 else 0),
                          _dpld=ld.v):
                    _cld = _dpld
                    e.wait_ge(st.h, sw)
                    if gp:
                        e.wait_ge(dv.h, gp)
                    e.wait_ge(ld.h, _cld)
                    e.dma_start(out=arena[:, 2048 + aslot:2048 + aslot + 512],
                                in_=s0_dram[tok, og // 2, :,
                                            (og % 2) * 512:(og % 2) * 512 + 512]) \
                        .then_inc(ld.h, 16)
                    _cld += 16
                    e.wait_ge(ld.h, _cld)
                    e.dma_start(out=arena[:, 4096 + aslot:4096 + aslot + 512],
                                in_=x_in[tok * 128:(tok + 1) * 128,
                                         og * 512:(og + 1) * 512]).then_inc(ld.h, 16)
                    _cld += 16
                blk.sync(b_gld)
                gv = ld.inc(32)

                def b_gate(e, og=og, tok=tok, aslot=aslot, t=yl_ac_done[-1],
                           gv=gv, base=dv.v,
                           xp=(b_st_done[-2] if len(b_st_done) >= 2 else 0)):
                    e.wait_ge(ac.h, t)
                    e.wait_ge(ld.h, gv)
                    if xp:
                        e.wait_ge(st.h, xp)
                    gsl = 6144 + (tok % 2) * 512
                    e.tensor_tensor(arena[:, gsl:gsl + 512],
                                    arena[:, 2048 + aslot:2048 + aslot + 512],
                                    arena[:, aslot:aslot + 512],
                                    ALU.mult).then_inc(dv.h, 1)
                    e.wait_ge(dv.h, base + 1)
                    e.tensor_tensor(arena[:, gsl:gsl + 512],
                                    arena[:, 4096 + aslot:4096 + aslot + 512],
                                    arena[:, gsl:gsl + 512], ALU.add) \
                        .then_inc(dv.h, 1)
                b_gate_dv = dv.inc(2)
                b_gate_dvs.append(b_gate_dv)
                blk.vector(b_gate)

                def b_x1s(e, og=og, tok=tok, t=b_gate_dv, _dpst=st.v):
                    _cst = _dpst
                    e.wait_ge(dv.h, t)
                    e.wait_ge(st.h, _cst)
                    gsl = 6144 + (tok % 2) * 512
                    e.dma_start(out=x1_out[tok * 128:(tok + 1) * 128,
                                           og * 512:(og + 1) * 512],
                                in_=arena[:, gsl:gsl + 512]).then_inc(st.h, 16)
                    _cst += 16
                blk.sync(b_x1s)
                b_st_done.append(st.inc(16))
        PB_ST = st.v
        PB_MM = mm.v

        pst = [ps_t0, ps_t1]
        pst_free = [0, 0]
        xt2 = [arena[:, 0:4096], arena[:, 4096:8192]]
        xnbs = [AR1[:, 0:4096], AR1[:, 4096:8192]]
        c_stats_dv = [0] * NT
        c_rstd_ac = [0] * NT
        c_norm_dv = [0] * NT
        c_tns_mm = [0] * NT

        def emit_tns_tile(i):
            for g in range(8):
                b = g % 2

                def tns4(e, i=i, g=g, b=b, fr=pst_free[b], t=c_norm_dv[i]):
                    e.wait_ge(dv.h, t)
                    if fr:
                        e.wait_ge(ev.h, fr)
                    r = None
                    for j in range(4):
                        cb = g * 4 + j
                        r = e.transpose(pst[b][:, j * 128:(j + 1) * 128],
                                        xnbs[i % 2][:, cb * 128:(cb + 1) * 128],
                                        ident[:])
                    r.then_inc(mm.h, 1)
                rd = mm.inc()
                blk.tensor(tns4)

                def dve4(e, rd=rd, b=b, i=i, g=g):
                    e.wait_ge(mm.h, rd)
                    e.tensor_copy(
                        AR0[:].rearrange("p (cb n) -> p cb n", n=NTOK)
                        [:, g * 4:(g + 1) * 4, i * 128:(i + 1) * 128],
                        pst[b][:, 0:512].rearrange("p (j n) -> p j n", n=128)) \
                        .then_inc(ev.h, 1)
                pst_free[b] = ev.inc()
                blk.vector(dve4)
            c_tns_mm[i] = mm.v

        for i in range(NT):
            def c_l(e, i=i, t=PB_ST,
                    pn=(c_norm_dv[i - 2] if i >= 2 else 0), _dpld=ld.v):
                _cld = _dpld
                e.wait_ge(st.h, t)
                if pn:
                    e.wait_ge(dv.h, pn)
                e.wait_ge(ld.h, _cld)
                e.dma_start(out=xt2[i % 2], in_=x1_out[i * 128:(i + 1) * 128, :]) \
                    .then_inc(ld.h, 16)
                _cld += 16
            blk.sync(c_l)
            xl = ld.inc(16)

            def c_stats(e, i=i, xl=xl,
                        ap=(c_rstd_ac[i - 2] if i >= 2 else 0)):
                e.wait_ge(ld.h, xl)
                if ap:
                    e.wait_ge(ac.h, ap)
                bns = bn[:, (i % 2) * 52:(i % 2) * 52 + 52]
                r = None
                for k in range(8):
                    r = e.bn_stats(bns[:, 6 * k:6 * k + 6],
                                   xt2[i % 2][:, 512 * k:512 * (k + 1)])
                r.then_inc(dv.h, 1)
            c_stats_dv[i] = dv.inc()
            blk.vector(c_stats)

            def c_aggr(e, i=i, t=c_stats_dv[i]):
                e.wait_ge(dv.h, t)
                bns = bn[:, (i % 2) * 52:(i % 2) * 52 + 52]
                e.bn_aggr(bns[:, 48:50], bns[:, 0:48]).then_inc(dv.h, 1)
            t2 = dv.inc()
            blk.vector(c_aggr)

            def c_rstd(e, i=i, t=t2, a0=ac.v):
                e.wait_ge(dv.h, t)
                bns = bn[:, (i % 2) * 52:(i % 2) * 52 + 52]
                e.activation(bns[:, 50:51], bns[:, 49:50], AF.Ln,
                             bias=consts[:, 1:2], scale=1.0).then_inc(ac.h, 1)
                e.wait_ge(ac.h, a0 + 1)
                e.activation(rstd[:, (i % 2):(i % 2) + 1], bns[:, 50:51], AF.Exp,
                             bias=consts[:, 2:3], scale=-0.5).then_inc(ac.h, 1)
            c_rstd_ac[i] = ac.inc(2)
            blk.scalar(c_rstd)

            def c_norm(e, i=i, t=c_rstd_ac[i],
                       pm=(c_tns_mm[i - 2] if i >= 2 else PB_MM)):
                e.wait_ge(ac.h, t)
                e.wait_ge(mm.h, pm)
                bns = bn[:, (i % 2) * 52:(i % 2) * 52 + 52]
                e.tensor_scalar(xnbs[i % 2], xt2[i % 2], bns[:, 48:49],
                                rstd[:, (i % 2):(i % 2) + 1],
                                ALU.subtract, ALU.mult).then_inc(dv.h, 1)
            c_norm_dv[i] = dv.inc()
            blk.vector(c_norm)

            if i >= 1:
                emit_tns_tile(i - 1)
        emit_tns_tile(NT - 1)
        PC_EV = ev.v
        PC_MM = mm.v
        C_NORM_LAST = c_norm_dv[NT - 1]

        def d_cs(e, t=PC_MM, cn=C_NORM_LAST, _dpld=ld.v):
            _cld = _dpld
            e.wait_ge(mm.h, t)
            e.wait_ge(dv.h, cn)
            e.wait_ge(ld.h, _cld)
            e.dma_start(out=cosb, in_=cos_in[:]).then_inc(ld.h, 16)
            _cld += 16
            e.wait_ge(ld.h, _cld)
            e.dma_start(out=sinb, in_=sin_in[:]).then_inc(ld.h, 16)
            _cld += 16
        blk.sync(d_cs)
        D_CS = ld.inc(32)
        d_mm_done = []
        d_ac_done = []
        d_dv_done = []
        qkb_st = []
        for ft in range(48):
            def d_wl(e, ft=ft, prev=(d_mm_done[ft - 2] if ft >= 2 else 0),
                     pa=PB_MM, _dpgw=gw.v):
                _cgw = _dpgw
                e.wait_ge(mm.h, max(prev, pa))
                e.wait_ge(gw.h, _cgw)
                e.dma_start(out=AR2[:, (ft % 2) * 4096:(ft % 2 + 1) * 4096],
                            in_=w1_in[ft, :, :]).then_inc(gw.h, 16)
                _cgw += 16
            blk.gpsimd(d_wl)
            wv = gw.inc(16)

            pso = [ps_a, ps_b][ft % 2]

            def d_mm(e, ft=ft, wv=wv, xv=PC_EV, pso=pso,
                     acp=(d_ac_done[ft - 2] if ft >= 2 else 0)):
                e.wait_ge(gw.h, wv)
                e.wait_ge(ev.h, xv)
                if acp:
                    e.wait_ge(ac.h, acp)
                r = None
                for ic in range(32):
                    lhsT = AR2[:, (ft % 2) * 4096 + ic * 128:(ft % 2) * 4096 + (ic + 1) * 128]
                    rhs = AR0[:, ic * NTOK:(ic + 1) * NTOK]
                    r = mm_split(e, pso, lhsT, rhs, NTOK, start=(ic == 0), stop=(ic == 31))
                r.then_inc(mm.h, 1)
            d_mm_done.append(mm.inc())
            blk.tensor(d_mm)

            if ft < 40:
                def d_ev(e, ft=ft, t=d_mm_done[-1], pso=pso,
                         dvp=(d_dv_done[ft - 2] if ft >= 2 else C_NORM_LAST)):
                    e.wait_ge(mm.h, t)
                    if dvp:
                        e.wait_ge(dv.h, dvp)
                    e.copy(arena[:, (ft % 2) * 1024:(ft % 2 + 1) * 1024], pso[:]) \
                        .then_inc(ac.h, 1)
                d_ac_done.append(ac.inc())
                blk.scalar(d_ev)

                def d_rope(e, ft=ft, t=d_ac_done[-1], base=dv.v, cs=D_CS,
                           qp=(qkb_st[ft - 2] if ft >= 2 else 0)):
                    e.wait_ge(ac.h, t)
                    e.wait_ge(ld.h, cs)
                    e.wait_ge(dv.h, base)
                    if qp:
                        e.wait_ge(st.h, qp)
                    sl = (ft % 2) * 1024
                    qk = arena[:, sl:sl + 1024]
                    rt = arena[:, 2048:3072]
                    t1 = arena[:, 3072:4096]
                    tt = base
                    e.tensor_scalar(rt[0:64, :], qk[64:128, :], -1.0, None,
                                    ALU.mult, ALU.bypass).then_inc(dv.h, 1)
                    tt += 1
                    e.wait_ge(dv.h, tt)
                    e.tensor_copy(rt[64:128, :], qk[0:64, :]).then_inc(dv.h, 1)
                    tt += 1
                    e.wait_ge(dv.h, tt)
                    e.tensor_tensor(t1, qk, cosb, ALU.mult).then_inc(dv.h, 1)
                    tt += 1
                    e.wait_ge(dv.h, tt)
                    e.tensor_tensor(rt, rt, sinb, ALU.mult).then_inc(dv.h, 1)
                    tt += 1
                    e.wait_ge(dv.h, tt)
                    e.tensor_tensor(qkb[:, (ft % 2) * 1024:(ft % 2 + 1) * 1024],
                                    t1, rt, ALU.add).then_inc(dv.h, 1)
                d_dv_done.append(dv.inc(5))
                blk.vector(d_rope)

                def d_st(e, ft=ft, t=d_dv_done[-1], _dpst=st.v):
                    _cst = _dpst
                    e.wait_ge(dv.h, t)
                    if ft < 32:
                        e.wait_ge(st.h, _cst)
                        e.dma_start(out=q1T_out[ft, :, :],
                                    in_=qkb[:, (ft % 2) * 1024:(ft % 2 + 1) * 1024]) \
                            .then_inc(st.h, 16)
                        _cst += 16
                    else:
                        e.wait_ge(st.h, _cst)
                        e.dma_start(out=k1T_out[ft - 32, :, :],
                                    in_=qkb[:, (ft % 2) * 1024:(ft % 2 + 1) * 1024]) \
                            .then_inc(st.h, 16)
                        _cst += 16
                blk.sync(d_st)
                qkb_st.append(st.inc(16))
            else:
                def d_evv(e, ft=ft, t=d_mm_done[-1], pso=pso,
                          qp=(qkb_st[ft - 2] if ft >= 2 else 0)):
                    e.wait_ge(mm.h, t)
                    if qp:
                        e.wait_ge(st.h, qp)
                    e.copy(qkb[:, (ft % 2) * 1024:(ft % 2 + 1) * 1024], pso[:]) \
                        .then_inc(ac.h, 1)
                d_ac_done.append(ac.inc())
                blk.scalar(d_evv)

                def d_stv(e, ft=ft, t=d_ac_done[-1], _dpst=st.v):
                    _cst = _dpst
                    e.wait_ge(ac.h, t)
                    e.wait_ge(st.h, _cst)
                    e.dma_start(out=v1T_out[ft - 40, :, :],
                                in_=qkb[:, (ft % 2) * 1024:(ft % 2 + 1) * 1024]) \
                        .then_inc(st.h, 16)
                    _cst += 16
                blk.sync(d_stv)
                qkb_st.append(st.inc(16))
        PD_MM = mm.v
        PD_AC = ac.v
        PD_DV = dv.v
        PD_ST = st.v

        def e_bias(e, t=PD_MM, _dpgw=gw.v):
            _cgw = _dpgw
            e.wait_ge(mm.h, t)
            e.wait_ge(gw.h, _cgw)
            e.dma_start(out=bias[:], in_=bb_in[:]).then_inc(gw.h, 16)
            _cgw += 16
            e.wait_ge(gw.h, _cgw)
            e.dma_start(out=at2, in_=at_in[:]).then_inc(gw.h, 16)
            _cgw += 16
            e.wait_ge(gw.h, _cgw)
            e.dma_start(out=brr2, in_=brr_in[:]).then_inc(gw.h, 16)
            _cgw += 16
        blk.gpsimd(e_bias)
        gw.inc(48)
        E_BIAS = E_AT = gw.v
        comb_dv = []
        combT_ev = []
        for tok in range(8):
            def e_rmm(e, tok=tok, t=PD_AC, xv=PC_EV, wv=GW_EMISC,
                      prev=(comb_dv[tok - 1] if tok >= 1 else 0)):
                e.wait_ge(ac.h, t)
                e.wait_ge(ev.h, xv)
                e.wait_ge(gw.h, wv)
                if prev:
                    e.wait_ge(dv.h, prev)
                r = None
                for ic in range(32):
                    lhsT = AR0[:, ic * 1024 + tok * 128:ic * 1024 + (tok + 1) * 128]
                    rhs = wr2[:, ic * 8:(ic + 1) * 8]
                    r = e.matmul(ps_a[:, 0:8], lhsT, rhs,
                                 start=(ic == 0), stop=(ic == 31))
                r.then_inc(mm.h, 1)
            t_rmm = mm.inc()
            blk.tensor(e_rmm)

            def e_rmax(e, tok=tok, t=t_rmm, base=dv.v):
                e.wait_ge(mm.h, t)
                e.wait_ge(dv.h, base)
                e.reduce_max(rout[:, 24:25], ps_a[:, 0:8], axis=mybir.AxisListType.X) \
                    .then_inc(dv.h, 1)
                e.wait_ge(dv.h, base + 1)
                e.tensor_scalar(rout[:, 25:26], rout[:, 24:25], -1.0, None,
                                ALU.mult, ALU.bypass).then_inc(dv.h, 1)
            t_rmax = dv.inc(2)
            blk.vector(e_rmax)

            def e_rexp(e, tok=tok, t=t_rmax):
                e.wait_ge(dv.h, t)
                e.activation(rout[:, 0:8], ps_a[:, 0:8], AF.Exp,
                             bias=rout[:, 25:26], scale=1.0).then_inc(ac.h, 1)
            t_rexp = ac.inc()
            blk.scalar(e_rexp)

            def e_rsel(e, tok=tok, t=t_rexp, base=dv.v):
                e.wait_ge(ac.h, t)
                tt = base
                e.wait_ge(dv.h, tt)
                el = rout[:, 0:8]

                def step(r):
                    nonlocal tt
                    tt += 1
                    r.then_inc(dv.h, 1)
                    e.wait_ge(dv.h, tt)

                step(e.tensor_scalar(rout[:, 8:16], el, 1.0, None,
                                     ALU.is_ge, ALU.bypass))
                step(e.scalar_tensor_tensor(rout[:, 16:24], rout[:, 8:16], -2.0,
                                            el, ALU.mult, ALU.add))
                step(e.reduce_max(rout[:, 26:27], rout[:, 16:24],
                                  axis=mybir.AxisListType.X))
                step(e.tensor_scalar(rout[:, 8:16], el, rout[:, 26:27], None,
                                     ALU.is_ge, ALU.bypass))
                step(e.tensor_tensor(rout[:, 16:24], el, rout[:, 8:16], ALU.mult))
                step(e.tensor_scalar(rout[:, 27:28], rout[:, 26:27], 1.0, None,
                                     ALU.add, ALU.bypass))
                step(e.reciprocal(rout[:, 28:29], rout[:, 27:28]))
                step(e.tensor_scalar(routb[:], rout[:, 16:24], rout[:, 28:29], None,
                                     ALU.mult, ALU.bypass))
            comb_dv.append(dv.inc(8))
            blk.vector(e_rsel)

            def e_ctr(e, tok=tok, t=comb_dv[-1], pc=PC_EV,
                      fr=(combT_ev[tok - 1] if tok >= 1 else 0)):
                e.wait_ge(dv.h, t)
                e.wait_ge(ev.h, max(pc, fr))
                e.transpose(ps_t0[0:8, 0:128], routb[:], ident[:]).then_inc(mm.h, 1)
            t_ctr = mm.inc()
            blk.tensor(e_ctr)

            def e_cev(e, tok=tok, t=t_ctr):
                e.wait_ge(mm.h, t)
                e.tensor_copy(combT[:, tok * 128:(tok + 1) * 128], ps_t0[0:8, 0:128]) \
                    .then_inc(ev.h, 1)
            combT_ev.append(ev.inc())
            blk.vector(e_cev)
        E_COMBT = ev.v

        def e_cmm(e, t=E_COMBT, wv=GW_EMISC):
            e.wait_ge(ev.h, t)
            e.wait_ge(gw.h, wv)
            r = mm_split(e, ps_b, sel2[:], combT[:], NTOK, start=True, stop=True)
            r.then_inc(mm.h, 1)
        t_cmm = mm.inc()
        blk.tensor(e_cmm)

        def e_cmev(e, t=t_cmm):
            e.wait_ge(mm.h, t)
            e.tensor_copy(cmat, ps_b[:]).then_inc(dv.h, 1)
        E_CMAT = dv.inc()
        blk.vector(e_cmev)

        def e_hmm(e, t=E_CMAT, wv=E_AT, xv=PC_EV):
            e.wait_ge(dv.h, t)
            e.wait_ge(gw.h, wv)
            e.wait_ge(ev.h, xv)
            r = None
            for ic in range(32):
                lhsT = at2[:, ic * 128:(ic + 1) * 128]
                rhs = AR0[:, ic * NTOK:(ic + 1) * NTOK]
                r = mm_split(e, ps_a, lhsT, rhs, NTOK, start=(ic == 0), stop=(ic == 31))
            r.then_inc(mm.h, 1)
        t_hmm = mm.inc()
        blk.tensor(e_hmm)

        def e_hev(e, t=t_hmm):
            e.wait_ge(mm.h, t)
            e.tensor_tensor(hbuf, ps_a[:], cmat, ALU.mult).then_inc(dv.h, 1)
        E_HBUF = dv.inc()
        blk.vector(e_hev)


        e_wld = []
        e_mm_done = []
        e_ac_done = []
        g1_st = []
        for og in range(8):
            def e_wl(e, og=og, pd=PD_MM,
                     prev=(e_mm_done[8 * (og - 2) + 7] if og >= 2 else 0),
                     _dpgw=gw.v):
                _cgw = _dpgw
                e.wait_ge(mm.h, max(pd, prev))
                e.wait_ge(gw.h, _cgw)
                e.dma_start(out=AR1[:, (og % 2) * 16384:(og % 2 + 1) * 16384],
                            in_=wb_in[og, :, :]).then_inc(gw.h, 16)
                _cgw += 16
            blk.gpsimd(e_wl)
            e_wld.append(gw.inc(16))

            for tok in range(8):
                pso = [ps_b, ps_a][tok % 2]
                qslot = ((og * 8 + tok) % 4) * 512

                def eb_mm(e, og=og, tok=tok, pso=pso, wv=max(e_wld[og], E_AT),
                          hb=E_HBUF, bi=E_BIAS,
                          acp=(e_ac_done[-2] if len(e_ac_done) >= 2 else 0)):
                    e.wait_ge(gw.h, max(wv, bi))
                    e.wait_ge(dv.h, hb)
                    if acp:
                        e.wait_ge(ac.h, acp)
                    e.matmul(pso[:, 0:512], ones[0:1, 0:128],
                             bias[0:1, og * 512:(og + 1) * 512],
                             start=True, stop=False)
                    lhsT = hbuf[:, tok * 128:(tok + 1) * 128]
                    e.matmul(pso[:, 0:512], lhsT,
                             brr2[:, og * 512:(og + 1) * 512],
                             start=False, stop=False)
                    r = None
                    for ic in range(32):
                        lh = AR0[:, ic * 1024 + tok * 128:ic * 1024 + (tok + 1) * 128]
                        rhs = AR1[:, (og % 2) * 16384 + ic * 512:
                                  (og % 2) * 16384 + (ic + 1) * 512]
                        r = e.matmul(pso[:, 0:512], lh, rhs,
                                     start=False, stop=(ic == 31))
                    r.then_inc(mm.h, 1)
                e_mm_done.append(mm.inc())
                blk.tensor(eb_mm)

                def eb_ev(e, og=og, tok=tok, pso=pso, qslot=qslot,
                          t=e_mm_done[-1], pdst=PD_ST,
                          gp=(g1_st[-4] if len(g1_st) >= 4 else 0)):
                    e.wait_ge(mm.h, t)
                    e.wait_ge(st.h, max(pdst, gp))
                    e.activation(qkb[:, qslot:qslot + 512],
                                 pso[:, 0:512], AF.Silu, bias=consts[:, 2:3],
                                 scale=1.0).then_inc(ac.h, 1)
                e_ac_done.append(ac.inc())
                blk.scalar(eb_ev)

                def eb_st(e, og=og, tok=tok, qslot=qslot, t=e_ac_done[-1],
                          _dpst=st.v):
                    _cst = _dpst
                    e.wait_ge(ac.h, t)
                    e.wait_ge(st.h, _cst)
                    e.dma_start(out=g1_out[tok * 128:(tok + 1) * 128,
                                           og * 512:(og + 1) * 512],
                                in_=qkb[:, qslot:qslot + 512]).then_inc(st.h, 16)
                    _cst += 16
                blk.sync(eb_st)
                g1_st.append(st.inc(16))

        def fin(e, t=st.v):
            e.wait_ge(st.h, t)
        blk.gpsimd(fin)

    return nc



_CACHE = {}


def _prep_launch1(inputs):
    Wqkv = np.asarray(inputs["L0_attn_Wqkv"], np.float32)
    wqkv_t = np.ascontiguousarray(
        Wqkv.reshape(32, 128, 24, 128).transpose(2, 1, 0, 3).reshape(24, 128, H))
    bqkv = np.ascontiguousarray(
        np.asarray(inputs["L0_attn_bqkv"], np.float32).reshape(1, 3072))
    qw = np.asarray(inputs["L0_ffn_qweight"])
    lut4 = np.asarray(inputs["L0_ffn_lut4"], np.float32)
    mob = mobius_matrix()
    ident = np.eye(128, dtype=BF)
    x = np.asarray(inputs["L0_attn_x"], np.float32).reshape(B * S, H)

    in_maps = []
    for c in range(NCORE := 8):
        rows = core_rows(c)
        o0 = 512 * c
        in_maps.append({
            "x": np.ascontiguousarray(x[rows]),
            "wqkv_t": wqkv_t,
            "bqkv": bqkv,
            "qw": np.ascontiguousarray(
                qw[0:4, o0:o0 + 512, :].reshape(4, 4, 128, 128)),
            "lut4T": np.ascontiguousarray(lut4[o0:o0 + 512, :].T),
            "mob": mob,
            "ident": ident,
            "identf": np.eye(128, dtype=np.float32),
        })
    return in_maps




def build_launch3():
    nc = bass.Bass("TRN2", target_bir_lowering=False)
    x1_in = nc.dram_tensor("x1", [NTOK, H], F32, kind="ExternalInput").ap()
    g1_in = nc.dram_tensor("g1", [NTOK, H], BF16, kind="ExternalInput").ap()
    q1T_in = nc.dram_tensor("q1T", [32, 128, NTOK], BF16, kind="ExternalInput").ap()
    k1_in = nc.dram_tensor("k1", [8, 128, S], BF16, kind="ExternalInput").ap()
    v1_in = nc.dram_tensor("v1", [128, 32768], BF16, kind="ExternalInput").ap()
    wo1_in = nc.dram_tensor("wo1", [8, 32, 128, 512], F32, kind="ExternalInput").ap()
    mask_in = nc.dram_tensor("mask", [128, 4096], BF16, kind="ExternalInput").ap()

    out_out = nc.dram_tensor("out", [NTOK, H], F32, kind="ExternalOutput").ap()

    SC1 = 0.08838834764831843

    from contextlib import ExitStack
    with ExitStack() as ctx:
        def sb(name, shape, dt):
            return ctx.enter_context(nc.sbuf_tensor(name, shape, dt))

        def psum(name, shape, dt=F32):
            return ctx.enter_context(nc.psum_tensor(name, shape, dt))

        AR0 = sb("V1", [128, 32768], BF16)
        AR1 = sb("OT", [128, 32768], BF16)
        k1b = sb("k1b", [128, 2 * 4096], BF16)
        q1h = sb("q1h", [128, 2 * 1024], BF16)
        exps = sb("exps", [128, 4 * 1024], BF16)
        maskb = sb("maskb", [128, 4096], BF16)
        rvec = sb("rvec3", [1, NTOK], F32)
        rrow = sb("rrow3", [1, NTOK], F32)
        onesf = sb("onesf3", [1, 128], F32)
        ones_col = sb("ones_col3", [128, 1], BF16)
        consts = sb("consts3", [128, 2], F32)
        x1t = sb("x1t", [128, 2 * 1024], F32)
        g1t = sb("g1t", [128, 2 * 1024], BF16)
        ot1 = sb("ot1", [128, 2 * 1024], F32)
        ps_s0 = psum("ps3_s0", [128, 1024], F32)
        ps_s1 = psum("ps3_s1", [128, 1024], F32)
        ps_o = psum("ps3_o", [128, 1024], F32)
        ps_r1 = psum("ps3_r1", [128, 512], F32)
        ps_r2 = psum("ps3_r2", [128, 512], F32)
        ld = Sem(ctx.enter_context(nc.semaphore("ld3")))
        gw = Sem(ctx.enter_context(nc.semaphore("gw3")))
        mm = Sem(ctx.enter_context(nc.semaphore("mm3")))
        dv = Sem(ctx.enter_context(nc.semaphore("dv3")))
        ac = Sem(ctx.enter_context(nc.semaphore("ac3")))
        st = Sem(ctx.enter_context(nc.semaphore("st3")))
        blk = ctx.enter_context(nc.Block())
        pss = [ps_s0, ps_s1]

        def c0(e, _dpld=ld.v):
            _cld = _dpld
            e.wait_ge(ld.h, _cld)
            e.dma_start(out=AR0[:], in_=v1_in[:]).then_inc(ld.h, 16)
            _cld += 16
            e.wait_ge(ld.h, _cld)
            e.dma_start(out=maskb[:], in_=mask_in[:]).then_inc(ld.h, 16)
            _cld += 16
        blk.sync(c0)
        LD_C = ld.inc(32)

        def v_c(e):
            e.memset(ones_col[:], 1.0)
            e.memset(onesf[:], 1.0)
            e.memset(consts[:, 0:1], -SHIFT)
            e.memset(consts[:, 1:2], 0.0).then_inc(dv.h, 1)
        blk.vector(v_c)
        DV_C = dv.inc()

        def nq_of(ka):
            return (8 - ka // 4) * 128

        def pe_ops_of_head(h):
            ops = [("smm", 0)]
            if h > 0:
                ops += [("hrbP", -1)]
            for ka in range(1, 32):
                ops += [("smm", ka), ("sum", ka - 1)]
            ops += [("sum", 31)]
            return ops

        plan = {}
        p_mm, p_ac, p_dv = mm.v, ac.v, dv.v
        for h in range(NH):
            for op, ka in pe_ops_of_head(h):
                p_mm += 1
                if op == "hrbP":
                    plan[("hrb", h - 1)] = p_mm
                else:
                    plan[(op, h, ka)] = p_mm
            for ka in range(32):
                p_ac += 1
                plan[("exp", h, ka)] = p_ac
            plan[("svexp", h)] = p_ac + 3
            plan[("otev", h)] = p_ac + 4
            p_ac += 4
            for ka in range(32):
                p_dv += 1
                plan[("mask", h, ka)] = p_dv
                if ka == 0 and h > 0:
                    p_dv += 1
                    plan[("omult", h - 1)] = p_dv
        plan[("hrb", NH - 1)] = p_mm + 1
        plan[("omult", NH - 1)] = p_dv + 1

        k_ld = [0] * 8
        q_ld = [0] * 32
        for h in range(NH):
            kv = h // 4
            if h % 4 == 0:
                def kld(e, kv=kv,
                        prev=(plan[("smm", 4 * (kv - 2) + 3, 31)]
                              if kv >= 2 else 0), _dpld=ld.v):
                    _cld = _dpld
                    if prev:
                        e.wait_ge(mm.h, prev)
                    e.wait_ge(ld.h, _cld)
                    e.dma_start(out=k1b[:, (kv % 2) * 4096:(kv % 2 + 1) * 4096],
                                in_=k1_in[kv, :, :]).then_inc(ld.h, 16)
                    _cld += 16
                blk.sync(kld)
                k_ld[kv] = ld.inc(16)

            def qld(e, h=h,
                    prev=(plan[("smm", h - 2, 31)] if h >= 2 else 0), _dpgw=gw.v):
                _cgw = _dpgw
                if prev:
                    e.wait_ge(mm.h, prev)
                e.wait_ge(gw.h, _cgw)
                e.dma_start(out=q1h[:, (h % 2) * 1024:(h % 2 + 1) * 1024],
                            in_=q1T_in[h, :, :]).then_inc(gw.h, 16)
                _cgw += 16
            blk.gpsimd(qld)
            q_ld[h] = gw.inc(16)

            def pe_head(e, h=h, kv=kv, kl=k_ld[kv], ql=q_ld[h]):
                e.wait_ge(ld.h, kl)
                e.wait_ge(gw.h, ql)
                for op, ka in pe_ops_of_head(h):
                    if op == "smm":
                        b = ka % 2
                        if ka >= 2:
                            e.wait_ge(ac.h, plan[("exp", h, ka - 2)])
                        elif h > 0:
                            e.wait_ge(ac.h, plan[("exp", h - 1, 30 + ka)])
                            if ka == 1:
                                e.wait_ge(dv.h, plan[("omult", h - 1)])
                        nq = nq_of(ka)
                        lhsT = k1b[:, (kv % 2) * 4096 + ka * 128:
                                   (kv % 2) * 4096 + (ka + 1) * 128]
                        rhs = q1h[:, (h % 2) * 1024:(h % 2) * 1024 + nq]
                        r = mm_split(e, pss[b], lhsT, rhs, nq,
                                     start=True, stop=True)
                        r.then_inc(mm.h, 1)
                    elif op == "sum":
                        e.wait_ge(dv.h, plan[("mask", h, ka)])
                        nq = nq_of(ka)
                        sl = (ka % 4) * 1024
                        st0 = (ka == 0)
                        n1 = min(nq, 512)
                        lhsT = AR0[:, (kv * 32 + ka) * 128:
                                   (kv * 32 + ka + 1) * 128]
                        r = e.matmul(ps_r1[0:1, 0:n1], ones_col[:],
                                     exps[:, sl:sl + n1],
                                     start=st0, stop=(ka == 31))
                        r = e.matmul(ps_o[:, 0:n1], lhsT, exps[:, sl:sl + n1],
                                     start=st0, stop=(ka == 31))
                        if nq > 512:
                            r = e.matmul(ps_r2[0:1, 0:nq - 512], ones_col[:],
                                         exps[:, sl + 512:sl + nq],
                                         start=st0, stop=(ka == 15))
                            r = e.matmul(ps_o[:, 512:nq], lhsT,
                                         exps[:, sl + 512:sl + nq],
                                         start=st0, stop=(ka == 15))
                        r.then_inc(mm.h, 1)
                    else:
                        e.wait_ge(ac.h, plan[("svexp", h - 1)])
                        e.matmul(ps_s1[:, 0:512], onesf[:], rrow[:, 0:512],
                                 start=True, stop=True)
                        e.matmul(ps_s1[:, 512:1024], onesf[:],
                                 rrow[:, 512:1024],
                                 start=True, stop=True).then_inc(mm.h, 1)
            for op, ka in pe_ops_of_head(h):
                v = mm.inc()
                key = ("hrb", h - 1) if op == "hrbP" else (op, h, ka)
                assert v == plan[key], (h, op, ka)
            blk.tensor(pe_head)

            def act_head(e, h=h, dc=DV_C):
                if h == 0:
                    e.wait_ge(dv.h, dc)
                for ka in range(32):
                    e.wait_ge(mm.h, plan[("smm", h, ka)])
                    nq = nq_of(ka)
                    sl = (ka % 4) * 1024
                    e.activation(exps[:, sl:sl + nq],
                                 pss[ka % 2][:, 0:nq], AF.Exp,
                                 bias=consts[:, 0:1], scale=SC1) \
                        .then_inc(ac.h, 1)
                e.wait_ge(mm.h, plan[("sum", h, 31)])
                e.activation(rvec[:, 0:512], ps_r1[0:1, 0:512], AF.Ln,
                             bias=consts[0:1, 1:2], scale=1.0).then_inc(ac.h, 1)
                e.activation(rvec[:, 512:1024], ps_r2[0:1, 0:512], AF.Ln,
                             bias=consts[0:1, 1:2], scale=1.0).then_inc(ac.h, 1)
                e.wait_ge(ac.h, plan[("svexp", h)] - 1)
                e.activation(rrow[:, 0:1024], rvec[:, 0:1024], AF.Exp,
                             bias=consts[0:1, 1:2], scale=-1.0).then_inc(ac.h, 1)
                e.copy(AR1[:, h * 1024:(h + 1) * 1024], ps_o[:]) \
                    .then_inc(ac.h, 1)
            for ka in range(32):
                v = ac.inc()
                assert v == plan[("exp", h, ka)], (h, ka)
            assert ac.inc(4) == plan[("otev", h)], h
            blk.scalar(act_head)

            def dve_head(e, h=h):
                for ka in range(32):
                    e.wait_ge(ac.h, plan[("exp", h, ka)])
                    nq = nq_of(ka)
                    sl = (ka % 4) * 1024
                    e.tensor_tensor(exps[:, sl + nq - 128:sl + nq],
                                    exps[:, sl + nq - 128:sl + nq],
                                    maskb[:, ka * 128:(ka + 1) * 128],
                                    ALU.mult).then_inc(dv.h, 1)
                    if ka == 0 and h > 0:
                        e.wait_ge(mm.h, plan[("hrb", h - 1)])
                        e.wait_ge(ac.h, plan[("otev", h - 1)])
                        e.tensor_tensor(AR1[:, (h - 1) * 1024:h * 1024],
                                        AR1[:, (h - 1) * 1024:h * 1024],
                                        ps_s1[:], ALU.mult).then_inc(dv.h, 1)
            for ka in range(32):
                v = dv.inc()
                assert v == plan[("mask", h, ka)], (h, ka)
                if ka == 0 and h > 0:
                    assert dv.inc() == plan[("omult", h - 1)], h
            blk.vector(dve_head)

        def pe_fin(e, hl=NH - 1):
            e.wait_ge(ac.h, plan[("svexp", hl)])
            e.matmul(ps_s1[:, 0:512], onesf[:], rrow[:, 0:512],
                     start=True, stop=True)
            e.matmul(ps_s1[:, 512:1024], onesf[:], rrow[:, 512:1024],
                     start=True, stop=True).then_inc(mm.h, 1)
        assert mm.inc() == plan[("hrb", NH - 1)]
        blk.tensor(pe_fin)

        def dve_fin(e, hl=NH - 1):
            e.wait_ge(mm.h, plan[("hrb", hl)])
            e.wait_ge(ac.h, plan[("otev", hl)])
            e.tensor_tensor(AR1[:, hl * 1024:(hl + 1) * 1024],
                            AR1[:, hl * 1024:(hl + 1) * 1024],
                            ps_s1[:], ALU.mult).then_inc(dv.h, 1)
        assert dv.inc() == plan[("omult", NH - 1)]
        blk.vector(dve_fin)
        ATT_DV = dv.v
        ATT_MM = mm.v
        last_exp_slot = [plan[("exp", NH - 1, 30)], plan[("exp", NH - 1, 31)]]

        psw = [ps_o, ps_s0]
        wo_win = []
        wo_mm = []
        gate_dv = []
        out_st = []
        for og in range(8):
            def wwin(e, og=og, att=ATT_MM,
                     lastmm=(wo_mm[8 * (og - 2) + 7] if og >= 2 else 0),
                     _dpgw=gw.v):
                _cgw = _dpgw
                e.wait_ge(mm.h, max(att, lastmm))
                e.wait_ge(gw.h, _cgw)
                e.dma_start(out=AR0[:, (og % 2) * 16384:(og % 2 + 1) * 16384]
                            .rearrange("p (dc f) -> p dc f", f=512),
                            in_=wo1_in[og, :, :, :].rearrange("dc p f -> p dc f")) \
                    .then_inc(gw.h, 16)
                _cgw += 16
            blk.gpsimd(wwin)
            wo_win.append(gw.inc(16))

            for tok in range(8):
                sl = ((og * 8 + tok) % 4) * 512

                def gld(e, og=og, tok=tok, sl=sl,
                        gp=(gate_dv[-4] if len(gate_dv) >= 4 else 0), _dpld=ld.v):
                    _cld = _dpld
                    if gp:
                        e.wait_ge(dv.h, gp)
                    e.wait_ge(ld.h, _cld)
                    e.dma_start(out=x1t[:, sl:sl + 512],
                                in_=x1_in[tok * 128:(tok + 1) * 128,
                                          og * 512:(og + 1) * 512]).then_inc(ld.h, 16)
                    _cld += 16
                    e.wait_ge(ld.h, _cld)
                    e.dma_start(out=g1t[:, sl:sl + 512],
                                in_=g1_in[tok * 128:(tok + 1) * 128,
                                          og * 512:(og + 1) * 512]).then_inc(ld.h, 16)
                    _cld += 16
                blk.sync(gld)
                gv = ld.inc(32)

                def wmm(e, og=og, tok=tok, t=ATT_DV, wv=wo_win[og],
                        ae=(last_exp_slot[0] if tok % 2 == 1 else 0),
                        gp=(gate_dv[-2] if len(gate_dv) >= 2 else 0)):
                    e.wait_ge(dv.h, t)
                    e.wait_ge(gw.h, wv)
                    if ae:
                        e.wait_ge(ac.h, ae)
                    if gp:
                        e.wait_ge(dv.h, gp)
                    r = None
                    for dc in range(32):
                        lhsT = AR1[:, dc * 1024 + tok * 128:dc * 1024 + (tok + 1) * 128]
                        rhs = AR0[:, (og % 2) * 16384 + dc * 512:
                                  (og % 2) * 16384 + (dc + 1) * 512]
                        r = e.matmul(psw[tok % 2][:, 0:512], lhsT, rhs,
                                     start=(dc == 0), stop=(dc == 31))
                    r.then_inc(mm.h, 1)
                wo_mm.append(mm.inc())
                blk.tensor(wmm)

                def gate(e, og=og, tok=tok, sl=sl, t=wo_mm[-1], gv=gv, base=dv.v,
                         sp=(out_st[-4] if len(out_st) >= 4 else 0)):
                    e.wait_ge(mm.h, t)
                    e.wait_ge(ld.h, gv)
                    if sp:
                        e.wait_ge(st.h, sp)
                    e.tensor_tensor(ot1[:, sl:sl + 512], psw[tok % 2][:, 0:512],
                                    g1t[:, sl:sl + 512], ALU.mult).then_inc(dv.h, 1)
                    e.wait_ge(dv.h, base + 1)
                    e.tensor_tensor(ot1[:, sl:sl + 512], ot1[:, sl:sl + 512],
                                    x1t[:, sl:sl + 512], ALU.add).then_inc(dv.h, 1)
                gate_dv.append(dv.inc(2))
                blk.vector(gate)

                def ost(e, og=og, tok=tok, sl=sl, t=gate_dv[-1], _dpst=st.v):
                    _cst = _dpst
                    e.wait_ge(dv.h, t)
                    e.wait_ge(st.h, _cst)
                    e.dma_start(out=out_out[tok * 128:(tok + 1) * 128,
                                            og * 512:(og + 1) * 512],
                                in_=ot1[:, sl:sl + 512]).then_inc(st.h, 16)
                    _cst += 16
                blk.sync(ost)
                out_st.append(st.inc(16))

        def fin(e, t=st.v):
            e.wait_ge(st.h, t)
        blk.gpsimd(fin)

    return nc


def _prep_launch3(inputs, r2):
    perm = batch_perm()
    Wo1T = np.asarray(inputs["L1_attn_Wo"], np.float32).T
    wo1t = np.ascontiguousarray(
        Wo1T.reshape(32, 128, 8, 512).transpose(2, 0, 1, 3))
    k1b_b, v1_b = [], []
    for b in range(B):
        K1_loc = np.concatenate(
            [np.asarray(r2[4 * b + cc]["k1T"]).reshape(1024, NTOK)
             for cc in range(4)], axis=1)
        K1_full = np.ascontiguousarray(K1_loc[:, perm])
        k1b_b.append(np.ascontiguousarray(K1_full.reshape(8, 128, S)))
        V1_loc = np.concatenate(
            [np.asarray(r2[4 * b + cc]["v1T"]).reshape(1024, NTOK)
             for cc in range(4)], axis=1)
        V1_full = np.ascontiguousarray(V1_loc[:, perm].T)
        v1_b.append(np.ascontiguousarray(
            V1_full.reshape(32, 128, 8, 128).transpose(1, 2, 0, 3).reshape(128, 32768)))
    in_maps = []
    for c in range(8):
        b, cc = divmod(c, 4)
        masks = causal_masks(cc)
        maskb = np.ascontiguousarray(masks.transpose(1, 0, 2).reshape(128, 4096))
        in_maps.append({
            "x1": np.asarray(r2[c]["x1"]),
            "g1": np.asarray(r2[c]["g1"]),
            "q1T": np.asarray(r2[c]["q1T"]),
            "k1": k1b_b[b],
            "v1": v1_b[b],
            "wo1": wo1t,
            "mask": maskb,
        })
    return in_maps


def _prep_launch2(inputs, r1):
    perm = batch_perm()
    x = np.asarray(inputs["L0_attn_x"], np.float32).reshape(B * S, H)
    Wo = np.asarray(inputs["L0_attn_Wo"], np.float32)
    wo2 = np.ascontiguousarray(Wo.reshape(8, 128, 4, 1024).transpose(2, 0, 1, 3))
    bo = np.ascontiguousarray(np.asarray(inputs["L0_attn_bo"], np.float32).reshape(1, H))
    bffn = np.ascontiguousarray(np.asarray(inputs["L0_ffn_bias"], np.float32).reshape(1, H))
    Wq = np.asarray(inputs["L1_attn_Wq"], np.float32)
    Wk = np.asarray(inputs["L1_attn_Wk"], np.float32)
    Wv = np.asarray(inputs["L1_attn_Wv"], np.float32)
    W1T = np.concatenate([Wq.T, Wk.T, Wv.T], axis=1)
    w1t = np.ascontiguousarray(
        W1T.reshape(32, 128, 48, 128).transpose(2, 1, 0, 3).reshape(48, 128, H))
    Wr = np.asarray(inputs["L1_ffn_W_router"], np.float32)
    wr2 = np.ascontiguousarray(Wr.T.reshape(32, 128, 8).transpose(1, 0, 2).reshape(128, 256))
    A_stack = np.asarray(inputs["L1_ffn_A_stack"], np.float32)
    AT = A_stack.transpose(2, 0, 1).reshape(H, 128)
    at2 = np.ascontiguousarray(AT.reshape(32, 128, 128).transpose(1, 0, 2).reshape(128, H))
    B_stack = np.asarray(inputs["L1_ffn_B_stack"], np.float32)
    brr = np.ascontiguousarray(B_stack.transpose(0, 2, 1).reshape(128, H))
    WbT = np.asarray(inputs["L1_ffn_W_base"], np.float32).T
    wb2 = np.ascontiguousarray(
        WbT.reshape(32, 128, 8, 512).transpose(2, 1, 0, 3).reshape(8, 128, 16384))
    bb = np.ascontiguousarray(np.asarray(inputs["L1_ffn_b_base"], np.float32).reshape(1, H))
    sel = np.zeros((8, 128), np.float32)
    for er in range(128):
        sel[er // 16, er] = 2.0
    ident = np.eye(128, dtype=BF)

    kt2b, v2b = [], []
    for b in range(B):
        KT_loc = np.concatenate(
            [np.asarray(r1[4 * b + cc]["qkvT"])[8:16].reshape(1024, NTOK)
             for cc in range(4)], axis=1)
        KT_full = KT_loc[:, perm]
        kt2b.append(np.ascontiguousarray(
            KT_full.reshape(8, 128, 32, 128).transpose(2, 1, 0, 3).reshape(32, 128, 1024)))
        VT_loc = np.concatenate(
            [np.asarray(r1[4 * b + cc]["qkvT"])[16:24].reshape(1024, NTOK)
             for cc in range(4)], axis=1)
        V_full = np.ascontiguousarray(VT_loc[:, perm].T)
        v2b.append(np.ascontiguousarray(
            V_full.reshape(32, 128, 8, 128).transpose(1, 0, 2, 3).reshape(128, 32768)))

    wl_full = np.concatenate(
        [np.asarray(r1[c]["wlT"]).reshape(H, 512) for c in range(8)], axis=1)
    wl2 = np.ascontiguousarray(
        wl_full.reshape(32, 128, 8, 512).transpose(2, 1, 0, 3).reshape(8, 128, 16384))

    in_maps = []
    for c in range(8):
        b = c // 4
        rows = core_rows(c)
        cosT, sinT = rope_tables(rows % S)
        in_maps.append({
            "x": np.ascontiguousarray(x[rows]),
            "xnT": np.asarray(r1[c]["xnT"]),
            "qT": np.ascontiguousarray(np.asarray(r1[c]["qkvT"])[0:8]),
            "kt": kt2b[b],
            "v": v2b[b],
            "wl": wl2,
            "wo": wo2,
            "bo": bo,
            "bffn": bffn,
            "w1": w1t,
            "cosT": cosT,
            "sinT": sinT,
            "wrT": wr2,
            "at2": at2,
            "brr": brr,
            "wb": wb2,
            "bb": bb,
            "sel": sel,
            "ident": ident,
        })
    return in_maps

def kernel(**inputs):
    if "nc1" not in _CACHE:
        _CACHE["nc1"] = build_launch1()
        _CACHE["nc2"] = build_launch2()
        _CACHE["nc3"] = build_launch3()
    im1 = _prep_launch1(inputs)
    r1 = run_bass_kernel_spmd(_CACHE["nc1"], im1, core_ids=list(range(8))).results
    im2 = _prep_launch2(inputs, r1)
    r2 = run_bass_kernel_spmd(_CACHE["nc2"], im2, core_ids=list(range(8))).results
    im3 = _prep_launch3(inputs, r2)
    r3 = run_bass_kernel_spmd(_CACHE["nc3"], im3, core_ids=list(range(8))).results
    out = np.empty((B * S, H), np.float32)
    for c in range(8):
        out[core_rows(c)] = np.asarray(r3[c]["out"], np.float32)
    return out.reshape(B, S, H)


# version 14
# speedup vs baseline: 1.0098x; 1.0001x over previous
import numpy as np
import ml_dtypes

import concourse.bass as bass
import concourse.mybir as mybir
from concourse.bass_utils import run_bass_kernel_spmd

F32 = mybir.dt.float32
BF16 = mybir.dt.bfloat16
I32 = mybir.dt.int32
AF = mybir.ActivationFunctionType
ALU = mybir.AluOpType

BF = ml_dtypes.bfloat16

H = 4096
S = 4096
B = 2
NTOK = 1024
NT = 8
NH, NKV, HD = 32, 8, 128
SHIFT = 40.0
PADEXT = [32, 28, 24, 20, 16, 12, 8, 4]



def core_tiles(cc):
    tiles = []
    for p in [cc, cc + 4, cc + 8, cc + 12]:
        tiles += [p, 31 - p]
    return sorted(tiles, reverse=True)


def core_rows(c):
    b, cc = divmod(c, 4)
    rows = []
    for t in core_tiles(cc):
        rows.extend(range(b * S + t * 128, b * S + t * 128 + 128))
    return np.asarray(rows, dtype=np.int64)


def batch_perm():
    pos = np.empty(S, np.int64)
    for cc in range(4):
        tiles = core_tiles(cc)
        for j, t in enumerate(tiles):
            pos[t * 128:(t + 1) * 128] = cc * NTOK + j * 128 + np.arange(128)
    return pos


def mobius_matrix():
    M = np.zeros((16, 16), np.float32)
    for s in range(16):
        for t in range(16):
            if (t & s) == t:
                M[t, s] = (-1.0) ** (bin(s).count("1") - bin(t).count("1"))
    return M


def rope_tables(rows_in_seq):
    inv = 1.0 / (1e4 ** (np.arange(0, HD, 2) / HD))
    freqs = rows_in_seq[:, None].astype(np.float64) * inv[None, :]
    emb = np.concatenate([freqs, freqs], -1)
    return (np.ascontiguousarray(np.cos(emb).astype(np.float32).T),
            np.ascontiguousarray(np.sin(emb).astype(np.float32).T))


def causal_masks(cc):
    tiles = core_tiles(cc)
    out = np.zeros((32, 128, 128), np.float32)
    for k in range(32):
        g = tiles[7 - k // 4]
        if k < g:
            out[k] = 1.0
        elif k == g:
            kpos = k * 128 + np.arange(128)
            gq = g * 128 + np.arange(128)
            out[k] = (kpos[:, None] <= gq[None, :]).astype(np.float32)
    return np.ascontiguousarray(out.astype(BF))


def tile_wT(wT, part=128, free=1024):
    C, Fd = wT.shape
    t = wT.reshape(C // part, part, Fd // free, free)
    return np.ascontiguousarray(t.transpose(2, 0, 1, 3))



class Sem:
    __slots__ = ("h", "v")

    def __init__(self, h):
        self.h = h
        self.v = 0

    def inc(self, n=1):
        self.v += n
        return self.v


def mm_split(t, out_ap, lhsT_ap, rhs_ap, n, start, stop, nmax=512):
    o = 0
    r = None
    while o < n:
        w = min(nmax, n - o)
        r = t.matmul(out_ap[:, o:o + w], lhsT_ap, rhs_ap[:, o:o + w],
                     start=start, stop=stop)
        o += w
    return r



def build_launch1():
    nc = bass.Bass("TRN2", target_bir_lowering=False)
    x_in = nc.dram_tensor("x", [NTOK, H], F32, kind="ExternalInput").ap()
    wqkv_in = nc.dram_tensor("wqkv_t", [24, 128, H], F32, kind="ExternalInput").ap()
    bqkv_in = nc.dram_tensor("bqkv", [1, 3072], F32, kind="ExternalInput").ap()
    qw_in = nc.dram_tensor("qw", [4, 4, 128, 128], I32, kind="ExternalInput").ap()
    lut4T_in = nc.dram_tensor("lut4T", [16, 512], F32, kind="ExternalInput").ap()
    mob_in = nc.dram_tensor("mob", [16, 16], F32, kind="ExternalInput").ap()
    ident_in = nc.dram_tensor("ident", [128, 128], BF16, kind="ExternalInput").ap()
    identf_in = nc.dram_tensor("identf", [128, 128], F32, kind="ExternalInput").ap()

    xnT_out = nc.dram_tensor("xnT", [32, 128, NTOK], BF16, kind="ExternalOutput").ap()
    qkvT_out = nc.dram_tensor("qkvT", [24, 128, NTOK], BF16, kind="ExternalOutput").ap()
    wlT_out = nc.dram_tensor("wlT", [32, 128, 512], BF16, kind="ExternalOutput").ap()

    from contextlib import ExitStack
    with ExitStack() as ctx:
        def sb(name, shape, dt):
            return ctx.enter_context(nc.sbuf_tensor(name, shape, dt))

        def psum(name, shape, dt=F32):
            return ctx.enter_context(nc.psum_tensor(name, shape, dt))

        def sem(name):
            return ctx.enter_context(nc.semaphore(name))

        xt = sb("xt", [128, 2 * H], F32)
        bn = sb("bn", [128, 52], F32)
        rstd = sb("rstd", [128, 1], F32)
        xnb = sb("xnb", [128, H], BF16)
        xnT = sb("xnTs", [128, 32 * NTOK], BF16)
        ident = sb("idents", [128, 128], BF16)
        identf = sb("identfs", [128, 128], F32)
        ones = sb("ones", [1, NTOK], BF16)
        consts = sb("consts", [128, 4], F32)
        wbuf = sb("wbuf", [128, 2 * 4096], BF16)
        bqb = sb("bqb", [1, 3072], BF16)
        qkvTs = sb("qkvTs", [128, 2 * NTOK], BF16)
        qws = sb("qws", [128, 16 * 128], I32)
        ibits = sb("ibits", [128, 2048], I32)
        bits = sb("bits", [128, 4 * 2048], BF16)
        lut4T = sb("lut4Ts", [16, 512], F32)
        mob = sb("mobs", [16, 16], F32)
        coef = sb("coef", [128, 64], F32)
        p01 = sb("p01", [128, 2048], F32)
        dq1 = sb("dq1", [128, 2048], F32)
        dq2 = sb("dq2", [128, 2048], F32)
        wacc = sb("wacc", [128, 2048], F32)
        wstage = sb("wstage", [128, 2048], BF16)
        ps_q0 = psum("ps_q0", [128, 1024], F32)
        ps_q1 = psum("ps_q1", [128, 1024], F32)
        ps_t0 = psum("ps_t0", [128, 1024], BF16)
        ps_t1 = psum("ps_t1", [128, 1024], BF16)
        ps_f0 = psum("ps_f0", [128, 512], F32)
        ps_f1 = psum("ps_f1", [128, 512], F32)
        h_ld = sem("ld")
        h_gw = sem("gw")
        h_mm = sem("mm")
        h_dv = sem("dv")
        h_ev = sem("ev")
        h_ac = sem("ac")
        h_st = sem("st")
        blk = ctx.enter_context(nc.Block())
        ld, gw, mm, dv, ev, ac, st = (Sem(h_ld), Sem(h_gw), Sem(h_mm), Sem(h_dv),
                                      Sem(h_ev), Sem(h_ac), Sem(h_st))
        pst = [ps_t0, ps_t1]
        pst_free = [0, 0]

        def c0(e, _dpld=ld.v):
            _cld = _dpld
            e.wait_ge(ld.h, _cld)
            e.dma_start(out=ident[:], in_=ident_in[:]).then_inc(ld.h, 16)
            _cld += 16
            e.wait_ge(ld.h, _cld)
            e.dma_start(out=identf[:], in_=identf_in[:]).then_inc(ld.h, 16)
            _cld += 16
            e.wait_ge(ld.h, _cld)
            e.dma_start(out=lut4T[:], in_=lut4T_in[:]).then_inc(ld.h, 16)
            _cld += 16
            e.wait_ge(ld.h, _cld)
            e.dma_start(out=mob[:], in_=mob_in[:]).then_inc(ld.h, 16)
            _cld += 16
        blk.sync(c0)
        LD_CONST = ld.inc(64)

        def v_ones(e):
            e.memset(ones[:], 1.0)
            e.memset(consts[:, 0:1], 1e-5)
            e.memset(consts[:, 1:2], 0.0).then_inc(dv.h, 1)
        blk.vector(v_ones)
        ONES_DV = dv.inc()

        def g0(e, _dpgw=gw.v):
            _cgw = _dpgw
            e.wait_ge(gw.h, _cgw)
            e.dma_start(out=bqb[:], in_=bqkv_in[:]).then_inc(gw.h, 16)
            _cgw += 16
            for ot in range(4):
                for pl in range(4):
                    e.wait_ge(gw.h, _cgw)
                    e.dma_start(out=qws[:, (ot * 4 + pl) * 128:(ot * 4 + pl + 1) * 128],
                                in_=qw_in[pl, ot, :, :]).then_inc(gw.h, 16)
                    _cgw += 16
        blk.gpsimd(g0)
        gw.inc(16 + 256)
        GW_BQ = GW_QW = gw.v

        ps_c_free = 0
        COEF_EV = 0
        for ot in range(4):
            def mobmm(e, ot=ot, fr=ps_c_free, need=LD_CONST):
                e.wait_ge(ld.h, need)
                if fr:
                    e.wait_ge(ev.h, fr)
                e.matmul(ps_f0[:, 0:16], lut4T[:, ot * 128:(ot + 1) * 128], mob[:],
                         start=True, stop=True).then_inc(mm.h, 1)
            t_mm = mm.inc()
            blk.tensor(mobmm)

            def mobev(e, ot=ot, t=t_mm):
                e.wait_ge(mm.h, t)
                e.tensor_copy(coef[:, ot * 16:(ot + 1) * 16], ps_f0[:, 0:16]) \
                    .then_inc(ev.h, 1)
            ps_c_free = COEF_EV = ev.inc()
            blk.vector(mobev)

        def ln_ld0(e, _dpld=ld.v):
            _cld = _dpld
            e.wait_ge(ld.h, _cld)
            e.dma_start(out=xt[:, 0:H], in_=x_in[0:128, :]).then_inc(ld.h, 16)
            _cld += 16
            e.wait_ge(ld.h, _cld)
            e.dma_start(out=xt[:, H:2 * H], in_=x_in[128:256, :]).then_inc(ld.h, 16)
            _cld += 16
        blk.sync(ln_ld0)
        x_loaded = [ld.inc(16), ld.inc(16)]
        xnb_free_mm = 0

        for i in range(NT):
            sl = (i % 2) * H

            def ln_stats(e, sl=sl, need=x_loaded[i], _sp=dv.v):
                e.wait_ge(ld.h, need)
                e.wait_ge(dv.h, _sp)
                xs = xt[:, sl:sl + H]
                r = None
                for k in range(8):
                    r = e.bn_stats(bn[:, 6 * k:6 * k + 6], xs[:, 512 * k:512 * (k + 1)])
                r.then_inc(dv.h, 1)
            t1 = dv.inc()
            blk.vector(ln_stats)

            def ln_aggr(e, t=t1):
                e.wait_ge(dv.h, t)
                e.bn_aggr(bn[:, 48:50], bn[:, 0:48]).then_inc(dv.h, 1)
            t2 = dv.inc()
            blk.vector(ln_aggr)

            def ln_rstd(e, t=t2, a0=ac.v):
                e.wait_ge(dv.h, t)
                e.activation(bn[:, 50:51], bn[:, 49:50], AF.Ln, bias=consts[:, 0:1], scale=1.0) \
                    .then_inc(ac.h, 1)
                e.wait_ge(ac.h, a0 + 1)
                e.activation(rstd[:], bn[:, 50:51], AF.Exp, bias=consts[:, 1:2], scale=-0.5) \
                    .then_inc(ac.h, 1)
            t3 = ac.inc(2)
            blk.scalar(ln_rstd)

            def ln_norm(e, sl=sl, t=t3, xf=xnb_free_mm):
                e.wait_ge(ac.h, t)
                if xf:
                    e.wait_ge(mm.h, xf)
                xs = xt[:, sl:sl + H]
                e.tensor_scalar(xnb[:], xs, bn[:, 48:49], rstd[:],
                                ALU.subtract, ALU.mult).then_inc(dv.h, 1)
            t4 = dv.inc()
            blk.vector(ln_norm)

            if i + 2 < NT:
                def ld_next(e, i=i, t=t4, _dpld=ld.v):
                    _cld = _dpld
                    e.wait_ge(dv.h, t)
                    e.wait_ge(ld.h, _cld)
                    e.dma_start(out=xt[:, (i % 2) * H:(i % 2) * H + H],
                                in_=x_in[(i + 2) * 128:(i + 3) * 128, :]) \
                        .then_inc(ld.h, 16)
                    _cld += 16
                blk.sync(ld_next)
                x_loaded.append(ld.inc(16))

            for cb in range(32):
                b = cb % 2

                def tns(e, cb=cb, b=b, fr=pst_free[b], t=t4):
                    e.wait_ge(dv.h, t)
                    if fr:
                        e.wait_ge(ev.h, fr)
                    e.transpose(pst[b][:, 0:128], xnb[:, cb * 128:(cb + 1) * 128], ident[:]) \
                        .then_inc(mm.h, 1)
                rd = mm.inc()
                blk.tensor(tns)

                def dve(e, rd=rd, b=b, i=i, cb=cb):
                    e.wait_ge(mm.h, rd)
                    e.tensor_copy(xnT[:, cb * NTOK + i * 128:cb * NTOK + (i + 1) * 128],
                                  pst[b][:, 0:128]).then_inc(ev.h, 1)
                pst_free[b] = ev.inc()
                blk.vector(dve)
            xnb_free_mm = mm.v
        XNT_EV = ev.v

        def xnT_store(e, t=XNT_EV, _dpst=st.v):
            _cst = _dpst
            e.wait_ge(ev.h, t)
            for cb in range(32):
                e.wait_ge(st.h, _cst)
                e.dma_start(out=xnT_out[cb, :, :],
                            in_=xnT[:, cb * NTOK:(cb + 1) * NTOK]).then_inc(st.h, 16)
                _cst += 16
        blk.sync(xnT_store)
        st.inc(16 * 32)

        psf = [ps_f0, ps_f1]
        psf_free = [COEF_EV, 0]
        wstage_free_st = 0
        qkv_mm_done = []
        qkv_ac_done = []
        qkv_st_done = []
        deq_slot = {8 + 2 * k: k for k in range(8)}

        def emit_qkv(ft):
            def wload(e, ft=ft, prev=(qkv_mm_done[ft - 2] if ft >= 2 else 0), _dpgw=gw.v):
                _cgw = _dpgw
                if prev:
                    e.wait_ge(mm.h, prev)
                e.wait_ge(gw.h, _cgw)
                e.dma_start(out=wbuf[:, (ft % 2) * 4096:(ft % 2 + 1) * 4096],
                            in_=wqkv_in[ft, :, :]).then_inc(gw.h, 16)
                _cgw += 16
            blk.gpsimd(wload)
            wv = gw.inc(16)
            pso = [ps_q0, ps_q1][ft % 2]

            def qmm(e, ft=ft, wv=wv, pso=pso, xv=XNT_EV, dvo=ONES_DV, acb=GW_BQ,
                    evprev=(qkv_ac_done[ft - 2] if ft >= 2 else 0)):
                e.wait_ge(gw.h, wv)
                e.wait_ge(ev.h, xv)
                e.wait_ge(dv.h, dvo)
                e.wait_ge(gw.h, acb)
                if evprev:
                    e.wait_ge(ac.h, evprev)
                e.matmul(pso[:, 0:512], bqb[:, ft * 128:(ft + 1) * 128],
                         ones[:, 0:512], start=True, stop=False)
                e.matmul(pso[:, 512:1024], bqb[:, ft * 128:(ft + 1) * 128],
                         ones[:, 512:1024], start=True, stop=False)
                for c in range(32):
                    lhsT = wbuf[:, (ft % 2) * 4096 + c * 128:(ft % 2) * 4096 + (c + 1) * 128]
                    rhs = xnT[:, c * NTOK:(c + 1) * NTOK]
                    r = mm_split(e, pso, lhsT, rhs, NTOK, start=False, stop=(c == 31))
                r.then_inc(mm.h, 1)
            qkv_mm_done.append(mm.inc())
            blk.tensor(qmm)

            def qev(e, ft=ft, pso=pso, t=qkv_mm_done[ft],
                    stp=(qkv_st_done[ft - 2] if ft >= 2 else 0)):
                e.wait_ge(mm.h, t)
                if stp:
                    e.wait_ge(st.h, stp)
                e.copy(qkvTs[:, (ft % 2) * NTOK:(ft % 2 + 1) * NTOK], pso[:]) \
                    .then_inc(ac.h, 1)
            qkv_ac_done.append(ac.inc())
            blk.scalar(qev)

            def qst(e, ft=ft, t=qkv_ac_done[ft], _dpst=st.v):
                _cst = _dpst
                e.wait_ge(ac.h, t)
                e.wait_ge(st.h, _cst)
                e.dma_start(out=qkvT_out[ft, :, :],
                            in_=qkvTs[:, (ft % 2) * NTOK:(ft % 2 + 1) * NTOK]) \
                    .then_inc(st.h, 16)
                _cst += 16
            blk.sync(qst)
            qkv_st_done.append(st.inc(16))

        def emit_deq_block(k):
            nonlocal wstage_free_st
            ot, ih = k // 2, k % 2

            if True:
                def extract(e, ot=ot, ih=ih, need=GW_QW, ce=COEF_EV, base=dv.v):
                    e.wait_ge(gw.h, need)
                    e.wait_ge(ev.h, ce)
                    t = base
                    e.wait_ge(dv.h, t)
                    for pl in range(4):
                        qv = qws[:, (ot * 4 + pl) * 128 + ih * 64:
                                 (ot * 4 + pl) * 128 + ih * 64 + 64]
                        r = None
                        for j in range(32):
                            r = e.tensor_scalar(
                                ibits.ap().rearrange("p (w j) -> p w j", j=32)[:, :, j],
                                qv, j, 1, ALU.logical_shift_right, ALU.bitwise_and)
                        r.then_inc(dv.h, 1)
                        t += 1
                        e.wait_ge(dv.h, t)
                        e.tensor_copy(bits[:, pl * 2048:(pl + 1) * 2048], ibits[:]) \
                            .then_inc(dv.h, 1)
                        t += 1
                        e.wait_ge(dv.h, t)
                dv.inc(8)
                blk.vector(extract)

                def multilin(e, ot=ot, base=dv.v):
                    t = base
                    e.wait_ge(dv.h, t)
                    bb = [bits[:, pl * 2048:(pl + 1) * 2048] for pl in range(4)]

                    def step(r):
                        nonlocal t
                        t += 1
                        r.then_inc(dv.h, 1)
                        e.wait_ge(dv.h, t)

                    def a(s):
                        return coef[:, ot * 16 + s:ot * 16 + s + 1]

                    def emit_G(out, T):
                        step(e.tensor_scalar(out, bb[0], a(T + 1), a(T),
                                             ALU.mult, ALU.add))
                        step(e.scalar_tensor_tensor(out, bb[1], a(T + 2), out,
                                                    ALU.mult, ALU.add))
                        step(e.scalar_tensor_tensor(out, p01[:], a(T + 3), out,
                                                    ALU.mult, ALU.add))

                    step(e.tensor_tensor(p01[:], bb[0], bb[1], ALU.mult))
                    emit_G(dq2[:], 12)
                    step(e.tensor_tensor(dq2[:], bb[2], dq2[:], ALU.mult))
                    emit_G(dq1[:], 8)
                    step(e.tensor_tensor(dq1[:], dq1[:], dq2[:], ALU.add))
                    step(e.tensor_tensor(dq1[:], bb[3], dq1[:], ALU.mult))
                    emit_G(dq2[:], 4)
                    step(e.tensor_tensor(dq2[:], bb[2], dq2[:], ALU.mult))
                    emit_G(wacc[:], 0)
                    step(e.tensor_tensor(wacc[:], wacc[:], dq2[:], ALU.add))
                    step(e.tensor_tensor(wacc[:], wacc[:], dq1[:], ALU.add))
                n_steps = 19
                blk.vector(multilin)
                ML_DONE = dv.inc(n_steps)

                for cb in range(16):
                    b = cb % 2

                    def tns(e, cb=cb, b=b, fr=psf_free[b], t=ML_DONE, ws=wstage_free_st):
                        e.wait_ge(dv.h, t)
                        if ws:
                            e.wait_ge(st.h, ws)
                        if fr:
                            e.wait_ge(ev.h, fr)
                        e.transpose(psf[b][:, 0:128], wacc[:, cb * 128:(cb + 1) * 128], identf[:]) \
                            .then_inc(mm.h, 1)
                    rd = mm.inc()
                    blk.tensor(tns)

                    def dve(e, rd=rd, b=b, cb=cb):
                        e.wait_ge(mm.h, rd)
                        e.tensor_copy(wstage[:, cb * 128:(cb + 1) * 128], psf[b][:, 0:128]) \
                            .then_inc(ev.h, 1)
                    psf_free[b] = ev.inc()
                    blk.vector(dve)
                WS_EV = ev.v

                def wl_store(e, ot=ot, ih=ih, t=WS_EV, _dpst=st.v):
                    _cst = _dpst
                    e.wait_ge(ev.h, t)
                    for cb in range(16):
                        ic = ih * 16 + cb
                        e.wait_ge(st.h, _cst)
                        e.dma_start(out=wlT_out[ic, :, ot * 128:(ot + 1) * 128],
                                    in_=wstage[:, cb * 128:(cb + 1) * 128]) \
                            .then_inc(st.h, 16)
                        _cst += 16
                blk.sync(wl_store)
                wstage_free_st = st.inc(16 * 16)

        for ft in range(24):
            emit_qkv(ft)
            if ft in deq_slot:
                emit_deq_block(deq_slot[ft])

        def fin(e, t=st.v):
            e.wait_ge(st.h, t)
        blk.gpsimd(fin)

    return nc




def build_launch2():
    nc = bass.Bass("TRN2", target_bir_lowering=False)
    x_in = nc.dram_tensor("x", [NTOK, H], F32, kind="ExternalInput").ap()
    xnT_in = nc.dram_tensor("xnT", [32, 128, NTOK], BF16, kind="ExternalInput").ap()
    qT_in = nc.dram_tensor("qT", [8, 128, NTOK], BF16, kind="ExternalInput").ap()
    kt_in = nc.dram_tensor("kt", [32, 128, 1024], BF16, kind="ExternalInput").ap()
    v_in = nc.dram_tensor("v", [128, 32768], BF16, kind="ExternalInput").ap()
    wl_in = nc.dram_tensor("wl", [8, 128, 16384], BF16, kind="ExternalInput").ap()
    wo_in = nc.dram_tensor("wo", [4, 8, 128, 1024], F32, kind="ExternalInput").ap()
    bo_in = nc.dram_tensor("bo", [1, H], F32, kind="ExternalInput").ap()
    bffn_in = nc.dram_tensor("bffn", [1, H], F32, kind="ExternalInput").ap()
    w1_in = nc.dram_tensor("w1", [48, 128, H], F32, kind="ExternalInput").ap()
    cos_in = nc.dram_tensor("cosT", [128, NTOK], F32, kind="ExternalInput").ap()
    sin_in = nc.dram_tensor("sinT", [128, NTOK], F32, kind="ExternalInput").ap()
    wr_in = nc.dram_tensor("wrT", [128, 256], F32, kind="ExternalInput").ap()
    at_in = nc.dram_tensor("at2", [128, H], F32, kind="ExternalInput").ap()
    brr_in = nc.dram_tensor("brr", [128, H], F32, kind="ExternalInput").ap()
    wb_in = nc.dram_tensor("wb", [8, 128, 16384], F32, kind="ExternalInput").ap()
    bb_in = nc.dram_tensor("bb", [1, H], F32, kind="ExternalInput").ap()
    sel_in = nc.dram_tensor("sel", [8, 128], F32, kind="ExternalInput").ap()
    ident_in = nc.dram_tensor("ident", [128, 128], BF16, kind="ExternalInput").ap()

    x1_out = nc.dram_tensor("x1", [NTOK, H], F32, kind="ExternalOutput").ap()
    g1_out = nc.dram_tensor("g1", [NTOK, H], BF16, kind="ExternalOutput").ap()
    q1T_out = nc.dram_tensor("q1T", [32, 128, NTOK], BF16, kind="ExternalOutput").ap()
    k1T_out = nc.dram_tensor("k1T", [8, 128, NTOK], BF16, kind="ExternalOutput").ap()
    v1T_out = nc.dram_tensor("v1T", [8, 128, NTOK], BF16, kind="ExternalOutput").ap()
    s0_dram = nc.dram_tensor("s0scr", [8, 4, 128, 1024], F32, kind="Internal").ap()

    from contextlib import ExitStack
    with ExitStack() as ctx:
        def sb(name, shape, dt):
            return ctx.enter_context(nc.sbuf_tensor(name, shape, dt))

        def psum(name, shape, dt=F32):
            return ctx.enter_context(nc.psum_tensor(name, shape, dt))

        AR0 = sb("AR0", [128, 32768], BF16)
        AR1 = sb("AR1", [128, 32768], BF16)
        AR2 = sb("AR2", [128, 8192], BF16)
        arena = sb("arena", [128, 8192], F32)
        scr_b = sb("scr_b", [128, 4096], BF16)
        bias = sb("bias", [1, H], BF16)
        rbuf = sb("rbuf", [128, NTOK], F32)
        onesf = sb("onesf", [1, 128], F32)
        ones = sb("ones2", [1, NTOK], BF16)
        ones_col = sb("ones_col", [128, 1], BF16)
        consts = sb("consts2", [128, 4], F32)
        bn = sb("bn2", [128, 104], F32)
        rstd = sb("rstd2", [128, 2], F32)
        wr2 = sb("wr2", [128, 256], BF16)
        sel2 = sb("sel2", [8, 128], BF16)
        rout = sb("rout", [128, 256], F32)
        routb = sb("routb", [128, 64], BF16)
        combT = sb("combT", [8, NTOK], BF16)
        ident = sb("ident2", [128, 128], BF16)
        ps_a = psum("ps2_a", [128, 1024], F32)
        ps_b = psum("ps2_b", [128, 1024], F32)
        ps_c1 = psum("ps2_c1", [128, 512], F32)
        ps_c2 = psum("ps2_c2", [128, 512], F32)
        ps_t0 = psum("ps2_t0", [128, 1024], BF16)
        ps_t1 = psum("ps2_t1", [128, 1024], BF16)
        xtile = arena[:, 0:4096]
        ktb = scr_b[:, 0:2048]
        cosb = rbuf[:]
        sinb = arena[:, 4096:5120]
        rvec = arena[0:1, 7168:8192]
        xnb = scr_b[:, 0:4096]
        qkb = scr_b[:, 0:2048]
        hbuf = scr_b[:, 2048:3072]
        cmat = scr_b[:, 3072:4096]
        at2 = AR2[:, 0:4096]
        brr2 = AR2[:, 4096:8192]
        ld = Sem(ctx.enter_context(nc.semaphore("ld2")))
        gw = Sem(ctx.enter_context(nc.semaphore("gw2")))
        mm = Sem(ctx.enter_context(nc.semaphore("mm2")))
        dv = Sem(ctx.enter_context(nc.semaphore("dv2")))
        ev = Sem(ctx.enter_context(nc.semaphore("ev2")))
        ac = Sem(ctx.enter_context(nc.semaphore("ac2")))
        st = Sem(ctx.enter_context(nc.semaphore("st2")))
        blk = ctx.enter_context(nc.Block())

        def c0(e, _dpld=ld.v):
            _cld = _dpld
            e.wait_ge(ld.h, _cld)
            e.dma_start(out=ident[:], in_=ident_in[:]).then_inc(ld.h, 16)
            _cld += 16
            e.wait_ge(ld.h, _cld)
            e.dma_start(out=AR0[:], in_=v_in[:]).then_inc(ld.h, 16)
            _cld += 16
            e.wait_ge(ld.h, _cld)
            e.dma_start(out=AR2[:].rearrange("p (dc n) -> p dc n", n=NTOK),
                        in_=qT_in[:].rearrange("dc p n -> p dc n")).then_inc(ld.h, 16)
            _cld += 16
        blk.sync(c0)
        LD_A = ld.inc(48)

        def g_c(e, _dpgw=gw.v):
            _cgw = _dpgw
            e.wait_ge(gw.h, _cgw)
            e.dma_start(out=bias[:], in_=bo_in[:]).then_inc(gw.h, 16)
            _cgw += 16
            e.wait_ge(gw.h, _cgw)
            e.dma_start(out=sel2[:], in_=sel_in[:]).then_inc(gw.h, 16)
            _cgw += 16
            e.wait_ge(gw.h, _cgw)
            e.dma_start(out=wr2[:], in_=wr_in[:]).then_inc(gw.h, 16)
            _cgw += 16
        blk.gpsimd(g_c)
        gw.inc(48)
        GW_BO = GW_EMISC = gw.v

        def v_c(e):
            e.memset(ones[:], 1.0)
            e.memset(ones_col[:], 1.0)
            e.memset(onesf[:], 1.0)
            e.memset(consts[:, 0:1], -SHIFT)
            e.memset(consts[:, 1:2], 1e-5)
            e.memset(consts[:, 2:3], 0.0).then_inc(dv.h, 1)
        blk.vector(v_c)
        DV_C = dv.inc()

        MM_A0 = mm.v
        AC_A0 = ac.v

        def a_exps(e, mm0=MM_A0, dvc=DV_C):
            e.wait_ge(dv.h, dvc)
            for kt in range(32):
                e.wait_ge(mm.h, mm0 + 2 * kt + 1)
                pso = [ps_a, ps_b][kt % 2]
                e.activation(AR1[:, kt * 1024:(kt + 1) * 1024], pso[:],
                             AF.Exp, bias=consts[:, 0:1], scale=1.0 / 32.0) \
                    .then_inc(ac.h, 1)
        blk.scalar(a_exps)
        exp_ac = [ac.inc() for _ in range(32)]
        assert exp_ac == [AC_A0 + kt + 1 for kt in range(32)]

        s_mm_done = []
        SUM_MM = 0
        for kt in range(32):
            def k_ld(e, kt=kt, prev=(s_mm_done[kt - 2] if kt >= 2 else 0), _dpld=ld.v):
                _cld = _dpld
                if prev:
                    e.wait_ge(mm.h, prev)
                e.wait_ge(ld.h, _cld)
                e.dma_start(out=ktb[:, (kt % 2) * 1024:(kt % 2 + 1) * 1024],
                            in_=kt_in[kt, :, :]).then_inc(ld.h, 16)
                _cld += 16
            blk.sync(k_ld)
            kv = ld.inc(16)

            pso = [ps_a, ps_b][kt % 2]

            def s_mm(e, kt=kt, kv=kv, need=LD_A, dvc=DV_C, pso=pso,
                     acprev=(exp_ac[kt - 2] if kt >= 2 else 0)):
                e.wait_ge(ld.h, max(kv, need))
                e.wait_ge(dv.h, dvc)
                if acprev:
                    e.wait_ge(ac.h, acprev)
                r = None
                for dc in range(8):
                    lhsT = ktb[:, (kt % 2) * 1024 + dc * 128:(kt % 2) * 1024 + (dc + 1) * 128]
                    rhs = AR2[:, dc * NTOK:(dc + 1) * NTOK]
                    r = mm_split(e, pso, lhsT, rhs, NTOK, start=(dc == 0), stop=(dc == 7))
                r.then_inc(mm.h, 1)
            s_mm_done.append(mm.inc())
            assert s_mm_done[kt] == MM_A0 + 2 * kt + 1
            blk.tensor(s_mm)

            def s_sum(e, kt=kt, t=exp_ac[kt]):
                e.wait_ge(ac.h, t)
                e.matmul(ps_c1[0:1, 0:512], ones_col[:],
                         AR1[:, kt * 1024:kt * 1024 + 512],
                         start=(kt == 0), stop=(kt == 31))
                e.matmul(ps_c2[0:1, 0:512], ones_col[:],
                         AR1[:, kt * 1024 + 512:(kt + 1) * 1024],
                         start=(kt == 0), stop=(kt == 31)).then_inc(mm.h, 1)
            SUM_MM = mm.inc()
            blk.tensor(s_sum)
        AEXP_AC = ac.v

        def recip(e, t=SUM_MM, base=dv.v):
            e.wait_ge(mm.h, t)
            e.wait_ge(dv.h, base)
            e.reciprocal(rvec[:, 0:512], ps_c1[0:1, 0:512]).then_inc(dv.h, 1)
            e.wait_ge(dv.h, base + 1)
            e.reciprocal(rvec[:, 512:1024], ps_c2[0:1, 0:512]).then_inc(dv.h, 1)
        RECIP_DV = dv.inc(2)
        blk.vector(recip)

        def rb_mm(e, t=RECIP_DV):
            e.wait_ge(dv.h, t)
            e.matmul(ps_b[:, 0:512], onesf[:], rvec[:, 0:512], start=True, stop=True)
            e.matmul(ps_b[:, 512:1024], onesf[:], rvec[:, 512:1024],
                     start=True, stop=True).then_inc(mm.h, 1)
        RB_MM = mm.inc()
        blk.tensor(rb_mm)

        def rb_ev(e, t=RB_MM):
            e.wait_ge(mm.h, t)
            e.tensor_copy(rbuf[:], ps_b[:]).then_inc(dv.h, 1)
        RB_DV = dv.inc()
        blk.vector(rb_ev)

        av_ev = []
        for dt in range(8):
            def av(e, dt=dt, t=AEXP_AC, rb=RB_DV,
                   evprev=(av_ev[dt - 1] if dt >= 1 else 0)):
                e.wait_ge(ac.h, t)
                e.wait_ge(dv.h, rb)
                if evprev:
                    e.wait_ge(dv.h, evprev)
                r = None
                for kt in range(32):
                    lhsT = AR0[:, (kt * 8 + dt) * 128:(kt * 8 + dt + 1) * 128]
                    rhs = AR1[:, kt * 1024:(kt + 1) * 1024]
                    r = mm_split(e, ps_b, lhsT, rhs, NTOK,
                                 start=(kt == 0), stop=(kt == 31))
                r.then_inc(mm.h, 1)
            t_av = mm.inc()
            blk.tensor(av)

            def av_e(e, dt=dt, t=t_av, rb=RB_DV):
                e.wait_ge(mm.h, t)
                e.wait_ge(dv.h, rb)
                e.tensor_tensor(AR2[:, dt * 1024:(dt + 1) * 1024], ps_b[:], rbuf[:],
                                ALU.mult).then_inc(dv.h, 1)
            av_ev.append(dv.inc())
            blk.vector(av_e)
        AV_DV = dv.v
        AV_MM = mm.v

        wo_ld = []
        wo_mm_done = []
        wo_ac_done = []
        wo_st_done = []
        for og in range(4):
            def wo_l(e, og=og, avt=AV_MM,
                     prev=(wo_mm_done[8 * (og - 2) + 7] if og >= 2 else 0),
                     _dpgw=gw.v):
                _cgw = _dpgw
                e.wait_ge(mm.h, max(avt, prev))
                for dc in range(8):
                    e.wait_ge(gw.h, _cgw)
                    e.dma_start(out=AR1[:, (og % 2) * 8192 + dc * 1024:
                                        (og % 2) * 8192 + (dc + 1) * 1024],
                                in_=wo_in[og, dc, :, :]).then_inc(gw.h, 16)
                    _cgw += 16
            blk.gpsimd(wo_l)
            wo_ld.append(gw.inc(128))

            for tok in range(8):
                pso = [ps_a, ps_b][tok % 2]

                def wo_mm(e, og=og, tok=tok, pso=pso, wv=wo_ld[og], avt=AV_DV,
                          bo=GW_BO,
                          acp=(wo_ac_done[-2] if len(wo_ac_done) >= 2 else 0)):
                    e.wait_ge(gw.h, max(wv, bo))
                    e.wait_ge(dv.h, avt)
                    if acp:
                        e.wait_ge(ac.h, acp)
                    e.matmul(pso[:, 0:512], ones[0:1, 0:128],
                             bias[0:1, og * 1024:og * 1024 + 512],
                             start=True, stop=False)
                    e.matmul(pso[:, 512:1024], ones[0:1, 0:128],
                             bias[0:1, og * 1024 + 512:(og + 1) * 1024],
                             start=True, stop=False)
                    r = None
                    for dc in range(8):
                        lhsT = AR2[:, dc * 1024 + tok * 128:dc * 1024 + (tok + 1) * 128]
                        rhs = AR1[:, (og % 2) * 8192 + dc * 1024:
                                  (og % 2) * 8192 + (dc + 1) * 1024]
                        r = mm_split(e, pso, lhsT, rhs, NTOK,
                                     start=False, stop=(dc == 7))
                    r.then_inc(mm.h, 1)
                wo_mm_done.append(mm.inc())
                blk.tensor(wo_mm)

                def wo_ev(e, og=og, tok=tok, pso=pso, t=wo_mm_done[-1],
                          stp=(wo_st_done[-2] if len(wo_st_done) >= 2 else 0)):
                    e.wait_ge(mm.h, t)
                    if stp:
                        e.wait_ge(st.h, stp)
                    e.copy(arena[:, (tok % 2) * 1024:(tok % 2 + 1) * 1024], pso[:]) \
                        .then_inc(ac.h, 1)
                wo_ac_done.append(ac.inc())
                blk.scalar(wo_ev)

                def wo_s(e, og=og, tok=tok, t=wo_ac_done[-1], _dpst=st.v):
                    _cst = _dpst
                    e.wait_ge(ac.h, t)
                    e.wait_ge(st.h, _cst)
                    e.dma_start(out=s0_dram[tok, og, :, :],
                                in_=arena[:, (tok % 2) * 1024:(tok % 2 + 1) * 1024]) \
                        .then_inc(st.h, 16)
                    _cst += 16
                blk.sync(wo_s)
                wo_st_done.append(st.inc(16))
        PA_ST = st.v
        PA_MM = mm.v
        PA_AC = ac.v

        def b_ld(e, t=PA_MM, _dpld=ld.v):
            _cld = _dpld
            e.wait_ge(mm.h, t)
            e.wait_ge(ld.h, _cld)
            e.dma_start(out=AR0[:].rearrange("p (cb n) -> p cb n", n=NTOK),
                        in_=xnT_in[:].rearrange("cb p n -> p cb n")).then_inc(ld.h, 16)
            _cld += 16
        blk.sync(b_ld)
        B_XNT = ld.inc(16)

        def b_bias(e, t=PA_AC, _dpgw=gw.v):
            _cgw = _dpgw
            e.wait_ge(ac.h, t)
            e.wait_ge(gw.h, _cgw)
            e.dma_start(out=bias[:], in_=bffn_in[:]).then_inc(gw.h, 16)
            _cgw += 16
        blk.gpsimd(b_bias)
        B_BIAS = gw.inc(16)

        wl_ld = []
        yl_mm_done = []
        yl_ac_done = []
        b_st_done = []
        b_gate_dvs = []
        for og in range(8):
            def wl_l(e, og=og, pa=(AV_MM if og == 0 else PA_MM),
                     prev=(yl_mm_done[8 * (og - 2) + 7] if og >= 2 else 0),
                     _dpgw=gw.v):
                _cgw = _dpgw
                e.wait_ge(mm.h, max(pa, prev))
                e.wait_ge(gw.h, _cgw)
                e.dma_start(out=AR1[:, ((og + 1) % 2) * 16384:
                                    ((og + 1) % 2 + 1) * 16384],
                            in_=wl_in[og, :, :]).then_inc(gw.h, 16)
                _cgw += 16
            blk.gpsimd(wl_l)
            wl_ld.append(gw.inc(16))

            for tok in range(8):
                pso = [ps_a, ps_b][tok % 2]
                aslot = ((og * 8 + tok) % 4) * 512

                def yl_mm(e, og=og, tok=tok, pso=pso, wv=wl_ld[og], xv=B_XNT,
                          bi=B_BIAS, pa=PA_AC,
                          acp=(yl_ac_done[-2] if len(yl_ac_done) >= 2 else 0)):
                    e.wait_ge(ld.h, xv)
                    e.wait_ge(gw.h, max(bi, wv))
                    e.wait_ge(ac.h, max(acp, pa))
                    e.matmul(pso[:, 0:512], ones[0:1, 0:128],
                             bias[0:1, og * 512:(og + 1) * 512],
                             start=True, stop=False)
                    r = None
                    for ic in range(32):
                        lhsT = AR0[:, ic * 1024 + tok * 128:ic * 1024 + (tok + 1) * 128]
                        rhs = AR1[:, ((og + 1) % 2) * 16384 + ic * 512:
                                  ((og + 1) % 2) * 16384 + (ic + 1) * 512]
                        r = e.matmul(pso[:, 0:512], lhsT, rhs,
                                     start=False, stop=(ic == 31))
                    r.then_inc(mm.h, 1)
                yl_mm_done.append(mm.inc())
                blk.tensor(yl_mm)

                def yl_ev(e, og=og, tok=tok, pso=pso, aslot=aslot,
                          t=yl_mm_done[-1], pa=PA_ST,
                          gp=(b_gate_dvs[-4] if len(b_gate_dvs) >= 4 else 0)):
                    e.wait_ge(mm.h, t)
                    e.wait_ge(st.h, pa)
                    if gp:
                        e.wait_ge(dv.h, gp)
                    e.activation(arena[:, aslot:aslot + 512],
                                 pso[:, 0:512], AF.Silu, bias=consts[:, 2:3],
                                 scale=1.0).then_inc(ac.h, 1)
                yl_ac_done.append(ac.inc())
                blk.scalar(yl_ev)

                def b_gld(e, og=og, tok=tok, aslot=aslot, sw=PA_ST,
                          gp=(b_gate_dvs[-4] if len(b_gate_dvs) >= 4 else 0),
                          _dpld=ld.v):
                    _cld = _dpld
                    e.wait_ge(st.h, sw)
                    if gp:
                        e.wait_ge(dv.h, gp)
                    e.wait_ge(ld.h, _cld)
                    e.dma_start(out=arena[:, 2048 + aslot:2048 + aslot + 512],
                                in_=s0_dram[tok, og // 2, :,
                                            (og % 2) * 512:(og % 2) * 512 + 512]) \
                        .then_inc(ld.h, 16)
                    _cld += 16
                    e.wait_ge(ld.h, _cld)
                    e.dma_start(out=arena[:, 4096 + aslot:4096 + aslot + 512],
                                in_=x_in[tok * 128:(tok + 1) * 128,
                                         og * 512:(og + 1) * 512]).then_inc(ld.h, 16)
                    _cld += 16
                blk.sync(b_gld)
                gv = ld.inc(32)

                def b_gate(e, og=og, tok=tok, aslot=aslot, t=yl_ac_done[-1],
                           gv=gv, base=dv.v,
                           xp=(b_st_done[-2] if len(b_st_done) >= 2 else 0)):
                    e.wait_ge(ac.h, t)
                    e.wait_ge(ld.h, gv)
                    if xp:
                        e.wait_ge(st.h, xp)
                    gsl = 6144 + (tok % 2) * 512
                    e.tensor_tensor(arena[:, gsl:gsl + 512],
                                    arena[:, 2048 + aslot:2048 + aslot + 512],
                                    arena[:, aslot:aslot + 512],
                                    ALU.mult).then_inc(dv.h, 1)
                    e.wait_ge(dv.h, base + 1)
                    e.tensor_tensor(arena[:, gsl:gsl + 512],
                                    arena[:, 4096 + aslot:4096 + aslot + 512],
                                    arena[:, gsl:gsl + 512], ALU.add) \
                        .then_inc(dv.h, 1)
                b_gate_dv = dv.inc(2)
                b_gate_dvs.append(b_gate_dv)
                blk.vector(b_gate)

                def b_x1s(e, og=og, tok=tok, t=b_gate_dv, _dpst=st.v):
                    _cst = _dpst
                    e.wait_ge(dv.h, t)
                    e.wait_ge(st.h, _cst)
                    gsl = 6144 + (tok % 2) * 512
                    e.dma_start(out=x1_out[tok * 128:(tok + 1) * 128,
                                           og * 512:(og + 1) * 512],
                                in_=arena[:, gsl:gsl + 512]).then_inc(st.h, 16)
                    _cst += 16
                blk.sync(b_x1s)
                b_st_done.append(st.inc(16))
        PB_ST = st.v
        PB_MM = mm.v

        pst = [ps_t0, ps_t1]
        pst_free = [0, 0]
        xt2 = [arena[:, 0:4096], arena[:, 4096:8192]]
        xnbs = [AR1[:, 0:4096], AR1[:, 4096:8192]]
        c_stats_dv = [0] * NT
        c_rstd_ac = [0] * NT
        c_norm_dv = [0] * NT
        c_tns_mm = [0] * NT

        def emit_tns_tile(i):
            for g in range(8):
                b = g % 2

                def tns4(e, i=i, g=g, b=b, fr=pst_free[b], t=c_norm_dv[i]):
                    e.wait_ge(dv.h, t)
                    if fr:
                        e.wait_ge(ev.h, fr)
                    r = None
                    for j in range(4):
                        cb = g * 4 + j
                        r = e.transpose(pst[b][:, j * 128:(j + 1) * 128],
                                        xnbs[i % 2][:, cb * 128:(cb + 1) * 128],
                                        ident[:])
                    r.then_inc(mm.h, 1)
                rd = mm.inc()
                blk.tensor(tns4)

                def dve4(e, rd=rd, b=b, i=i, g=g):
                    e.wait_ge(mm.h, rd)
                    e.tensor_copy(
                        AR0[:].rearrange("p (cb n) -> p cb n", n=NTOK)
                        [:, g * 4:(g + 1) * 4, i * 128:(i + 1) * 128],
                        pst[b][:, 0:512].rearrange("p (j n) -> p j n", n=128)) \
                        .then_inc(ev.h, 1)
                pst_free[b] = ev.inc()
                blk.vector(dve4)
            c_tns_mm[i] = mm.v

        for i in range(NT):
            def c_l(e, i=i, t=PB_ST,
                    pn=(c_norm_dv[i - 2] if i >= 2 else 0), _dpld=ld.v):
                _cld = _dpld
                e.wait_ge(st.h, t)
                if pn:
                    e.wait_ge(dv.h, pn)
                e.wait_ge(ld.h, _cld)
                e.dma_start(out=xt2[i % 2], in_=x1_out[i * 128:(i + 1) * 128, :]) \
                    .then_inc(ld.h, 16)
                _cld += 16
            blk.sync(c_l)
            xl = ld.inc(16)

            def c_stats(e, i=i, xl=xl,
                        ap=(c_rstd_ac[i - 2] if i >= 2 else 0)):
                e.wait_ge(ld.h, xl)
                if ap:
                    e.wait_ge(ac.h, ap)
                bns = bn[:, (i % 2) * 52:(i % 2) * 52 + 52]
                r = None
                for k in range(8):
                    r = e.bn_stats(bns[:, 6 * k:6 * k + 6],
                                   xt2[i % 2][:, 512 * k:512 * (k + 1)])
                r.then_inc(dv.h, 1)
            c_stats_dv[i] = dv.inc()
            blk.vector(c_stats)

            def c_aggr(e, i=i, t=c_stats_dv[i]):
                e.wait_ge(dv.h, t)
                bns = bn[:, (i % 2) * 52:(i % 2) * 52 + 52]
                e.bn_aggr(bns[:, 48:50], bns[:, 0:48]).then_inc(dv.h, 1)
            t2 = dv.inc()
            blk.vector(c_aggr)

            def c_rstd(e, i=i, t=t2, a0=ac.v):
                e.wait_ge(dv.h, t)
                bns = bn[:, (i % 2) * 52:(i % 2) * 52 + 52]
                e.activation(bns[:, 50:51], bns[:, 49:50], AF.Ln,
                             bias=consts[:, 1:2], scale=1.0).then_inc(ac.h, 1)
                e.wait_ge(ac.h, a0 + 1)
                e.activation(rstd[:, (i % 2):(i % 2) + 1], bns[:, 50:51], AF.Exp,
                             bias=consts[:, 2:3], scale=-0.5).then_inc(ac.h, 1)
            c_rstd_ac[i] = ac.inc(2)
            blk.scalar(c_rstd)

            def c_norm(e, i=i, t=c_rstd_ac[i],
                       pm=(c_tns_mm[i - 2] if i >= 2 else PB_MM)):
                e.wait_ge(ac.h, t)
                e.wait_ge(mm.h, pm)
                bns = bn[:, (i % 2) * 52:(i % 2) * 52 + 52]
                e.tensor_scalar(xnbs[i % 2], xt2[i % 2], bns[:, 48:49],
                                rstd[:, (i % 2):(i % 2) + 1],
                                ALU.subtract, ALU.mult).then_inc(dv.h, 1)
            c_norm_dv[i] = dv.inc()
            blk.vector(c_norm)

            if i >= 1:
                emit_tns_tile(i - 1)
        emit_tns_tile(NT - 1)
        PC_EV = ev.v
        PC_MM = mm.v
        C_NORM_LAST = c_norm_dv[NT - 1]

        def d_cs(e, t=PC_MM, cn=C_NORM_LAST, _dpld=ld.v):
            _cld = _dpld
            e.wait_ge(mm.h, t)
            e.wait_ge(dv.h, cn)
            e.wait_ge(ld.h, _cld)
            e.dma_start(out=cosb, in_=cos_in[:]).then_inc(ld.h, 16)
            _cld += 16
            e.wait_ge(ld.h, _cld)
            e.dma_start(out=sinb, in_=sin_in[:]).then_inc(ld.h, 16)
            _cld += 16
        blk.sync(d_cs)
        D_CS = ld.inc(32)
        d_mm_done = []
        d_ac_done = []
        d_dv_done = []
        qkb_st = []
        for ft in range(48):
            def d_wl(e, ft=ft, prev=(d_mm_done[ft - 2] if ft >= 2 else 0),
                     pa=PB_MM, _dpgw=gw.v):
                _cgw = _dpgw
                e.wait_ge(mm.h, max(prev, pa))
                e.wait_ge(gw.h, _cgw)
                e.dma_start(out=AR2[:, (ft % 2) * 4096:(ft % 2 + 1) * 4096],
                            in_=w1_in[ft, :, :]).then_inc(gw.h, 16)
                _cgw += 16
            blk.gpsimd(d_wl)
            wv = gw.inc(16)

            pso = [ps_a, ps_b][ft % 2]

            def d_mm(e, ft=ft, wv=wv, xv=PC_EV, pso=pso,
                     acp=(d_ac_done[ft - 2] if ft >= 2 else 0)):
                e.wait_ge(gw.h, wv)
                e.wait_ge(ev.h, xv)
                if acp:
                    e.wait_ge(ac.h, acp)
                r = None
                for ic in range(32):
                    lhsT = AR2[:, (ft % 2) * 4096 + ic * 128:(ft % 2) * 4096 + (ic + 1) * 128]
                    rhs = AR0[:, ic * NTOK:(ic + 1) * NTOK]
                    r = mm_split(e, pso, lhsT, rhs, NTOK, start=(ic == 0), stop=(ic == 31))
                r.then_inc(mm.h, 1)
            d_mm_done.append(mm.inc())
            blk.tensor(d_mm)

            if ft < 40:
                def d_ev(e, ft=ft, t=d_mm_done[-1], pso=pso,
                         dvp=(d_dv_done[ft - 2] if ft >= 2 else C_NORM_LAST)):
                    e.wait_ge(mm.h, t)
                    if dvp:
                        e.wait_ge(dv.h, dvp)
                    e.copy(arena[:, (ft % 2) * 1024:(ft % 2 + 1) * 1024], pso[:]) \
                        .then_inc(ac.h, 1)
                d_ac_done.append(ac.inc())
                blk.scalar(d_ev)

                def d_rope(e, ft=ft, t=d_ac_done[-1], base=dv.v, cs=D_CS,
                           qp=(qkb_st[ft - 2] if ft >= 2 else 0)):
                    e.wait_ge(ac.h, t)
                    e.wait_ge(ld.h, cs)
                    e.wait_ge(dv.h, base)
                    if qp:
                        e.wait_ge(st.h, qp)
                    sl = (ft % 2) * 1024
                    qk = arena[:, sl:sl + 1024]
                    rt = arena[:, 2048:3072]
                    t1 = arena[:, 3072:4096]
                    tt = base
                    e.tensor_scalar(rt[0:64, :], qk[64:128, :], -1.0, None,
                                    ALU.mult, ALU.bypass).then_inc(dv.h, 1)
                    tt += 1
                    e.wait_ge(dv.h, tt)
                    e.tensor_copy(rt[64:128, :], qk[0:64, :]).then_inc(dv.h, 1)
                    tt += 1
                    e.wait_ge(dv.h, tt)
                    e.tensor_tensor(t1, qk, cosb, ALU.mult).then_inc(dv.h, 1)
                    tt += 1
                    e.wait_ge(dv.h, tt)
                    e.tensor_tensor(rt, rt, sinb, ALU.mult).then_inc(dv.h, 1)
                    tt += 1
                    e.wait_ge(dv.h, tt)
                    e.tensor_tensor(qkb[:, (ft % 2) * 1024:(ft % 2 + 1) * 1024],
                                    t1, rt, ALU.add).then_inc(dv.h, 1)
                d_dv_done.append(dv.inc(5))
                blk.vector(d_rope)

                def d_st(e, ft=ft, t=d_dv_done[-1], _dpst=st.v):
                    _cst = _dpst
                    e.wait_ge(dv.h, t)
                    if ft < 32:
                        e.wait_ge(st.h, _cst)
                        e.dma_start(out=q1T_out[ft, :, :],
                                    in_=qkb[:, (ft % 2) * 1024:(ft % 2 + 1) * 1024]) \
                            .then_inc(st.h, 16)
                        _cst += 16
                    else:
                        e.wait_ge(st.h, _cst)
                        e.dma_start(out=k1T_out[ft - 32, :, :],
                                    in_=qkb[:, (ft % 2) * 1024:(ft % 2 + 1) * 1024]) \
                            .then_inc(st.h, 16)
                        _cst += 16
                blk.sync(d_st)
                qkb_st.append(st.inc(16))
            else:
                def d_evv(e, ft=ft, t=d_mm_done[-1], pso=pso,
                          qp=(qkb_st[ft - 2] if ft >= 2 else 0)):
                    e.wait_ge(mm.h, t)
                    if qp:
                        e.wait_ge(st.h, qp)
                    e.copy(qkb[:, (ft % 2) * 1024:(ft % 2 + 1) * 1024], pso[:]) \
                        .then_inc(ac.h, 1)
                d_ac_done.append(ac.inc())
                blk.scalar(d_evv)

                def d_stv(e, ft=ft, t=d_ac_done[-1], _dpst=st.v):
                    _cst = _dpst
                    e.wait_ge(ac.h, t)
                    e.wait_ge(st.h, _cst)
                    e.dma_start(out=v1T_out[ft - 40, :, :],
                                in_=qkb[:, (ft % 2) * 1024:(ft % 2 + 1) * 1024]) \
                        .then_inc(st.h, 16)
                    _cst += 16
                blk.sync(d_stv)
                qkb_st.append(st.inc(16))
        PD_MM = mm.v
        PD_AC = ac.v
        PD_DV = dv.v
        PD_ST = st.v

        def e_bias(e, t=PD_MM, _dpgw=gw.v):
            _cgw = _dpgw
            e.wait_ge(mm.h, t)
            e.wait_ge(gw.h, _cgw)
            e.dma_start(out=bias[:], in_=bb_in[:]).then_inc(gw.h, 16)
            _cgw += 16
            e.wait_ge(gw.h, _cgw)
            e.dma_start(out=at2, in_=at_in[:]).then_inc(gw.h, 16)
            _cgw += 16
            e.wait_ge(gw.h, _cgw)
            e.dma_start(out=brr2, in_=brr_in[:]).then_inc(gw.h, 16)
            _cgw += 16
        blk.gpsimd(e_bias)
        gw.inc(48)
        E_BIAS = E_AT = gw.v
        comb_dv = []
        combT_ev = []
        for tok in range(8):
            def e_rmm(e, tok=tok, t=PD_AC, xv=PC_EV, wv=GW_EMISC):
                e.wait_ge(ac.h, t)
                e.wait_ge(ev.h, xv)
                e.wait_ge(gw.h, wv)
                r = None
                for ic in range(32):
                    lhsT = AR0[:, ic * 1024 + tok * 128:ic * 1024 + (tok + 1) * 128]
                    rhs = wr2[:, ic * 8:(ic + 1) * 8]
                    r = e.matmul(ps_a[:, tok * 8:(tok + 1) * 8], lhsT, rhs,
                                 start=(ic == 0), stop=(ic == 31))
                r.then_inc(mm.h, 1)
            t_rmm = mm.inc()
            blk.tensor(e_rmm)

            ro = tok * 32

            def e_rmax(e, tok=tok, ro=ro, t=t_rmm, base=dv.v):
                e.wait_ge(mm.h, t)
                e.wait_ge(dv.h, base)
                e.reduce_max(rout[:, ro + 24:ro + 25], ps_a[:, tok * 8:(tok + 1) * 8],
                             axis=mybir.AxisListType.X).then_inc(dv.h, 1)
                e.wait_ge(dv.h, base + 1)
                e.tensor_scalar(rout[:, ro + 25:ro + 26], rout[:, ro + 24:ro + 25],
                                -1.0, None, ALU.mult, ALU.bypass).then_inc(dv.h, 1)
            t_rmax = dv.inc(2)
            blk.vector(e_rmax)

            def e_rexp(e, tok=tok, ro=ro, t=t_rmax):
                e.wait_ge(dv.h, t)
                e.activation(rout[:, ro:ro + 8], ps_a[:, tok * 8:(tok + 1) * 8],
                             AF.Exp, bias=rout[:, ro + 25:ro + 26], scale=1.0) \
                    .then_inc(ac.h, 1)
            t_rexp = ac.inc()
            blk.scalar(e_rexp)

            def e_rsel(e, tok=tok, ro=ro, t=t_rexp, base=dv.v):
                e.wait_ge(ac.h, t)
                tt = base
                e.wait_ge(dv.h, tt)
                el = rout[:, ro:ro + 8]

                def step(r):
                    nonlocal tt
                    tt += 1
                    r.then_inc(dv.h, 1)
                    e.wait_ge(dv.h, tt)

                step(e.tensor_scalar(rout[:, ro + 8:ro + 16], el, 1.0, None,
                                     ALU.is_ge, ALU.bypass))
                step(e.scalar_tensor_tensor(rout[:, ro + 16:ro + 24],
                                            rout[:, ro + 8:ro + 16], -2.0,
                                            el, ALU.mult, ALU.add))
                step(e.reduce_max(rout[:, ro + 26:ro + 27], rout[:, ro + 16:ro + 24],
                                  axis=mybir.AxisListType.X))
                step(e.tensor_scalar(rout[:, ro + 8:ro + 16], el,
                                     rout[:, ro + 26:ro + 27], None,
                                     ALU.is_ge, ALU.bypass))
                step(e.tensor_tensor(rout[:, ro + 16:ro + 24], el,
                                     rout[:, ro + 8:ro + 16], ALU.mult))
                step(e.tensor_scalar(rout[:, ro + 27:ro + 28],
                                     rout[:, ro + 26:ro + 27], 1.0, None,
                                     ALU.add, ALU.bypass))
                step(e.reciprocal(rout[:, ro + 28:ro + 29], rout[:, ro + 27:ro + 28]))
                step(e.tensor_scalar(routb[:, tok * 8:(tok + 1) * 8],
                                     rout[:, ro + 16:ro + 24],
                                     rout[:, ro + 28:ro + 29], None,
                                     ALU.mult, ALU.bypass))
            comb_dv.append(dv.inc(8))
            blk.vector(e_rsel)

            def e_ctr(e, tok=tok, t=comb_dv[-1], pc=PC_EV,
                      fr=(combT_ev[tok - 1] if tok >= 1 else 0)):
                e.wait_ge(dv.h, t)
                e.wait_ge(ev.h, max(pc, fr))
                e.transpose(ps_t0[0:8, 0:128], routb[:, tok * 8:(tok + 1) * 8],
                            ident[:]).then_inc(mm.h, 1)
            t_ctr = mm.inc()
            blk.tensor(e_ctr)

            def e_cev(e, tok=tok, t=t_ctr):
                e.wait_ge(mm.h, t)
                e.tensor_copy(combT[:, tok * 128:(tok + 1) * 128], ps_t0[0:8, 0:128]) \
                    .then_inc(ev.h, 1)
            combT_ev.append(ev.inc())
            blk.vector(e_cev)
        E_COMBT = ev.v

        def e_cmm(e, t=E_COMBT, wv=GW_EMISC):
            e.wait_ge(ev.h, t)
            e.wait_ge(gw.h, wv)
            r = mm_split(e, ps_b, sel2[:], combT[:], NTOK, start=True, stop=True)
            r.then_inc(mm.h, 1)
        t_cmm = mm.inc()
        blk.tensor(e_cmm)

        def e_cmev(e, t=t_cmm):
            e.wait_ge(mm.h, t)
            e.tensor_copy(cmat, ps_b[:]).then_inc(dv.h, 1)
        E_CMAT = dv.inc()
        blk.vector(e_cmev)

        def e_hmm(e, t=E_CMAT, wv=E_AT, xv=PC_EV):
            e.wait_ge(dv.h, t)
            e.wait_ge(gw.h, wv)
            e.wait_ge(ev.h, xv)
            r = None
            for ic in range(32):
                lhsT = at2[:, ic * 128:(ic + 1) * 128]
                rhs = AR0[:, ic * NTOK:(ic + 1) * NTOK]
                r = mm_split(e, ps_a, lhsT, rhs, NTOK, start=(ic == 0), stop=(ic == 31))
            r.then_inc(mm.h, 1)
        t_hmm = mm.inc()
        blk.tensor(e_hmm)

        def e_hev(e, t=t_hmm):
            e.wait_ge(mm.h, t)
            e.tensor_tensor(hbuf, ps_a[:], cmat, ALU.mult).then_inc(dv.h, 1)
        E_HBUF = dv.inc()
        blk.vector(e_hev)


        e_wld = []
        e_mm_done = []
        e_ac_done = []
        g1_st = []
        for og in range(8):
            def e_wl(e, og=og, pd=PD_MM,
                     prev=(e_mm_done[8 * (og - 2) + 7] if og >= 2 else 0),
                     _dpgw=gw.v):
                _cgw = _dpgw
                e.wait_ge(mm.h, max(pd, prev))
                e.wait_ge(gw.h, _cgw)
                e.dma_start(out=AR1[:, (og % 2) * 16384:(og % 2 + 1) * 16384],
                            in_=wb_in[og, :, :]).then_inc(gw.h, 16)
                _cgw += 16
            blk.gpsimd(e_wl)
            e_wld.append(gw.inc(16))

            for tok in range(8):
                pso = [ps_b, ps_a][tok % 2]
                qslot = ((og * 8 + tok) % 4) * 512

                def eb_mm(e, og=og, tok=tok, pso=pso, wv=max(e_wld[og], E_AT),
                          hb=E_HBUF, bi=E_BIAS,
                          acp=(e_ac_done[-2] if len(e_ac_done) >= 2 else 0)):
                    e.wait_ge(gw.h, max(wv, bi))
                    e.wait_ge(dv.h, hb)
                    if acp:
                        e.wait_ge(ac.h, acp)
                    e.matmul(pso[:, 0:512], ones[0:1, 0:128],
                             bias[0:1, og * 512:(og + 1) * 512],
                             start=True, stop=False)
                    lhsT = hbuf[:, tok * 128:(tok + 1) * 128]
                    e.matmul(pso[:, 0:512], lhsT,
                             brr2[:, og * 512:(og + 1) * 512],
                             start=False, stop=False)
                    r = None
                    for ic in range(32):
                        lh = AR0[:, ic * 1024 + tok * 128:ic * 1024 + (tok + 1) * 128]
                        rhs = AR1[:, (og % 2) * 16384 + ic * 512:
                                  (og % 2) * 16384 + (ic + 1) * 512]
                        r = e.matmul(pso[:, 0:512], lh, rhs,
                                     start=False, stop=(ic == 31))
                    r.then_inc(mm.h, 1)
                e_mm_done.append(mm.inc())
                blk.tensor(eb_mm)

                def eb_ev(e, og=og, tok=tok, pso=pso, qslot=qslot,
                          t=e_mm_done[-1], pdst=PD_ST,
                          gp=(g1_st[-4] if len(g1_st) >= 4 else 0)):
                    e.wait_ge(mm.h, t)
                    e.wait_ge(st.h, max(pdst, gp))
                    e.activation(qkb[:, qslot:qslot + 512],
                                 pso[:, 0:512], AF.Silu, bias=consts[:, 2:3],
                                 scale=1.0).then_inc(ac.h, 1)
                e_ac_done.append(ac.inc())
                blk.scalar(eb_ev)

                def eb_st(e, og=og, tok=tok, qslot=qslot, t=e_ac_done[-1],
                          _dpst=st.v):
                    _cst = _dpst
                    e.wait_ge(ac.h, t)
                    e.wait_ge(st.h, _cst)
                    e.dma_start(out=g1_out[tok * 128:(tok + 1) * 128,
                                           og * 512:(og + 1) * 512],
                                in_=qkb[:, qslot:qslot + 512]).then_inc(st.h, 16)
                    _cst += 16
                blk.sync(eb_st)
                g1_st.append(st.inc(16))

        def fin(e, t=st.v):
            e.wait_ge(st.h, t)
        blk.gpsimd(fin)

    return nc



_CACHE = {}


def _prep_launch1(inputs):
    Wqkv = np.asarray(inputs["L0_attn_Wqkv"], np.float32)
    wqkv_t = np.ascontiguousarray(
        Wqkv.reshape(32, 128, 24, 128).transpose(2, 1, 0, 3).reshape(24, 128, H))
    bqkv = np.ascontiguousarray(
        np.asarray(inputs["L0_attn_bqkv"], np.float32).reshape(1, 3072))
    qw = np.asarray(inputs["L0_ffn_qweight"])
    lut4 = np.asarray(inputs["L0_ffn_lut4"], np.float32)
    mob = mobius_matrix()
    ident = np.eye(128, dtype=BF)
    x = np.asarray(inputs["L0_attn_x"], np.float32).reshape(B * S, H)

    in_maps = []
    for c in range(NCORE := 8):
        rows = core_rows(c)
        o0 = 512 * c
        in_maps.append({
            "x": np.ascontiguousarray(x[rows]),
            "wqkv_t": wqkv_t,
            "bqkv": bqkv,
            "qw": np.ascontiguousarray(
                qw[0:4, o0:o0 + 512, :].reshape(4, 4, 128, 128)),
            "lut4T": np.ascontiguousarray(lut4[o0:o0 + 512, :].T),
            "mob": mob,
            "ident": ident,
            "identf": np.eye(128, dtype=np.float32),
        })
    return in_maps




def build_launch3():
    nc = bass.Bass("TRN2", target_bir_lowering=False)
    x1_in = nc.dram_tensor("x1", [NTOK, H], F32, kind="ExternalInput").ap()
    g1_in = nc.dram_tensor("g1", [NTOK, H], BF16, kind="ExternalInput").ap()
    q1T_in = nc.dram_tensor("q1T", [32, 128, NTOK], BF16, kind="ExternalInput").ap()
    k1_in = nc.dram_tensor("k1", [8, 128, S], BF16, kind="ExternalInput").ap()
    v1_in = nc.dram_tensor("v1", [128, 32768], BF16, kind="ExternalInput").ap()
    wo1_in = nc.dram_tensor("wo1", [8, 32, 128, 512], F32, kind="ExternalInput").ap()
    mask_in = nc.dram_tensor("mask", [128, 4096], BF16, kind="ExternalInput").ap()

    out_out = nc.dram_tensor("out", [NTOK, H], F32, kind="ExternalOutput").ap()

    SC1 = 0.08838834764831843

    from contextlib import ExitStack
    with ExitStack() as ctx:
        def sb(name, shape, dt):
            return ctx.enter_context(nc.sbuf_tensor(name, shape, dt))

        def psum(name, shape, dt=F32):
            return ctx.enter_context(nc.psum_tensor(name, shape, dt))

        AR0 = sb("V1", [128, 32768], BF16)
        AR1 = sb("OT", [128, 32768], BF16)
        k1b = sb("k1b", [128, 2 * 4096], BF16)
        q1h = sb("q1h", [128, 2 * 1024], BF16)
        exps = sb("exps", [128, 4 * 1024], BF16)
        maskb = sb("maskb", [128, 4096], BF16)
        rvec = sb("rvec3", [1, NTOK], F32)
        rrow = sb("rrow3", [1, NTOK], F32)
        onesf = sb("onesf3", [1, 128], F32)
        ones_col = sb("ones_col3", [128, 1], BF16)
        consts = sb("consts3", [128, 2], F32)
        x1t = sb("x1t", [128, 2 * 1024], F32)
        g1t = sb("g1t", [128, 2 * 1024], BF16)
        ot1 = sb("ot1", [128, 2 * 1024], F32)
        ps_s0 = psum("ps3_s0", [128, 1024], F32)
        ps_s1 = psum("ps3_s1", [128, 1024], F32)
        ps_o = psum("ps3_o", [128, 1024], F32)
        ps_r1 = psum("ps3_r1", [128, 512], F32)
        ps_r2 = psum("ps3_r2", [128, 512], F32)
        ld = Sem(ctx.enter_context(nc.semaphore("ld3")))
        gw = Sem(ctx.enter_context(nc.semaphore("gw3")))
        mm = Sem(ctx.enter_context(nc.semaphore("mm3")))
        dv = Sem(ctx.enter_context(nc.semaphore("dv3")))
        ac = Sem(ctx.enter_context(nc.semaphore("ac3")))
        st = Sem(ctx.enter_context(nc.semaphore("st3")))
        blk = ctx.enter_context(nc.Block())
        pss = [ps_s0, ps_s1]

        def c0(e, _dpld=ld.v):
            _cld = _dpld
            e.wait_ge(ld.h, _cld)
            e.dma_start(out=AR0[:], in_=v1_in[:]).then_inc(ld.h, 16)
            _cld += 16
            e.wait_ge(ld.h, _cld)
            e.dma_start(out=maskb[:], in_=mask_in[:]).then_inc(ld.h, 16)
            _cld += 16
        blk.sync(c0)
        LD_C = ld.inc(32)

        def v_c(e):
            e.memset(ones_col[:], 1.0)
            e.memset(onesf[:], 1.0)
            e.memset(consts[:, 0:1], -SHIFT)
            e.memset(consts[:, 1:2], 0.0).then_inc(dv.h, 1)
        blk.vector(v_c)
        DV_C = dv.inc()

        def nq_of(ka):
            return (8 - ka // 4) * 128

        def pe_ops_of_head(h):
            ops = [("smm", 0)]
            if h > 0:
                ops += [("hrbP", -1)]
            for ka in range(1, 32):
                ops += [("smm", ka), ("sum", ka - 1)]
            ops += [("sum", 31)]
            return ops

        plan = {}
        p_mm, p_ac, p_dv = mm.v, ac.v, dv.v
        for h in range(NH):
            for op, ka in pe_ops_of_head(h):
                p_mm += 1
                if op == "hrbP":
                    plan[("hrb", h - 1)] = p_mm
                else:
                    plan[(op, h, ka)] = p_mm
            for ka in range(32):
                p_ac += 1
                plan[("exp", h, ka)] = p_ac
            plan[("svexp", h)] = p_ac + 3
            plan[("otev", h)] = p_ac + 4
            p_ac += 4
            for ka in range(32):
                p_dv += 1
                plan[("mask", h, ka)] = p_dv
                if ka == 0 and h > 0:
                    p_dv += 1
                    plan[("omult", h - 1)] = p_dv
        plan[("hrb", NH - 1)] = p_mm + 1
        plan[("omult", NH - 1)] = p_dv + 1

        k_ld = [0] * 8
        q_ld = [0] * 32
        for h in range(NH):
            kv = h // 4
            if h % 4 == 0:
                def kld(e, kv=kv,
                        prev=(plan[("smm", 4 * (kv - 2) + 3, 31)]
                              if kv >= 2 else 0), _dpld=ld.v):
                    _cld = _dpld
                    if prev:
                        e.wait_ge(mm.h, prev)
                    e.wait_ge(ld.h, _cld)
                    e.dma_start(out=k1b[:, (kv % 2) * 4096:(kv % 2 + 1) * 4096],
                                in_=k1_in[kv, :, :]).then_inc(ld.h, 16)
                    _cld += 16
                blk.sync(kld)
                k_ld[kv] = ld.inc(16)

            def qld(e, h=h,
                    prev=(plan[("smm", h - 2, 31)] if h >= 2 else 0), _dpgw=gw.v):
                _cgw = _dpgw
                if prev:
                    e.wait_ge(mm.h, prev)
                e.wait_ge(gw.h, _cgw)
                e.dma_start(out=q1h[:, (h % 2) * 1024:(h % 2 + 1) * 1024],
                            in_=q1T_in[h, :, :]).then_inc(gw.h, 16)
                _cgw += 16
            blk.gpsimd(qld)
            q_ld[h] = gw.inc(16)

            def pe_head(e, h=h, kv=kv, kl=k_ld[kv], ql=q_ld[h]):
                e.wait_ge(ld.h, kl)
                e.wait_ge(gw.h, ql)
                for op, ka in pe_ops_of_head(h):
                    if op == "smm":
                        b = ka % 2
                        if ka >= 2:
                            e.wait_ge(ac.h, plan[("exp", h, ka - 2)])
                        elif h > 0:
                            e.wait_ge(ac.h, plan[("exp", h - 1, 30 + ka)])
                            if ka == 1:
                                e.wait_ge(dv.h, plan[("omult", h - 1)])
                        nq = nq_of(ka)
                        lhsT = k1b[:, (kv % 2) * 4096 + ka * 128:
                                   (kv % 2) * 4096 + (ka + 1) * 128]
                        rhs = q1h[:, (h % 2) * 1024:(h % 2) * 1024 + nq]
                        r = mm_split(e, pss[b], lhsT, rhs, nq,
                                     start=True, stop=True)
                        r.then_inc(mm.h, 1)
                    elif op == "sum":
                        e.wait_ge(dv.h, plan[("mask", h, ka)])
                        nq = nq_of(ka)
                        sl = (ka % 4) * 1024
                        st0 = (ka == 0)
                        n1 = min(nq, 512)
                        lhsT = AR0[:, (kv * 32 + ka) * 128:
                                   (kv * 32 + ka + 1) * 128]
                        r = e.matmul(ps_r1[0:1, 0:n1], ones_col[:],
                                     exps[:, sl:sl + n1],
                                     start=st0, stop=(ka == 31))
                        r = e.matmul(ps_o[:, 0:n1], lhsT, exps[:, sl:sl + n1],
                                     start=st0, stop=(ka == 31))
                        if nq > 512:
                            r = e.matmul(ps_r2[0:1, 0:nq - 512], ones_col[:],
                                         exps[:, sl + 512:sl + nq],
                                         start=st0, stop=(ka == 15))
                            r = e.matmul(ps_o[:, 512:nq], lhsT,
                                         exps[:, sl + 512:sl + nq],
                                         start=st0, stop=(ka == 15))
                        r.then_inc(mm.h, 1)
                    else:
                        e.wait_ge(ac.h, plan[("svexp", h - 1)])
                        e.matmul(ps_s1[:, 0:512], onesf[:], rrow[:, 0:512],
                                 start=True, stop=True)
                        e.matmul(ps_s1[:, 512:1024], onesf[:],
                                 rrow[:, 512:1024],
                                 start=True, stop=True).then_inc(mm.h, 1)
            for op, ka in pe_ops_of_head(h):
                v = mm.inc()
                key = ("hrb", h - 1) if op == "hrbP" else (op, h, ka)
                assert v == plan[key], (h, op, ka)
            blk.tensor(pe_head)

            def act_head(e, h=h, dc=DV_C):
                if h == 0:
                    e.wait_ge(dv.h, dc)
                for ka in range(32):
                    e.wait_ge(mm.h, plan[("smm", h, ka)])
                    nq = nq_of(ka)
                    sl = (ka % 4) * 1024
                    e.activation(exps[:, sl:sl + nq],
                                 pss[ka % 2][:, 0:nq], AF.Exp,
                                 bias=consts[:, 0:1], scale=SC1) \
                        .then_inc(ac.h, 1)
                e.wait_ge(mm.h, plan[("sum", h, 31)])
                e.activation(rvec[:, 0:512], ps_r1[0:1, 0:512], AF.Ln,
                             bias=consts[0:1, 1:2], scale=1.0).then_inc(ac.h, 1)
                e.activation(rvec[:, 512:1024], ps_r2[0:1, 0:512], AF.Ln,
                             bias=consts[0:1, 1:2], scale=1.0).then_inc(ac.h, 1)
                e.wait_ge(ac.h, plan[("svexp", h)] - 1)
                e.activation(rrow[:, 0:1024], rvec[:, 0:1024], AF.Exp,
                             bias=consts[0:1, 1:2], scale=-1.0).then_inc(ac.h, 1)
                e.copy(AR1[:, h * 1024:(h + 1) * 1024], ps_o[:]) \
                    .then_inc(ac.h, 1)
            for ka in range(32):
                v = ac.inc()
                assert v == plan[("exp", h, ka)], (h, ka)
            assert ac.inc(4) == plan[("otev", h)], h
            blk.scalar(act_head)

            def dve_head(e, h=h):
                for ka in range(32):
                    e.wait_ge(ac.h, plan[("exp", h, ka)])
                    nq = nq_of(ka)
                    sl = (ka % 4) * 1024
                    e.tensor_tensor(exps[:, sl + nq - 128:sl + nq],
                                    exps[:, sl + nq - 128:sl + nq],
                                    maskb[:, ka * 128:(ka + 1) * 128],
                                    ALU.mult).then_inc(dv.h, 1)
                    if ka == 0 and h > 0:
                        e.wait_ge(mm.h, plan[("hrb", h - 1)])
                        e.wait_ge(ac.h, plan[("otev", h - 1)])
                        e.tensor_tensor(AR1[:, (h - 1) * 1024:h * 1024],
                                        AR1[:, (h - 1) * 1024:h * 1024],
                                        ps_s1[:], ALU.mult).then_inc(dv.h, 1)
            for ka in range(32):
                v = dv.inc()
                assert v == plan[("mask", h, ka)], (h, ka)
                if ka == 0 and h > 0:
                    assert dv.inc() == plan[("omult", h - 1)], h
            blk.vector(dve_head)

        def pe_fin(e, hl=NH - 1):
            e.wait_ge(ac.h, plan[("svexp", hl)])
            e.matmul(ps_s1[:, 0:512], onesf[:], rrow[:, 0:512],
                     start=True, stop=True)
            e.matmul(ps_s1[:, 512:1024], onesf[:], rrow[:, 512:1024],
                     start=True, stop=True).then_inc(mm.h, 1)
        assert mm.inc() == plan[("hrb", NH - 1)]
        blk.tensor(pe_fin)

        def dve_fin(e, hl=NH - 1):
            e.wait_ge(mm.h, plan[("hrb", hl)])
            e.wait_ge(ac.h, plan[("otev", hl)])
            e.tensor_tensor(AR1[:, hl * 1024:(hl + 1) * 1024],
                            AR1[:, hl * 1024:(hl + 1) * 1024],
                            ps_s1[:], ALU.mult).then_inc(dv.h, 1)
        assert dv.inc() == plan[("omult", NH - 1)]
        blk.vector(dve_fin)
        ATT_DV = dv.v
        ATT_MM = mm.v
        last_exp_slot = [plan[("exp", NH - 1, 30)], plan[("exp", NH - 1, 31)]]

        psw = [ps_o, ps_s0]
        wo_win = []
        wo_mm = []
        gate_dv = []
        out_st = []
        for og in range(8):
            def wwin(e, og=og, att=ATT_MM,
                     lastmm=(wo_mm[8 * (og - 2) + 7] if og >= 2 else 0),
                     _dpgw=gw.v):
                _cgw = _dpgw
                e.wait_ge(mm.h, max(att, lastmm))
                e.wait_ge(gw.h, _cgw)
                e.dma_start(out=AR0[:, (og % 2) * 16384:(og % 2 + 1) * 16384]
                            .rearrange("p (dc f) -> p dc f", f=512),
                            in_=wo1_in[og, :, :, :].rearrange("dc p f -> p dc f")) \
                    .then_inc(gw.h, 16)
                _cgw += 16
            blk.gpsimd(wwin)
            wo_win.append(gw.inc(16))

            for tok in range(8):
                sl = ((og * 8 + tok) % 4) * 512

                def gld(e, og=og, tok=tok, sl=sl,
                        gp=(gate_dv[-4] if len(gate_dv) >= 4 else 0), _dpld=ld.v):
                    _cld = _dpld
                    if gp:
                        e.wait_ge(dv.h, gp)
                    e.wait_ge(ld.h, _cld)
                    e.dma_start(out=x1t[:, sl:sl + 512],
                                in_=x1_in[tok * 128:(tok + 1) * 128,
                                          og * 512:(og + 1) * 512]).then_inc(ld.h, 16)
                    _cld += 16
                    e.wait_ge(ld.h, _cld)
                    e.dma_start(out=g1t[:, sl:sl + 512],
                                in_=g1_in[tok * 128:(tok + 1) * 128,
                                          og * 512:(og + 1) * 512]).then_inc(ld.h, 16)
                    _cld += 16
                blk.sync(gld)
                gv = ld.inc(32)

                def wmm(e, og=og, tok=tok, t=ATT_DV, wv=wo_win[og],
                        ae=(last_exp_slot[0] if tok % 2 == 1 else 0),
                        gp=(gate_dv[-2] if len(gate_dv) >= 2 else 0)):
                    e.wait_ge(dv.h, t)
                    e.wait_ge(gw.h, wv)
                    if ae:
                        e.wait_ge(ac.h, ae)
                    if gp:
                        e.wait_ge(dv.h, gp)
                    r = None
                    for dc in range(32):
                        lhsT = AR1[:, dc * 1024 + tok * 128:dc * 1024 + (tok + 1) * 128]
                        rhs = AR0[:, (og % 2) * 16384 + dc * 512:
                                  (og % 2) * 16384 + (dc + 1) * 512]
                        r = e.matmul(psw[tok % 2][:, 0:512], lhsT, rhs,
                                     start=(dc == 0), stop=(dc == 31))
                    r.then_inc(mm.h, 1)
                wo_mm.append(mm.inc())
                blk.tensor(wmm)

                def gate(e, og=og, tok=tok, sl=sl, t=wo_mm[-1], gv=gv, base=dv.v,
                         sp=(out_st[-4] if len(out_st) >= 4 else 0)):
                    e.wait_ge(mm.h, t)
                    e.wait_ge(ld.h, gv)
                    if sp:
                        e.wait_ge(st.h, sp)
                    e.tensor_tensor(ot1[:, sl:sl + 512], psw[tok % 2][:, 0:512],
                                    g1t[:, sl:sl + 512], ALU.mult).then_inc(dv.h, 1)
                    e.wait_ge(dv.h, base + 1)
                    e.tensor_tensor(ot1[:, sl:sl + 512], ot1[:, sl:sl + 512],
                                    x1t[:, sl:sl + 512], ALU.add).then_inc(dv.h, 1)
                gate_dv.append(dv.inc(2))
                blk.vector(gate)

                def ost(e, og=og, tok=tok, sl=sl, t=gate_dv[-1], _dpst=st.v):
                    _cst = _dpst
                    e.wait_ge(dv.h, t)
                    e.wait_ge(st.h, _cst)
                    e.dma_start(out=out_out[tok * 128:(tok + 1) * 128,
                                            og * 512:(og + 1) * 512],
                                in_=ot1[:, sl:sl + 512]).then_inc(st.h, 16)
                    _cst += 16
                blk.sync(ost)
                out_st.append(st.inc(16))

        def fin(e, t=st.v):
            e.wait_ge(st.h, t)
        blk.gpsimd(fin)

    return nc


def _prep_launch3(inputs, r2):
    perm = batch_perm()
    Wo1T = np.asarray(inputs["L1_attn_Wo"], np.float32).T
    wo1t = np.ascontiguousarray(
        Wo1T.reshape(32, 128, 8, 512).transpose(2, 0, 1, 3))
    k1b_b, v1_b = [], []
    for b in range(B):
        K1_loc = np.concatenate(
            [np.asarray(r2[4 * b + cc]["k1T"]).reshape(1024, NTOK)
             for cc in range(4)], axis=1)
        K1_full = np.ascontiguousarray(K1_loc[:, perm])
        k1b_b.append(np.ascontiguousarray(K1_full.reshape(8, 128, S)))
        V1_loc = np.concatenate(
            [np.asarray(r2[4 * b + cc]["v1T"]).reshape(1024, NTOK)
             for cc in range(4)], axis=1)
        V1_full = np.ascontiguousarray(V1_loc[:, perm].T)
        v1_b.append(np.ascontiguousarray(
            V1_full.reshape(32, 128, 8, 128).transpose(1, 2, 0, 3).reshape(128, 32768)))
    in_maps = []
    for c in range(8):
        b, cc = divmod(c, 4)
        masks = causal_masks(cc)
        maskb = np.ascontiguousarray(masks.transpose(1, 0, 2).reshape(128, 4096))
        in_maps.append({
            "x1": np.asarray(r2[c]["x1"]),
            "g1": np.asarray(r2[c]["g1"]),
            "q1T": np.asarray(r2[c]["q1T"]),
            "k1": k1b_b[b],
            "v1": v1_b[b],
            "wo1": wo1t,
            "mask": maskb,
        })
    return in_maps


def _prep_launch2(inputs, r1):
    perm = batch_perm()
    x = np.asarray(inputs["L0_attn_x"], np.float32).reshape(B * S, H)
    Wo = np.asarray(inputs["L0_attn_Wo"], np.float32)
    wo2 = np.ascontiguousarray(Wo.reshape(8, 128, 4, 1024).transpose(2, 0, 1, 3))
    bo = np.ascontiguousarray(np.asarray(inputs["L0_attn_bo"], np.float32).reshape(1, H))
    bffn = np.ascontiguousarray(np.asarray(inputs["L0_ffn_bias"], np.float32).reshape(1, H))
    Wq = np.asarray(inputs["L1_attn_Wq"], np.float32)
    Wk = np.asarray(inputs["L1_attn_Wk"], np.float32)
    Wv = np.asarray(inputs["L1_attn_Wv"], np.float32)
    W1T = np.concatenate([Wq.T, Wk.T, Wv.T], axis=1)
    w1t = np.ascontiguousarray(
        W1T.reshape(32, 128, 48, 128).transpose(2, 1, 0, 3).reshape(48, 128, H))
    Wr = np.asarray(inputs["L1_ffn_W_router"], np.float32)
    wr2 = np.ascontiguousarray(Wr.T.reshape(32, 128, 8).transpose(1, 0, 2).reshape(128, 256))
    A_stack = np.asarray(inputs["L1_ffn_A_stack"], np.float32)
    AT = A_stack.transpose(2, 0, 1).reshape(H, 128)
    at2 = np.ascontiguousarray(AT.reshape(32, 128, 128).transpose(1, 0, 2).reshape(128, H))
    B_stack = np.asarray(inputs["L1_ffn_B_stack"], np.float32)
    brr = np.ascontiguousarray(B_stack.transpose(0, 2, 1).reshape(128, H))
    WbT = np.asarray(inputs["L1_ffn_W_base"], np.float32).T
    wb2 = np.ascontiguousarray(
        WbT.reshape(32, 128, 8, 512).transpose(2, 1, 0, 3).reshape(8, 128, 16384))
    bb = np.ascontiguousarray(np.asarray(inputs["L1_ffn_b_base"], np.float32).reshape(1, H))
    sel = np.zeros((8, 128), np.float32)
    for er in range(128):
        sel[er // 16, er] = 2.0
    ident = np.eye(128, dtype=BF)

    kt2b, v2b = [], []
    for b in range(B):
        KT_loc = np.concatenate(
            [np.asarray(r1[4 * b + cc]["qkvT"])[8:16].reshape(1024, NTOK)
             for cc in range(4)], axis=1)
        KT_full = KT_loc[:, perm]
        kt2b.append(np.ascontiguousarray(
            KT_full.reshape(8, 128, 32, 128).transpose(2, 1, 0, 3).reshape(32, 128, 1024)))
        VT_loc = np.concatenate(
            [np.asarray(r1[4 * b + cc]["qkvT"])[16:24].reshape(1024, NTOK)
             for cc in range(4)], axis=1)
        V_full = np.ascontiguousarray(VT_loc[:, perm].T)
        v2b.append(np.ascontiguousarray(
            V_full.reshape(32, 128, 8, 128).transpose(1, 0, 2, 3).reshape(128, 32768)))

    wl_full = np.concatenate(
        [np.asarray(r1[c]["wlT"]).reshape(H, 512) for c in range(8)], axis=1)
    wl2 = np.ascontiguousarray(
        wl_full.reshape(32, 128, 8, 512).transpose(2, 1, 0, 3).reshape(8, 128, 16384))

    in_maps = []
    for c in range(8):
        b = c // 4
        rows = core_rows(c)
        cosT, sinT = rope_tables(rows % S)
        in_maps.append({
            "x": np.ascontiguousarray(x[rows]),
            "xnT": np.asarray(r1[c]["xnT"]),
            "qT": np.ascontiguousarray(np.asarray(r1[c]["qkvT"])[0:8]),
            "kt": kt2b[b],
            "v": v2b[b],
            "wl": wl2,
            "wo": wo2,
            "bo": bo,
            "bffn": bffn,
            "w1": w1t,
            "cosT": cosT,
            "sinT": sinT,
            "wrT": wr2,
            "at2": at2,
            "brr": brr,
            "wb": wb2,
            "bb": bb,
            "sel": sel,
            "ident": ident,
        })
    return in_maps

def kernel(**inputs):
    if "nc1" not in _CACHE:
        _CACHE["nc1"] = build_launch1()
        _CACHE["nc2"] = build_launch2()
        _CACHE["nc3"] = build_launch3()
    im1 = _prep_launch1(inputs)
    r1 = run_bass_kernel_spmd(_CACHE["nc1"], im1, core_ids=list(range(8))).results
    im2 = _prep_launch2(inputs, r1)
    r2 = run_bass_kernel_spmd(_CACHE["nc2"], im2, core_ids=list(range(8))).results
    im3 = _prep_launch3(inputs, r2)
    r3 = run_bass_kernel_spmd(_CACHE["nc3"], im3, core_ids=list(range(8))).results
    out = np.empty((B * S, H), np.float32)
    for c in range(8):
        out[core_rows(c)] = np.asarray(r3[c]["out"], np.float32)
    return out.reshape(B, S, H)


# version 15 (final)
# speedup vs baseline: 1.0422x; 1.0320x over previous
import numpy as np
import ml_dtypes

import concourse.bass as bass
import concourse.mybir as mybir
from concourse.bass_utils import run_bass_kernel_spmd

F32 = mybir.dt.float32
BF16 = mybir.dt.bfloat16
I32 = mybir.dt.int32
AF = mybir.ActivationFunctionType
ALU = mybir.AluOpType

BF = ml_dtypes.bfloat16

H = 4096
S = 4096
B = 2
NTOK = 1024
NT = 8
NH, NKV, HD = 32, 8, 128
SHIFT = 40.0
PADEXT = [32, 28, 24, 20, 16, 12, 8, 4]



def core_tiles(cc):
    tiles = []
    for p in [cc, cc + 4, cc + 8, cc + 12]:
        tiles += [p, 31 - p]
    return sorted(tiles, reverse=True)


def core_rows(c):
    b, cc = divmod(c, 4)
    rows = []
    for t in core_tiles(cc):
        rows.extend(range(b * S + t * 128, b * S + t * 128 + 128))
    return np.asarray(rows, dtype=np.int64)


def batch_perm():
    pos = np.empty(S, np.int64)
    for cc in range(4):
        tiles = core_tiles(cc)
        for j, t in enumerate(tiles):
            pos[t * 128:(t + 1) * 128] = cc * NTOK + j * 128 + np.arange(128)
    return pos


def mobius_matrix():
    M = np.zeros((16, 16), np.float32)
    for s in range(16):
        for t in range(16):
            if (t & s) == t:
                M[t, s] = (-1.0) ** (bin(s).count("1") - bin(t).count("1"))
    return M


def rope_tables(rows_in_seq):
    inv = 1.0 / (1e4 ** (np.arange(0, HD, 2) / HD))
    freqs = rows_in_seq[:, None].astype(np.float64) * inv[None, :]
    emb = np.concatenate([freqs, freqs], -1)
    return (np.ascontiguousarray(np.cos(emb).astype(np.float32).T),
            np.ascontiguousarray(np.sin(emb).astype(np.float32).T))


def causal_masks(cc):
    tiles = core_tiles(cc)
    out = np.zeros((32, 128, 128), np.float32)
    for k in range(32):
        g = tiles[7 - k // 4]
        if k < g:
            out[k] = 1.0
        elif k == g:
            kpos = k * 128 + np.arange(128)
            gq = g * 128 + np.arange(128)
            out[k] = (kpos[:, None] <= gq[None, :]).astype(np.float32)
    return np.ascontiguousarray(out.astype(BF))


def tile_wT(wT, part=128, free=1024):
    C, Fd = wT.shape
    t = wT.reshape(C // part, part, Fd // free, free)
    return np.ascontiguousarray(t.transpose(2, 0, 1, 3))



class Sem:
    __slots__ = ("h", "v")

    def __init__(self, h):
        self.h = h
        self.v = 0

    def inc(self, n=1):
        self.v += n
        return self.v


def mm_split(t, out_ap, lhsT_ap, rhs_ap, n, start, stop, nmax=512):
    o = 0
    r = None
    while o < n:
        w = min(nmax, n - o)
        r = t.matmul(out_ap[:, o:o + w], lhsT_ap, rhs_ap[:, o:o + w],
                     start=start, stop=stop)
        o += w
    return r



def build_launch1():
    nc = bass.Bass("TRN2", target_bir_lowering=False)
    x_in = nc.dram_tensor("x", [NTOK, H], F32, kind="ExternalInput").ap()
    wqkv_in = nc.dram_tensor("wqkv_t", [24, 128, H], F32, kind="ExternalInput").ap()
    bqkv_in = nc.dram_tensor("bqkv", [1, 3072], F32, kind="ExternalInput").ap()
    qw_in = nc.dram_tensor("qw", [4, 4, 128, 128], I32, kind="ExternalInput").ap()
    lut4T_in = nc.dram_tensor("lut4T", [16, 512], F32, kind="ExternalInput").ap()
    mob_in = nc.dram_tensor("mob", [16, 16], F32, kind="ExternalInput").ap()
    ident_in = nc.dram_tensor("ident", [128, 128], BF16, kind="ExternalInput").ap()
    identf_in = nc.dram_tensor("identf", [128, 128], F32, kind="ExternalInput").ap()

    xnT_out = nc.dram_tensor("xnT", [32, 128, NTOK], BF16, kind="ExternalOutput").ap()
    qkvT_out = nc.dram_tensor("qkvT", [24, 128, NTOK], BF16, kind="ExternalOutput").ap()
    wlT_out = nc.dram_tensor("wlT", [32, 128, 512], BF16, kind="ExternalOutput").ap()

    from contextlib import ExitStack
    with ExitStack() as ctx:
        def sb(name, shape, dt):
            return ctx.enter_context(nc.sbuf_tensor(name, shape, dt))

        def psum(name, shape, dt=F32):
            return ctx.enter_context(nc.psum_tensor(name, shape, dt))

        def sem(name):
            return ctx.enter_context(nc.semaphore(name))

        xt = sb("xt", [128, 2 * H], F32)
        bn = sb("bn", [128, 52], F32)
        rstd = sb("rstd", [128, 1], F32)
        xnb = sb("xnb", [128, H], BF16)
        xnT = sb("xnTs", [128, 32 * NTOK], BF16)
        ident = sb("idents", [128, 128], BF16)
        identf = sb("identfs", [128, 128], F32)
        ones = sb("ones", [1, NTOK], BF16)
        consts = sb("consts", [128, 4], F32)
        wbuf = sb("wbuf", [128, 2 * 4096], BF16)
        bqb = sb("bqb", [1, 3072], BF16)
        qkvTs = sb("qkvTs", [128, 2 * NTOK], BF16)
        qws = sb("qws", [128, 16 * 128], I32)
        ibits = sb("ibits", [128, 2048], I32)
        bits = sb("bits", [128, 4 * 2048], BF16)
        lut4T = sb("lut4Ts", [16, 512], F32)
        mob = sb("mobs", [16, 16], F32)
        coef = sb("coef", [128, 64], F32)
        p01 = sb("p01", [128, 2048], F32)
        dq1 = sb("dq1", [128, 2048], F32)
        dq2 = sb("dq2", [128, 2048], F32)
        wacc = sb("wacc", [128, 2048], F32)
        wstage = sb("wstage", [128, 2048], BF16)
        ps_q0 = psum("ps_q0", [128, 1024], F32)
        ps_q1 = psum("ps_q1", [128, 1024], F32)
        ps_t0 = psum("ps_t0", [128, 1024], BF16)
        ps_t1 = psum("ps_t1", [128, 1024], BF16)
        ps_f0 = psum("ps_f0", [128, 512], F32)
        ps_f1 = psum("ps_f1", [128, 512], F32)
        h_ld = sem("ld")
        h_gw = sem("gw")
        h_mm = sem("mm")
        h_dv = sem("dv")
        h_ev = sem("ev")
        h_ac = sem("ac")
        h_st = sem("st")
        blk = ctx.enter_context(nc.Block())
        ld, gw, mm, dv, ev, ac, st = (Sem(h_ld), Sem(h_gw), Sem(h_mm), Sem(h_dv),
                                      Sem(h_ev), Sem(h_ac), Sem(h_st))
        pst = [ps_t0, ps_t1]
        pst_free = [0, 0]

        def c0(e, _dpld=ld.v):
            _cld = _dpld
            e.wait_ge(ld.h, _cld)
            e.dma_start(out=ident[:], in_=ident_in[:]).then_inc(ld.h, 16)
            _cld += 16
            e.wait_ge(ld.h, _cld)
            e.dma_start(out=identf[:], in_=identf_in[:]).then_inc(ld.h, 16)
            _cld += 16
            e.wait_ge(ld.h, _cld)
            e.dma_start(out=lut4T[:], in_=lut4T_in[:]).then_inc(ld.h, 16)
            _cld += 16
            e.wait_ge(ld.h, _cld)
            e.dma_start(out=mob[:], in_=mob_in[:]).then_inc(ld.h, 16)
            _cld += 16
        blk.sync(c0)
        LD_CONST = ld.inc(64)

        def v_ones(e):
            e.memset(ones[:], 1.0)
            e.memset(consts[:, 0:1], 1e-5)
            e.memset(consts[:, 1:2], 0.0).then_inc(dv.h, 1)
        blk.vector(v_ones)
        ONES_DV = dv.inc()

        def g0(e, _dpgw=gw.v):
            _cgw = _dpgw
            e.wait_ge(gw.h, _cgw)
            e.dma_start(out=bqb[:], in_=bqkv_in[:]).then_inc(gw.h, 16)
            _cgw += 16
            for ot in range(4):
                for pl in range(4):
                    e.wait_ge(gw.h, _cgw)
                    e.dma_start(out=qws[:, (ot * 4 + pl) * 128:(ot * 4 + pl + 1) * 128],
                                in_=qw_in[pl, ot, :, :]).then_inc(gw.h, 16)
                    _cgw += 16
        blk.gpsimd(g0)
        gw.inc(16 + 256)
        GW_BQ = GW_QW = gw.v

        ps_c_free = 0
        COEF_EV = 0
        for ot in range(4):
            def mobmm(e, ot=ot, fr=ps_c_free, need=LD_CONST):
                e.wait_ge(ld.h, need)
                if fr:
                    e.wait_ge(ev.h, fr)
                e.matmul(ps_f0[:, 0:16], lut4T[:, ot * 128:(ot + 1) * 128], mob[:],
                         start=True, stop=True).then_inc(mm.h, 1)
            t_mm = mm.inc()
            blk.tensor(mobmm)

            def mobev(e, ot=ot, t=t_mm):
                e.wait_ge(mm.h, t)
                e.tensor_copy(coef[:, ot * 16:(ot + 1) * 16], ps_f0[:, 0:16]) \
                    .then_inc(ev.h, 1)
            ps_c_free = COEF_EV = ev.inc()
            blk.vector(mobev)

        def ln_ld0(e, _dpld=ld.v):
            _cld = _dpld
            e.wait_ge(ld.h, _cld)
            e.dma_start(out=xt[:, 0:H], in_=x_in[0:128, :]).then_inc(ld.h, 16)
            _cld += 16
            e.wait_ge(ld.h, _cld)
            e.dma_start(out=xt[:, H:2 * H], in_=x_in[128:256, :]).then_inc(ld.h, 16)
            _cld += 16
        blk.sync(ln_ld0)
        x_loaded = [ld.inc(16), ld.inc(16)]
        xnb_free_mm = 0

        for i in range(NT):
            sl = (i % 2) * H

            def ln_stats(e, sl=sl, need=x_loaded[i], _sp=dv.v):
                e.wait_ge(ld.h, need)
                e.wait_ge(dv.h, _sp)
                xs = xt[:, sl:sl + H]
                r = None
                for k in range(8):
                    r = e.bn_stats(bn[:, 6 * k:6 * k + 6], xs[:, 512 * k:512 * (k + 1)])
                r.then_inc(dv.h, 1)
            t1 = dv.inc()
            blk.vector(ln_stats)

            def ln_aggr(e, t=t1):
                e.wait_ge(dv.h, t)
                e.bn_aggr(bn[:, 48:50], bn[:, 0:48]).then_inc(dv.h, 1)
            t2 = dv.inc()
            blk.vector(ln_aggr)

            def ln_rstd(e, t=t2, a0=ac.v):
                e.wait_ge(dv.h, t)
                e.activation(bn[:, 50:51], bn[:, 49:50], AF.Ln, bias=consts[:, 0:1], scale=1.0) \
                    .then_inc(ac.h, 1)
                e.wait_ge(ac.h, a0 + 1)
                e.activation(rstd[:], bn[:, 50:51], AF.Exp, bias=consts[:, 1:2], scale=-0.5) \
                    .then_inc(ac.h, 1)
            t3 = ac.inc(2)
            blk.scalar(ln_rstd)

            def ln_norm(e, sl=sl, t=t3, xf=xnb_free_mm):
                e.wait_ge(ac.h, t)
                if xf:
                    e.wait_ge(mm.h, xf)
                xs = xt[:, sl:sl + H]
                e.tensor_scalar(xnb[:], xs, bn[:, 48:49], rstd[:],
                                ALU.subtract, ALU.mult).then_inc(dv.h, 1)
            t4 = dv.inc()
            blk.vector(ln_norm)

            if i + 2 < NT:
                def ld_next(e, i=i, t=t4, _dpld=ld.v):
                    _cld = _dpld
                    e.wait_ge(dv.h, t)
                    e.wait_ge(ld.h, _cld)
                    e.dma_start(out=xt[:, (i % 2) * H:(i % 2) * H + H],
                                in_=x_in[(i + 2) * 128:(i + 3) * 128, :]) \
                        .then_inc(ld.h, 16)
                    _cld += 16
                blk.sync(ld_next)
                x_loaded.append(ld.inc(16))

            for cb in range(32):
                b = cb % 2

                def tns(e, cb=cb, b=b, fr=pst_free[b], t=t4):
                    e.wait_ge(dv.h, t)
                    if fr:
                        e.wait_ge(ev.h, fr)
                    e.transpose(pst[b][:, 0:128], xnb[:, cb * 128:(cb + 1) * 128], ident[:]) \
                        .then_inc(mm.h, 1)
                rd = mm.inc()
                blk.tensor(tns)

                def dve(e, rd=rd, b=b, i=i, cb=cb):
                    e.wait_ge(mm.h, rd)
                    e.tensor_copy(xnT[:, cb * NTOK + i * 128:cb * NTOK + (i + 1) * 128],
                                  pst[b][:, 0:128]).then_inc(ev.h, 1)
                pst_free[b] = ev.inc()
                blk.vector(dve)
            xnb_free_mm = mm.v
        XNT_EV = ev.v

        def xnT_store(e, t=XNT_EV, _dpst=st.v):
            _cst = _dpst
            e.wait_ge(ev.h, t)
            for cb in range(32):
                e.wait_ge(st.h, _cst)
                e.dma_start(out=xnT_out[cb, :, :],
                            in_=xnT[:, cb * NTOK:(cb + 1) * NTOK]).then_inc(st.h, 16)
                _cst += 16
        blk.sync(xnT_store)
        st.inc(16 * 32)

        psf = [ps_f0, ps_f1]
        psf_free = [COEF_EV, 0]
        wstage_free_st = 0
        qkv_mm_done = []
        qkv_ac_done = []
        qkv_st_done = []
        deq_slot = {8 + 2 * k: k for k in range(8)}

        def emit_qkv(ft):
            def wload(e, ft=ft, prev=(qkv_mm_done[ft - 2] if ft >= 2 else 0), _dpgw=gw.v):
                _cgw = _dpgw
                if prev:
                    e.wait_ge(mm.h, prev)
                e.wait_ge(gw.h, _cgw)
                e.dma_start(out=wbuf[:, (ft % 2) * 4096:(ft % 2 + 1) * 4096],
                            in_=wqkv_in[ft, :, :]).then_inc(gw.h, 16)
                _cgw += 16
            blk.gpsimd(wload)
            wv = gw.inc(16)
            pso = [ps_q0, ps_q1][ft % 2]

            def qmm(e, ft=ft, wv=wv, pso=pso, xv=XNT_EV, dvo=ONES_DV, acb=GW_BQ,
                    evprev=(qkv_ac_done[ft - 2] if ft >= 2 else 0)):
                e.wait_ge(gw.h, wv)
                e.wait_ge(ev.h, xv)
                e.wait_ge(dv.h, dvo)
                e.wait_ge(gw.h, acb)
                if evprev:
                    e.wait_ge(ac.h, evprev)
                e.matmul(pso[:, 0:512], bqb[:, ft * 128:(ft + 1) * 128],
                         ones[:, 0:512], start=True, stop=False)
                e.matmul(pso[:, 512:1024], bqb[:, ft * 128:(ft + 1) * 128],
                         ones[:, 512:1024], start=True, stop=False)
                for c in range(32):
                    lhsT = wbuf[:, (ft % 2) * 4096 + c * 128:(ft % 2) * 4096 + (c + 1) * 128]
                    rhs = xnT[:, c * NTOK:(c + 1) * NTOK]
                    r = mm_split(e, pso, lhsT, rhs, NTOK, start=False, stop=(c == 31))
                r.then_inc(mm.h, 1)
            qkv_mm_done.append(mm.inc())
            blk.tensor(qmm)

            def qev(e, ft=ft, pso=pso, t=qkv_mm_done[ft],
                    stp=(qkv_st_done[ft - 2] if ft >= 2 else 0)):
                e.wait_ge(mm.h, t)
                if stp:
                    e.wait_ge(st.h, stp)
                e.copy(qkvTs[:, (ft % 2) * NTOK:(ft % 2 + 1) * NTOK], pso[:]) \
                    .then_inc(ac.h, 1)
            qkv_ac_done.append(ac.inc())
            blk.scalar(qev)

            def qst(e, ft=ft, t=qkv_ac_done[ft], _dpst=st.v):
                _cst = _dpst
                e.wait_ge(ac.h, t)
                e.wait_ge(st.h, _cst)
                e.dma_start(out=qkvT_out[ft, :, :],
                            in_=qkvTs[:, (ft % 2) * NTOK:(ft % 2 + 1) * NTOK]) \
                    .then_inc(st.h, 16)
                _cst += 16
            blk.sync(qst)
            qkv_st_done.append(st.inc(16))

        def emit_deq_block(k):
            nonlocal wstage_free_st
            ot, ih = k // 2, k % 2

            if True:
                def extract(e, ot=ot, ih=ih, need=GW_QW, ce=COEF_EV, base=dv.v):
                    e.wait_ge(gw.h, need)
                    e.wait_ge(ev.h, ce)
                    t = base
                    e.wait_ge(dv.h, t)
                    for pl in range(4):
                        qv = qws[:, (ot * 4 + pl) * 128 + ih * 64:
                                 (ot * 4 + pl) * 128 + ih * 64 + 64]
                        r = None
                        for j in range(32):
                            r = e.tensor_scalar(
                                ibits.ap().rearrange("p (w j) -> p w j", j=32)[:, :, j],
                                qv, j, 1, ALU.logical_shift_right, ALU.bitwise_and)
                        r.then_inc(dv.h, 1)
                        t += 1
                        e.wait_ge(dv.h, t)
                        e.tensor_copy(bits[:, pl * 2048:(pl + 1) * 2048], ibits[:]) \
                            .then_inc(dv.h, 1)
                        t += 1
                        e.wait_ge(dv.h, t)
                dv.inc(8)
                blk.vector(extract)

                def multilin(e, ot=ot, base=dv.v):
                    t = base
                    e.wait_ge(dv.h, t)
                    bb = [bits[:, pl * 2048:(pl + 1) * 2048] for pl in range(4)]

                    def step(r):
                        nonlocal t
                        t += 1
                        r.then_inc(dv.h, 1)
                        e.wait_ge(dv.h, t)

                    def a(s):
                        return coef[:, ot * 16 + s:ot * 16 + s + 1]

                    def emit_G(out, T):
                        step(e.tensor_scalar(out, bb[0], a(T + 1), a(T),
                                             ALU.mult, ALU.add))
                        step(e.scalar_tensor_tensor(out, bb[1], a(T + 2), out,
                                                    ALU.mult, ALU.add))
                        step(e.scalar_tensor_tensor(out, p01[:], a(T + 3), out,
                                                    ALU.mult, ALU.add))

                    step(e.tensor_tensor(p01[:], bb[0], bb[1], ALU.mult))
                    emit_G(dq2[:], 12)
                    step(e.tensor_tensor(dq2[:], bb[2], dq2[:], ALU.mult))
                    emit_G(dq1[:], 8)
                    step(e.tensor_tensor(dq1[:], dq1[:], dq2[:], ALU.add))
                    step(e.tensor_tensor(dq1[:], bb[3], dq1[:], ALU.mult))
                    emit_G(dq2[:], 4)
                    step(e.tensor_tensor(dq2[:], bb[2], dq2[:], ALU.mult))
                    emit_G(wacc[:], 0)
                    step(e.tensor_tensor(wacc[:], wacc[:], dq2[:], ALU.add))
                    step(e.tensor_tensor(wacc[:], wacc[:], dq1[:], ALU.add))
                n_steps = 19
                blk.vector(multilin)
                ML_DONE = dv.inc(n_steps)

                for cb in range(16):
                    b = cb % 2

                    def tns(e, cb=cb, b=b, fr=psf_free[b], t=ML_DONE, ws=wstage_free_st):
                        e.wait_ge(dv.h, t)
                        if ws:
                            e.wait_ge(st.h, ws)
                        if fr:
                            e.wait_ge(ev.h, fr)
                        e.transpose(psf[b][:, 0:128], wacc[:, cb * 128:(cb + 1) * 128], identf[:]) \
                            .then_inc(mm.h, 1)
                    rd = mm.inc()
                    blk.tensor(tns)

                    def dve(e, rd=rd, b=b, cb=cb):
                        e.wait_ge(mm.h, rd)
                        e.tensor_copy(wstage[:, cb * 128:(cb + 1) * 128], psf[b][:, 0:128]) \
                            .then_inc(ev.h, 1)
                    psf_free[b] = ev.inc()
                    blk.vector(dve)
                WS_EV = ev.v

                def wl_store(e, ot=ot, ih=ih, t=WS_EV, _dpst=st.v):
                    _cst = _dpst
                    e.wait_ge(ev.h, t)
                    for cb in range(16):
                        ic = ih * 16 + cb
                        e.wait_ge(st.h, _cst)
                        e.dma_start(out=wlT_out[ic, :, ot * 128:(ot + 1) * 128],
                                    in_=wstage[:, cb * 128:(cb + 1) * 128]) \
                            .then_inc(st.h, 16)
                        _cst += 16
                blk.sync(wl_store)
                wstage_free_st = st.inc(16 * 16)

        for ft in range(24):
            emit_qkv(ft)
            if ft in deq_slot:
                emit_deq_block(deq_slot[ft])

        def fin(e, t=st.v):
            e.wait_ge(st.h, t)
        blk.gpsimd(fin)

    return nc




def build_launch2():
    nc = bass.Bass("TRN2", target_bir_lowering=False)
    x_in = nc.dram_tensor("x", [NTOK, H], F32, kind="ExternalInput").ap()
    xnT_in = nc.dram_tensor("xnT", [32, 128, NTOK], BF16, kind="ExternalInput").ap()
    qT_in = nc.dram_tensor("qT", [8, 128, NTOK], BF16, kind="ExternalInput").ap()
    kt_in = nc.dram_tensor("kt", [32, 128, 1024], BF16, kind="ExternalInput").ap()
    v_in = nc.dram_tensor("v", [128, 32768], BF16, kind="ExternalInput").ap()
    wl_in = nc.dram_tensor("wl", [8, 128, 16384], BF16, kind="ExternalInput").ap()
    wo_in = nc.dram_tensor("wo", [4, 8, 128, 1024], F32, kind="ExternalInput").ap()
    bo_in = nc.dram_tensor("bo", [1, H], F32, kind="ExternalInput").ap()
    bffn_in = nc.dram_tensor("bffn", [1, H], F32, kind="ExternalInput").ap()
    w1_in = nc.dram_tensor("w1", [48, 128, H], F32, kind="ExternalInput").ap()
    cos_in = nc.dram_tensor("cosT", [128, NTOK], F32, kind="ExternalInput").ap()
    sin_in = nc.dram_tensor("sinT", [128, NTOK], F32, kind="ExternalInput").ap()
    wr_in = nc.dram_tensor("wrT", [128, 256], F32, kind="ExternalInput").ap()
    at_in = nc.dram_tensor("at2", [128, H], F32, kind="ExternalInput").ap()
    brr_in = nc.dram_tensor("brr", [128, H], F32, kind="ExternalInput").ap()
    wb_in = nc.dram_tensor("wb", [8, 128, 16384], F32, kind="ExternalInput").ap()
    bb_in = nc.dram_tensor("bb", [1, H], F32, kind="ExternalInput").ap()
    sel_in = nc.dram_tensor("sel", [8, 128], F32, kind="ExternalInput").ap()
    ident_in = nc.dram_tensor("ident", [128, 128], BF16, kind="ExternalInput").ap()

    x1_out = nc.dram_tensor("x1", [NTOK, H], F32, kind="ExternalOutput").ap()
    g1_out = nc.dram_tensor("g1", [NTOK, H], BF16, kind="ExternalOutput").ap()
    q1T_out = nc.dram_tensor("q1T", [32, 128, NTOK], BF16, kind="ExternalOutput").ap()
    k1T_out = nc.dram_tensor("k1T", [8, 128, NTOK], BF16, kind="ExternalOutput").ap()
    v1T_out = nc.dram_tensor("v1T", [8, 128, NTOK], BF16, kind="ExternalOutput").ap()
    s0_dram = nc.dram_tensor("s0scr", [8, 4, 128, 1024], F32, kind="Internal").ap()

    from contextlib import ExitStack
    with ExitStack() as ctx:
        def sb(name, shape, dt):
            return ctx.enter_context(nc.sbuf_tensor(name, shape, dt))

        def psum(name, shape, dt=F32):
            return ctx.enter_context(nc.psum_tensor(name, shape, dt))

        AR0 = sb("AR0", [128, 32768], BF16)
        AR1 = sb("AR1", [128, 32768], BF16)
        AR2 = sb("AR2", [128, 8192], BF16)
        arena = sb("arena", [128, 8192], F32)
        scr_b = sb("scr_b", [128, 4096], BF16)
        bias = sb("bias", [1, H], BF16)
        rbuf = sb("rbuf", [128, NTOK], F32)
        onesf = sb("onesf", [1, 128], F32)
        ones = sb("ones2", [1, NTOK], BF16)
        ones_col = sb("ones_col", [128, 1], BF16)
        consts = sb("consts2", [128, 4], F32)
        bn = sb("bn2", [128, 104], F32)
        rstd = sb("rstd2", [128, 2], F32)
        wr2 = sb("wr2", [128, 256], BF16)
        sel2 = sb("sel2", [8, 128], BF16)
        rout = sb("rout", [128, 256], F32)
        routb = sb("routb", [128, 64], BF16)
        combT = sb("combT", [8, NTOK], BF16)
        ident = sb("ident2", [128, 128], BF16)
        ps_a = psum("ps2_a", [128, 1024], F32)
        ps_b = psum("ps2_b", [128, 1024], F32)
        ps_c1 = psum("ps2_c1", [128, 512], F32)
        ps_c2 = psum("ps2_c2", [128, 512], F32)
        ps_t0 = psum("ps2_t0", [128, 1024], BF16)
        ps_t1 = psum("ps2_t1", [128, 1024], BF16)
        xtile = arena[:, 0:4096]
        ktb = scr_b[:, 0:2048]
        cosb = rbuf[:]
        sinb = arena[:, 4096:5120]
        rvec = arena[0:1, 7168:8192]
        xnb = scr_b[:, 0:4096]
        qkb = scr_b[:, 0:2048]
        hbuf = scr_b[:, 2048:3072]
        cmat = scr_b[:, 3072:4096]
        at2 = AR2[:, 0:4096]
        brr2 = AR2[:, 4096:8192]
        ld = Sem(ctx.enter_context(nc.semaphore("ld2")))
        gw = Sem(ctx.enter_context(nc.semaphore("gw2")))
        mm = Sem(ctx.enter_context(nc.semaphore("mm2")))
        dv = Sem(ctx.enter_context(nc.semaphore("dv2")))
        ev = Sem(ctx.enter_context(nc.semaphore("ev2")))
        ac = Sem(ctx.enter_context(nc.semaphore("ac2")))
        st = Sem(ctx.enter_context(nc.semaphore("st2")))
        blk = ctx.enter_context(nc.Block())

        def c0(e, _dpld=ld.v):
            _cld = _dpld
            e.wait_ge(ld.h, _cld)
            e.dma_start(out=ident[:], in_=ident_in[:]).then_inc(ld.h, 16)
            _cld += 16
            e.wait_ge(ld.h, _cld)
            e.dma_start(out=AR0[:], in_=v_in[:]).then_inc(ld.h, 16)
            _cld += 16
            e.wait_ge(ld.h, _cld)
            e.dma_start(out=AR2[:].rearrange("p (dc n) -> p dc n", n=NTOK),
                        in_=qT_in[:].rearrange("dc p n -> p dc n")).then_inc(ld.h, 16)
            _cld += 16
        blk.sync(c0)
        LD_A = ld.inc(48)

        def g_c(e, _dpgw=gw.v):
            _cgw = _dpgw
            e.wait_ge(gw.h, _cgw)
            e.dma_start(out=bias[:], in_=bo_in[:]).then_inc(gw.h, 16)
            _cgw += 16
            e.wait_ge(gw.h, _cgw)
            e.dma_start(out=sel2[:], in_=sel_in[:]).then_inc(gw.h, 16)
            _cgw += 16
            e.wait_ge(gw.h, _cgw)
            e.dma_start(out=wr2[:], in_=wr_in[:]).then_inc(gw.h, 16)
            _cgw += 16
        blk.gpsimd(g_c)
        gw.inc(48)
        GW_BO = GW_EMISC = gw.v

        def v_c(e):
            e.memset(ones[:], 1.0)
            e.memset(ones_col[:], 1.0)
            e.memset(onesf[:], 1.0)
            e.memset(consts[:, 0:1], -SHIFT)
            e.memset(consts[:, 1:2], 1e-5)
            e.memset(consts[:, 2:3], 0.0).then_inc(dv.h, 1)
        blk.vector(v_c)
        DV_C = dv.inc()

        MM_A0 = mm.v
        AC_A0 = ac.v

        def a_exps(e, mm0=MM_A0, dvc=DV_C):
            e.wait_ge(dv.h, dvc)
            for kt in range(32):
                e.wait_ge(mm.h, mm0 + 2 * kt + 1)
                pso = [ps_a, ps_b][kt % 2]
                e.activation(AR1[:, kt * 1024:(kt + 1) * 1024], pso[:],
                             AF.Exp, bias=consts[:, 0:1], scale=1.0 / 32.0) \
                    .then_inc(ac.h, 1)
        blk.scalar(a_exps)
        exp_ac = [ac.inc() for _ in range(32)]
        assert exp_ac == [AC_A0 + kt + 1 for kt in range(32)]

        s_mm_done = []
        SUM_MM = 0
        for kt in range(32):
            def k_ld(e, kt=kt, prev=(s_mm_done[kt - 2] if kt >= 2 else 0), _dpld=ld.v):
                _cld = _dpld
                if prev:
                    e.wait_ge(mm.h, prev)
                e.wait_ge(ld.h, _cld)
                e.dma_start(out=ktb[:, (kt % 2) * 1024:(kt % 2 + 1) * 1024],
                            in_=kt_in[kt, :, :]).then_inc(ld.h, 16)
                _cld += 16
            blk.sync(k_ld)
            kv = ld.inc(16)

            pso = [ps_a, ps_b][kt % 2]

            def s_mm(e, kt=kt, kv=kv, need=LD_A, dvc=DV_C, pso=pso,
                     acprev=(exp_ac[kt - 2] if kt >= 2 else 0)):
                e.wait_ge(ld.h, max(kv, need))
                e.wait_ge(dv.h, dvc)
                if acprev:
                    e.wait_ge(ac.h, acprev)
                r = None
                for dc in range(8):
                    lhsT = ktb[:, (kt % 2) * 1024 + dc * 128:(kt % 2) * 1024 + (dc + 1) * 128]
                    rhs = AR2[:, dc * NTOK:(dc + 1) * NTOK]
                    r = mm_split(e, pso, lhsT, rhs, NTOK, start=(dc == 0), stop=(dc == 7))
                r.then_inc(mm.h, 1)
            s_mm_done.append(mm.inc())
            assert s_mm_done[kt] == MM_A0 + 2 * kt + 1
            blk.tensor(s_mm)

            def s_sum(e, kt=kt, t=exp_ac[kt]):
                e.wait_ge(ac.h, t)
                e.matmul(ps_c1[0:1, 0:512], ones_col[:],
                         AR1[:, kt * 1024:kt * 1024 + 512],
                         start=(kt == 0), stop=(kt == 31))
                e.matmul(ps_c2[0:1, 0:512], ones_col[:],
                         AR1[:, kt * 1024 + 512:(kt + 1) * 1024],
                         start=(kt == 0), stop=(kt == 31)).then_inc(mm.h, 1)
            SUM_MM = mm.inc()
            blk.tensor(s_sum)
        AEXP_AC = ac.v

        def recip(e, t=SUM_MM, base=dv.v):
            e.wait_ge(mm.h, t)
            e.wait_ge(dv.h, base)
            e.reciprocal(rvec[:, 0:512], ps_c1[0:1, 0:512]).then_inc(dv.h, 1)
            e.wait_ge(dv.h, base + 1)
            e.reciprocal(rvec[:, 512:1024], ps_c2[0:1, 0:512]).then_inc(dv.h, 1)
        RECIP_DV = dv.inc(2)
        blk.vector(recip)

        def rb_mm(e, t=RECIP_DV):
            e.wait_ge(dv.h, t)
            e.matmul(ps_b[:, 0:512], onesf[:], rvec[:, 0:512], start=True, stop=True)
            e.matmul(ps_b[:, 512:1024], onesf[:], rvec[:, 512:1024],
                     start=True, stop=True).then_inc(mm.h, 1)
        RB_MM = mm.inc()
        blk.tensor(rb_mm)

        def rb_ev(e, t=RB_MM):
            e.wait_ge(mm.h, t)
            e.tensor_copy(rbuf[:], ps_b[:]).then_inc(dv.h, 1)
        RB_DV = dv.inc()
        blk.vector(rb_ev)

        av_ev = []
        for dt in range(8):
            def av(e, dt=dt, t=AEXP_AC, rb=RB_DV,
                   evprev=(av_ev[dt - 1] if dt >= 1 else 0)):
                e.wait_ge(ac.h, t)
                e.wait_ge(dv.h, rb)
                if evprev:
                    e.wait_ge(dv.h, evprev)
                r = None
                for kt in range(32):
                    lhsT = AR0[:, (kt * 8 + dt) * 128:(kt * 8 + dt + 1) * 128]
                    rhs = AR1[:, kt * 1024:(kt + 1) * 1024]
                    r = mm_split(e, ps_b, lhsT, rhs, NTOK,
                                 start=(kt == 0), stop=(kt == 31))
                r.then_inc(mm.h, 1)
            t_av = mm.inc()
            blk.tensor(av)

            def av_e(e, dt=dt, t=t_av, rb=RB_DV):
                e.wait_ge(mm.h, t)
                e.wait_ge(dv.h, rb)
                e.tensor_tensor(AR2[:, dt * 1024:(dt + 1) * 1024], ps_b[:], rbuf[:],
                                ALU.mult).then_inc(dv.h, 1)
            av_ev.append(dv.inc())
            blk.vector(av_e)
        AV_DV = dv.v
        AV_MM = mm.v

        wo_ld = []
        wo_mm_done = []
        wo_ac_done = []
        wo_st_done = []
        for og in range(4):
            def wo_l(e, og=og, avt=AV_MM,
                     prev=(wo_mm_done[8 * (og - 2) + 7] if og >= 2 else 0),
                     _dpgw=gw.v):
                _cgw = _dpgw
                e.wait_ge(mm.h, max(avt, prev))
                for dc in range(8):
                    e.wait_ge(gw.h, _cgw)
                    e.dma_start(out=AR1[:, (og % 2) * 8192 + dc * 1024:
                                        (og % 2) * 8192 + (dc + 1) * 1024],
                                in_=wo_in[og, dc, :, :]).then_inc(gw.h, 16)
                    _cgw += 16
            blk.gpsimd(wo_l)
            wo_ld.append(gw.inc(128))

            for tok in range(8):
                pso = [ps_a, ps_b][tok % 2]

                def wo_mm(e, og=og, tok=tok, pso=pso, wv=wo_ld[og], avt=AV_DV,
                          bo=GW_BO,
                          acp=(wo_ac_done[-2] if len(wo_ac_done) >= 2 else 0)):
                    e.wait_ge(gw.h, max(wv, bo))
                    e.wait_ge(dv.h, avt)
                    if acp:
                        e.wait_ge(ac.h, acp)
                    e.matmul(pso[:, 0:512], ones[0:1, 0:128],
                             bias[0:1, og * 1024:og * 1024 + 512],
                             start=True, stop=False)
                    e.matmul(pso[:, 512:1024], ones[0:1, 0:128],
                             bias[0:1, og * 1024 + 512:(og + 1) * 1024],
                             start=True, stop=False)
                    r = None
                    for dc in range(8):
                        lhsT = AR2[:, dc * 1024 + tok * 128:dc * 1024 + (tok + 1) * 128]
                        rhs = AR1[:, (og % 2) * 8192 + dc * 1024:
                                  (og % 2) * 8192 + (dc + 1) * 1024]
                        r = mm_split(e, pso, lhsT, rhs, NTOK,
                                     start=False, stop=(dc == 7))
                    r.then_inc(mm.h, 1)
                wo_mm_done.append(mm.inc())
                blk.tensor(wo_mm)

                def wo_ev(e, og=og, tok=tok, pso=pso, t=wo_mm_done[-1],
                          stp=(wo_st_done[-2] if len(wo_st_done) >= 2 else 0)):
                    e.wait_ge(mm.h, t)
                    if stp:
                        e.wait_ge(st.h, stp)
                    e.copy(arena[:, (tok % 2) * 1024:(tok % 2 + 1) * 1024], pso[:]) \
                        .then_inc(ac.h, 1)
                wo_ac_done.append(ac.inc())
                blk.scalar(wo_ev)

                def wo_s(e, og=og, tok=tok, t=wo_ac_done[-1], _dpst=st.v):
                    _cst = _dpst
                    e.wait_ge(ac.h, t)
                    e.wait_ge(st.h, _cst)
                    e.dma_start(out=s0_dram[tok, og, :, :],
                                in_=arena[:, (tok % 2) * 1024:(tok % 2 + 1) * 1024]) \
                        .then_inc(st.h, 16)
                    _cst += 16
                blk.sync(wo_s)
                wo_st_done.append(st.inc(16))
        PA_ST = st.v
        PA_MM = mm.v
        PA_AC = ac.v

        def b_ld(e, t=PA_MM, _dpld=ld.v):
            _cld = _dpld
            e.wait_ge(mm.h, t)
            e.wait_ge(ld.h, _cld)
            e.dma_start(out=AR0[:].rearrange("p (cb n) -> p cb n", n=NTOK),
                        in_=xnT_in[:].rearrange("cb p n -> p cb n")).then_inc(ld.h, 16)
            _cld += 16
        blk.sync(b_ld)
        B_XNT = ld.inc(16)

        def b_bias(e, t=PA_AC, _dpgw=gw.v):
            _cgw = _dpgw
            e.wait_ge(ac.h, t)
            e.wait_ge(gw.h, _cgw)
            e.dma_start(out=bias[:], in_=bffn_in[:]).then_inc(gw.h, 16)
            _cgw += 16
        blk.gpsimd(b_bias)
        B_BIAS = gw.inc(16)

        wl_ld = []
        yl_mm_done = []
        yl_ac_done = []
        b_st_done = []
        b_gate_dvs = []
        for og in range(8):
            def wl_l(e, og=og, pa=(AV_MM if og == 0 else PA_MM),
                     prev=(yl_mm_done[8 * (og - 2) + 7] if og >= 2 else 0),
                     _dpgw=gw.v):
                _cgw = _dpgw
                e.wait_ge(mm.h, max(pa, prev))
                e.wait_ge(gw.h, _cgw)
                e.dma_start(out=AR1[:, ((og + 1) % 2) * 16384:
                                    ((og + 1) % 2 + 1) * 16384],
                            in_=wl_in[og, :, :]).then_inc(gw.h, 16)
                _cgw += 16
            blk.gpsimd(wl_l)
            wl_ld.append(gw.inc(16))

            for tok in range(8):
                pso = [ps_a, ps_b][tok % 2]
                aslot = ((og * 8 + tok) % 4) * 512

                def yl_mm(e, og=og, tok=tok, pso=pso, wv=wl_ld[og], xv=B_XNT,
                          bi=B_BIAS, pa=PA_AC,
                          acp=(yl_ac_done[-2] if len(yl_ac_done) >= 2 else 0)):
                    e.wait_ge(ld.h, xv)
                    e.wait_ge(gw.h, max(bi, wv))
                    e.wait_ge(ac.h, max(acp, pa))
                    e.matmul(pso[:, 0:512], ones[0:1, 0:128],
                             bias[0:1, og * 512:(og + 1) * 512],
                             start=True, stop=False)
                    r = None
                    for ic in range(32):
                        lhsT = AR0[:, ic * 1024 + tok * 128:ic * 1024 + (tok + 1) * 128]
                        rhs = AR1[:, ((og + 1) % 2) * 16384 + ic * 512:
                                  ((og + 1) % 2) * 16384 + (ic + 1) * 512]
                        r = e.matmul(pso[:, 0:512], lhsT, rhs,
                                     start=False, stop=(ic == 31))
                    r.then_inc(mm.h, 1)
                yl_mm_done.append(mm.inc())
                blk.tensor(yl_mm)

                def yl_ev(e, og=og, tok=tok, pso=pso, aslot=aslot,
                          t=yl_mm_done[-1], pa=PA_ST,
                          gp=(b_gate_dvs[-4] if len(b_gate_dvs) >= 4 else 0)):
                    e.wait_ge(mm.h, t)
                    e.wait_ge(st.h, pa)
                    if gp:
                        e.wait_ge(dv.h, gp)
                    e.activation(arena[:, aslot:aslot + 512],
                                 pso[:, 0:512], AF.Silu, bias=consts[:, 2:3],
                                 scale=1.0).then_inc(ac.h, 1)
                yl_ac_done.append(ac.inc())
                blk.scalar(yl_ev)

                def b_gld(e, og=og, tok=tok, aslot=aslot, sw=PA_ST,
                          gp=(b_gate_dvs[-4] if len(b_gate_dvs) >= 4 else 0),
                          _dpld=ld.v):
                    _cld = _dpld
                    e.wait_ge(st.h, sw)
                    if gp:
                        e.wait_ge(dv.h, gp)
                    e.wait_ge(ld.h, _cld)
                    e.dma_start(out=arena[:, 2048 + aslot:2048 + aslot + 512],
                                in_=s0_dram[tok, og // 2, :,
                                            (og % 2) * 512:(og % 2) * 512 + 512]) \
                        .then_inc(ld.h, 16)
                    _cld += 16
                    e.wait_ge(ld.h, _cld)
                    e.dma_start(out=arena[:, 4096 + aslot:4096 + aslot + 512],
                                in_=x_in[tok * 128:(tok + 1) * 128,
                                         og * 512:(og + 1) * 512]).then_inc(ld.h, 16)
                    _cld += 16
                blk.sync(b_gld)
                gv = ld.inc(32)

                def b_gate(e, og=og, tok=tok, aslot=aslot, t=yl_ac_done[-1],
                           gv=gv, base=dv.v,
                           xp=(b_st_done[-2] if len(b_st_done) >= 2 else 0)):
                    e.wait_ge(ac.h, t)
                    e.wait_ge(ld.h, gv)
                    if xp:
                        e.wait_ge(st.h, xp)
                    gsl = 6144 + (tok % 2) * 512
                    e.tensor_tensor(arena[:, gsl:gsl + 512],
                                    arena[:, 2048 + aslot:2048 + aslot + 512],
                                    arena[:, aslot:aslot + 512],
                                    ALU.mult).then_inc(dv.h, 1)
                    e.wait_ge(dv.h, base + 1)
                    e.tensor_tensor(arena[:, gsl:gsl + 512],
                                    arena[:, 4096 + aslot:4096 + aslot + 512],
                                    arena[:, gsl:gsl + 512], ALU.add) \
                        .then_inc(dv.h, 1)
                b_gate_dv = dv.inc(2)
                b_gate_dvs.append(b_gate_dv)
                blk.vector(b_gate)

                def b_x1s(e, og=og, tok=tok, t=b_gate_dv, _dpst=st.v):
                    _cst = _dpst
                    e.wait_ge(dv.h, t)
                    e.wait_ge(st.h, _cst)
                    gsl = 6144 + (tok % 2) * 512
                    e.dma_start(out=x1_out[tok * 128:(tok + 1) * 128,
                                           og * 512:(og + 1) * 512],
                                in_=arena[:, gsl:gsl + 512]).then_inc(st.h, 16)
                    _cst += 16
                blk.sync(b_x1s)
                b_st_done.append(st.inc(16))
        PB_ST = st.v
        PB_MM = mm.v

        pst = [ps_t0, ps_t1]
        pst_free = [0, 0]
        xt2 = [arena[:, 0:4096], arena[:, 4096:8192]]
        xnbs = [AR1[:, 0:4096], AR1[:, 4096:8192]]
        c_stats_dv = [0] * NT
        c_rstd_ac = [0] * NT
        c_norm_dv = [0] * NT
        c_tns_mm = [0] * NT

        def emit_tns_tile(i):
            for g in range(8):
                b = g % 2

                def tns4(e, i=i, g=g, b=b, fr=pst_free[b], t=c_norm_dv[i]):
                    e.wait_ge(dv.h, t)
                    if fr:
                        e.wait_ge(ev.h, fr)
                    r = None
                    for j in range(4):
                        cb = g * 4 + j
                        r = e.transpose(pst[b][:, j * 128:(j + 1) * 128],
                                        xnbs[i % 2][:, cb * 128:(cb + 1) * 128],
                                        ident[:])
                    r.then_inc(mm.h, 1)
                rd = mm.inc()
                blk.tensor(tns4)

                def dve4(e, rd=rd, b=b, i=i, g=g):
                    e.wait_ge(mm.h, rd)
                    e.tensor_copy(
                        AR0[:].rearrange("p (cb n) -> p cb n", n=NTOK)
                        [:, g * 4:(g + 1) * 4, i * 128:(i + 1) * 128],
                        pst[b][:, 0:512].rearrange("p (j n) -> p j n", n=128)) \
                        .then_inc(ev.h, 1)
                pst_free[b] = ev.inc()
                blk.vector(dve4)
            c_tns_mm[i] = mm.v

        for i in range(NT):
            def c_l(e, i=i, t=PB_ST,
                    pn=(c_norm_dv[i - 2] if i >= 2 else 0), _dpld=ld.v):
                _cld = _dpld
                e.wait_ge(st.h, t)
                if pn:
                    e.wait_ge(dv.h, pn)
                e.wait_ge(ld.h, _cld)
                e.dma_start(out=xt2[i % 2], in_=x1_out[i * 128:(i + 1) * 128, :]) \
                    .then_inc(ld.h, 16)
                _cld += 16
            blk.sync(c_l)
            xl = ld.inc(16)

            def c_stats(e, i=i, xl=xl,
                        ap=(c_rstd_ac[i - 2] if i >= 2 else 0)):
                e.wait_ge(ld.h, xl)
                if ap:
                    e.wait_ge(ac.h, ap)
                bns = bn[:, (i % 2) * 52:(i % 2) * 52 + 52]
                r = None
                for k in range(8):
                    r = e.bn_stats(bns[:, 6 * k:6 * k + 6],
                                   xt2[i % 2][:, 512 * k:512 * (k + 1)])
                r.then_inc(dv.h, 1)
            c_stats_dv[i] = dv.inc()
            blk.vector(c_stats)

            def c_aggr(e, i=i, t=c_stats_dv[i]):
                e.wait_ge(dv.h, t)
                bns = bn[:, (i % 2) * 52:(i % 2) * 52 + 52]
                e.bn_aggr(bns[:, 48:50], bns[:, 0:48]).then_inc(dv.h, 1)
            t2 = dv.inc()
            blk.vector(c_aggr)

            def c_rstd(e, i=i, t=t2, a0=ac.v):
                e.wait_ge(dv.h, t)
                bns = bn[:, (i % 2) * 52:(i % 2) * 52 + 52]
                e.activation(bns[:, 50:51], bns[:, 49:50], AF.Ln,
                             bias=consts[:, 1:2], scale=1.0).then_inc(ac.h, 1)
                e.wait_ge(ac.h, a0 + 1)
                e.activation(rstd[:, (i % 2):(i % 2) + 1], bns[:, 50:51], AF.Exp,
                             bias=consts[:, 2:3], scale=-0.5).then_inc(ac.h, 1)
            c_rstd_ac[i] = ac.inc(2)
            blk.scalar(c_rstd)

            def c_norm(e, i=i, t=c_rstd_ac[i],
                       pm=(c_tns_mm[i - 2] if i >= 2 else PB_MM)):
                e.wait_ge(ac.h, t)
                e.wait_ge(mm.h, pm)
                bns = bn[:, (i % 2) * 52:(i % 2) * 52 + 52]
                e.tensor_scalar(xnbs[i % 2], xt2[i % 2], bns[:, 48:49],
                                rstd[:, (i % 2):(i % 2) + 1],
                                ALU.subtract, ALU.mult).then_inc(dv.h, 1)
            c_norm_dv[i] = dv.inc()
            blk.vector(c_norm)

            if i >= 1:
                emit_tns_tile(i - 1)
        emit_tns_tile(NT - 1)
        PC_EV = ev.v
        PC_MM = mm.v
        C_NORM_LAST = c_norm_dv[NT - 1]

        def d_cs(e, t=PC_MM, cn=C_NORM_LAST, _dpld=ld.v):
            _cld = _dpld
            e.wait_ge(mm.h, t)
            e.wait_ge(dv.h, cn)
            e.wait_ge(ld.h, _cld)
            e.dma_start(out=cosb, in_=cos_in[:]).then_inc(ld.h, 16)
            _cld += 16
            e.wait_ge(ld.h, _cld)
            e.dma_start(out=sinb, in_=sin_in[:]).then_inc(ld.h, 16)
            _cld += 16
        blk.sync(d_cs)
        D_CS = ld.inc(32)
        d_mm_done = []
        d_ac_done = []
        d_dv_done = []
        qkb_st = []
        for ft in range(48):
            def d_wl(e, ft=ft, prev=(d_mm_done[ft - 2] if ft >= 2 else 0),
                     pa=PB_MM, _dpgw=gw.v):
                _cgw = _dpgw
                e.wait_ge(mm.h, max(prev, pa))
                e.wait_ge(gw.h, _cgw)
                e.dma_start(out=AR2[:, (ft % 2) * 4096:(ft % 2 + 1) * 4096],
                            in_=w1_in[ft, :, :]).then_inc(gw.h, 16)
                _cgw += 16
            blk.gpsimd(d_wl)
            wv = gw.inc(16)

            pso = [ps_a, ps_b][ft % 2]

            def d_mm(e, ft=ft, wv=wv, xv=PC_EV, pso=pso,
                     acp=(d_ac_done[ft - 2] if ft >= 2 else 0)):
                e.wait_ge(gw.h, wv)
                e.wait_ge(ev.h, xv)
                if acp:
                    e.wait_ge(ac.h, acp)
                r = None
                for ic in range(32):
                    lhsT = AR2[:, (ft % 2) * 4096 + ic * 128:(ft % 2) * 4096 + (ic + 1) * 128]
                    rhs = AR0[:, ic * NTOK:(ic + 1) * NTOK]
                    r = mm_split(e, pso, lhsT, rhs, NTOK, start=(ic == 0), stop=(ic == 31))
                r.then_inc(mm.h, 1)
            d_mm_done.append(mm.inc())
            blk.tensor(d_mm)

            if ft < 40:
                def d_ev(e, ft=ft, t=d_mm_done[-1], pso=pso,
                         dvp=(d_dv_done[ft - 2] if ft >= 2 else C_NORM_LAST)):
                    e.wait_ge(mm.h, t)
                    if dvp:
                        e.wait_ge(dv.h, dvp)
                    e.copy(arena[:, (ft % 2) * 1024:(ft % 2 + 1) * 1024], pso[:]) \
                        .then_inc(ac.h, 1)
                d_ac_done.append(ac.inc())
                blk.scalar(d_ev)

                def d_rope(e, ft=ft, t=d_ac_done[-1], base=dv.v, cs=D_CS,
                           qp=(qkb_st[ft - 2] if ft >= 2 else 0)):
                    e.wait_ge(ac.h, t)
                    e.wait_ge(ld.h, cs)
                    e.wait_ge(dv.h, base)
                    if qp:
                        e.wait_ge(st.h, qp)
                    sl = (ft % 2) * 1024
                    qk = arena[:, sl:sl + 1024]
                    rt = arena[:, 2048:3072]
                    t1 = arena[:, 3072:4096]
                    tt = base
                    e.tensor_scalar(rt[0:64, :], qk[64:128, :], -1.0, None,
                                    ALU.mult, ALU.bypass).then_inc(dv.h, 1)
                    tt += 1
                    e.wait_ge(dv.h, tt)
                    e.tensor_copy(rt[64:128, :], qk[0:64, :]).then_inc(dv.h, 1)
                    tt += 1
                    e.wait_ge(dv.h, tt)
                    e.tensor_tensor(t1, qk, cosb, ALU.mult).then_inc(dv.h, 1)
                    tt += 1
                    e.wait_ge(dv.h, tt)
                    e.tensor_tensor(rt, rt, sinb, ALU.mult).then_inc(dv.h, 1)
                    tt += 1
                    e.wait_ge(dv.h, tt)
                    e.tensor_tensor(qkb[:, (ft % 2) * 1024:(ft % 2 + 1) * 1024],
                                    t1, rt, ALU.add).then_inc(dv.h, 1)
                d_dv_done.append(dv.inc(5))
                blk.vector(d_rope)

                def d_st(e, ft=ft, t=d_dv_done[-1], _dpst=st.v):
                    _cst = _dpst
                    e.wait_ge(dv.h, t)
                    if ft < 32:
                        e.wait_ge(st.h, _cst)
                        e.dma_start(out=q1T_out[ft, :, :],
                                    in_=qkb[:, (ft % 2) * 1024:(ft % 2 + 1) * 1024]) \
                            .then_inc(st.h, 16)
                        _cst += 16
                    else:
                        e.wait_ge(st.h, _cst)
                        e.dma_start(out=k1T_out[ft - 32, :, :],
                                    in_=qkb[:, (ft % 2) * 1024:(ft % 2 + 1) * 1024]) \
                            .then_inc(st.h, 16)
                        _cst += 16
                blk.sync(d_st)
                qkb_st.append(st.inc(16))
            else:
                def d_evv(e, ft=ft, t=d_mm_done[-1], pso=pso,
                          qp=(qkb_st[ft - 2] if ft >= 2 else 0)):
                    e.wait_ge(mm.h, t)
                    if qp:
                        e.wait_ge(st.h, qp)
                    e.copy(qkb[:, (ft % 2) * 1024:(ft % 2 + 1) * 1024], pso[:]) \
                        .then_inc(ac.h, 1)
                d_ac_done.append(ac.inc())
                blk.scalar(d_evv)

                def d_stv(e, ft=ft, t=d_ac_done[-1], _dpst=st.v):
                    _cst = _dpst
                    e.wait_ge(ac.h, t)
                    e.wait_ge(st.h, _cst)
                    e.dma_start(out=v1T_out[ft - 40, :, :],
                                in_=qkb[:, (ft % 2) * 1024:(ft % 2 + 1) * 1024]) \
                        .then_inc(st.h, 16)
                    _cst += 16
                blk.sync(d_stv)
                qkb_st.append(st.inc(16))
        PD_MM = mm.v
        PD_AC = ac.v
        PD_DV = dv.v
        PD_ST = st.v

        def e_bias(e, t=PD_MM, _dpgw=gw.v):
            _cgw = _dpgw
            e.wait_ge(mm.h, t)
            e.wait_ge(gw.h, _cgw)
            e.dma_start(out=bias[:], in_=bb_in[:]).then_inc(gw.h, 16)
            _cgw += 16
            e.wait_ge(gw.h, _cgw)
            e.dma_start(out=at2, in_=at_in[:]).then_inc(gw.h, 16)
            _cgw += 16
            e.wait_ge(gw.h, _cgw)
            e.dma_start(out=brr2, in_=brr_in[:]).then_inc(gw.h, 16)
            _cgw += 16
        blk.gpsimd(e_bias)
        gw.inc(48)
        E_BIAS = E_AT = gw.v
        comb_dv = []
        combT_ev = []
        for tok in range(8):
            def e_rmm(e, tok=tok, t=PD_AC, xv=PC_EV, wv=GW_EMISC):
                e.wait_ge(ac.h, t)
                e.wait_ge(ev.h, xv)
                e.wait_ge(gw.h, wv)
                r = None
                for ic in range(32):
                    lhsT = AR0[:, ic * 1024 + tok * 128:ic * 1024 + (tok + 1) * 128]
                    rhs = wr2[:, ic * 8:(ic + 1) * 8]
                    r = e.matmul(ps_a[:, tok * 8:(tok + 1) * 8], lhsT, rhs,
                                 start=(ic == 0), stop=(ic == 31))
                r.then_inc(mm.h, 1)
            t_rmm = mm.inc()
            blk.tensor(e_rmm)

            ro = tok * 32

            def e_rmax(e, tok=tok, ro=ro, t=t_rmm, base=dv.v):
                e.wait_ge(mm.h, t)
                e.wait_ge(dv.h, base)
                e.reduce_max(rout[:, ro + 24:ro + 25], ps_a[:, tok * 8:(tok + 1) * 8],
                             axis=mybir.AxisListType.X).then_inc(dv.h, 1)
                e.wait_ge(dv.h, base + 1)
                e.tensor_scalar(rout[:, ro + 25:ro + 26], rout[:, ro + 24:ro + 25],
                                -1.0, None, ALU.mult, ALU.bypass).then_inc(dv.h, 1)
            t_rmax = dv.inc(2)
            blk.vector(e_rmax)

            def e_rexp(e, tok=tok, ro=ro, t=t_rmax):
                e.wait_ge(dv.h, t)
                e.activation(rout[:, ro:ro + 8], ps_a[:, tok * 8:(tok + 1) * 8],
                             AF.Exp, bias=rout[:, ro + 25:ro + 26], scale=1.0) \
                    .then_inc(ac.h, 1)
            t_rexp = ac.inc()
            blk.scalar(e_rexp)

            def e_rsel(e, tok=tok, ro=ro, t=t_rexp, base=dv.v):
                e.wait_ge(ac.h, t)
                tt = base
                e.wait_ge(dv.h, tt)
                el = rout[:, ro:ro + 8]

                def step(r):
                    nonlocal tt
                    tt += 1
                    r.then_inc(dv.h, 1)
                    e.wait_ge(dv.h, tt)

                step(e.tensor_scalar(rout[:, ro + 8:ro + 16], el, 1.0, None,
                                     ALU.is_ge, ALU.bypass))
                step(e.scalar_tensor_tensor(rout[:, ro + 16:ro + 24],
                                            rout[:, ro + 8:ro + 16], -2.0,
                                            el, ALU.mult, ALU.add))
                step(e.reduce_max(rout[:, ro + 26:ro + 27], rout[:, ro + 16:ro + 24],
                                  axis=mybir.AxisListType.X))
                step(e.tensor_scalar(rout[:, ro + 8:ro + 16], el,
                                     rout[:, ro + 26:ro + 27], None,
                                     ALU.is_ge, ALU.bypass))
                step(e.tensor_tensor(rout[:, ro + 16:ro + 24], el,
                                     rout[:, ro + 8:ro + 16], ALU.mult))
                step(e.tensor_scalar(rout[:, ro + 27:ro + 28],
                                     rout[:, ro + 26:ro + 27], 1.0, None,
                                     ALU.add, ALU.bypass))
                step(e.reciprocal(rout[:, ro + 28:ro + 29], rout[:, ro + 27:ro + 28]))
                step(e.tensor_scalar(routb[:, tok * 8:(tok + 1) * 8],
                                     rout[:, ro + 16:ro + 24],
                                     rout[:, ro + 28:ro + 29], None,
                                     ALU.mult, ALU.bypass))
            comb_dv.append(dv.inc(8))
            blk.vector(e_rsel)

            def e_ctr(e, tok=tok, t=comb_dv[-1], pc=PC_EV,
                      fr=(combT_ev[tok - 1] if tok >= 1 else 0)):
                e.wait_ge(dv.h, t)
                e.wait_ge(ev.h, max(pc, fr))
                e.transpose(ps_t0[0:8, 0:128], routb[:, tok * 8:(tok + 1) * 8],
                            ident[:]).then_inc(mm.h, 1)
            t_ctr = mm.inc()
            blk.tensor(e_ctr)

            def e_cev(e, tok=tok, t=t_ctr):
                e.wait_ge(mm.h, t)
                e.tensor_copy(combT[:, tok * 128:(tok + 1) * 128], ps_t0[0:8, 0:128]) \
                    .then_inc(ev.h, 1)
            combT_ev.append(ev.inc())
            blk.vector(e_cev)
        E_COMBT = ev.v

        def e_cmm(e, t=E_COMBT, wv=GW_EMISC):
            e.wait_ge(ev.h, t)
            e.wait_ge(gw.h, wv)
            r = mm_split(e, ps_b, sel2[:], combT[:], NTOK, start=True, stop=True)
            r.then_inc(mm.h, 1)
        t_cmm = mm.inc()
        blk.tensor(e_cmm)

        def e_cmev(e, t=t_cmm):
            e.wait_ge(mm.h, t)
            e.tensor_copy(cmat, ps_b[:]).then_inc(dv.h, 1)
        E_CMAT = dv.inc()
        blk.vector(e_cmev)

        def e_hmm(e, t=E_CMAT, wv=E_AT, xv=PC_EV):
            e.wait_ge(dv.h, t)
            e.wait_ge(gw.h, wv)
            e.wait_ge(ev.h, xv)
            r = None
            for ic in range(32):
                lhsT = at2[:, ic * 128:(ic + 1) * 128]
                rhs = AR0[:, ic * NTOK:(ic + 1) * NTOK]
                r = mm_split(e, ps_a, lhsT, rhs, NTOK, start=(ic == 0), stop=(ic == 31))
            r.then_inc(mm.h, 1)
        t_hmm = mm.inc()
        blk.tensor(e_hmm)

        def e_hev(e, t=t_hmm):
            e.wait_ge(mm.h, t)
            e.tensor_tensor(hbuf, ps_a[:], cmat, ALU.mult).then_inc(dv.h, 1)
        E_HBUF = dv.inc()
        blk.vector(e_hev)


        e_wld = []
        e_mm_done = []
        e_ac_done = []
        g1_st = []
        for og in range(8):
            def e_wl(e, og=og, pd=PD_MM,
                     prev=(e_mm_done[8 * (og - 2) + 7] if og >= 2 else 0),
                     _dpgw=gw.v):
                _cgw = _dpgw
                e.wait_ge(mm.h, max(pd, prev))
                e.wait_ge(gw.h, _cgw)
                e.dma_start(out=AR1[:, (og % 2) * 16384:(og % 2 + 1) * 16384],
                            in_=wb_in[og, :, :]).then_inc(gw.h, 16)
                _cgw += 16
            blk.gpsimd(e_wl)
            e_wld.append(gw.inc(16))

            for tok in range(8):
                pso = [ps_b, ps_a][tok % 2]
                qslot = ((og * 8 + tok) % 4) * 512

                def eb_mm(e, og=og, tok=tok, pso=pso, wv=max(e_wld[og], E_AT),
                          hb=E_HBUF, bi=E_BIAS,
                          acp=(e_ac_done[-2] if len(e_ac_done) >= 2 else 0)):
                    e.wait_ge(gw.h, max(wv, bi))
                    e.wait_ge(dv.h, hb)
                    if acp:
                        e.wait_ge(ac.h, acp)
                    e.matmul(pso[:, 0:512], ones[0:1, 0:128],
                             bias[0:1, og * 512:(og + 1) * 512],
                             start=True, stop=False)
                    lhsT = hbuf[:, tok * 128:(tok + 1) * 128]
                    e.matmul(pso[:, 0:512], lhsT,
                             brr2[:, og * 512:(og + 1) * 512],
                             start=False, stop=False)
                    r = None
                    for ic in range(32):
                        lh = AR0[:, ic * 1024 + tok * 128:ic * 1024 + (tok + 1) * 128]
                        rhs = AR1[:, (og % 2) * 16384 + ic * 512:
                                  (og % 2) * 16384 + (ic + 1) * 512]
                        r = e.matmul(pso[:, 0:512], lh, rhs,
                                     start=False, stop=(ic == 31))
                    r.then_inc(mm.h, 1)
                e_mm_done.append(mm.inc())
                blk.tensor(eb_mm)

                def eb_ev(e, og=og, tok=tok, pso=pso, qslot=qslot,
                          t=e_mm_done[-1], pdst=PD_ST,
                          gp=(g1_st[-4] if len(g1_st) >= 4 else 0)):
                    e.wait_ge(mm.h, t)
                    e.wait_ge(st.h, max(pdst, gp))
                    e.activation(qkb[:, qslot:qslot + 512],
                                 pso[:, 0:512], AF.Silu, bias=consts[:, 2:3],
                                 scale=1.0).then_inc(ac.h, 1)
                e_ac_done.append(ac.inc())
                blk.scalar(eb_ev)

                def eb_st(e, og=og, tok=tok, qslot=qslot, t=e_ac_done[-1],
                          _dpst=st.v):
                    _cst = _dpst
                    e.wait_ge(ac.h, t)
                    e.wait_ge(st.h, _cst)
                    e.dma_start(out=g1_out[tok * 128:(tok + 1) * 128,
                                           og * 512:(og + 1) * 512],
                                in_=qkb[:, qslot:qslot + 512]).then_inc(st.h, 16)
                    _cst += 16
                blk.sync(eb_st)
                g1_st.append(st.inc(16))

        def fin(e, t=st.v):
            e.wait_ge(st.h, t)
        blk.gpsimd(fin)

    return nc



_CACHE = {}


def _prep_launch1(inputs):
    Wqkv = np.asarray(inputs["L0_attn_Wqkv"], np.float32)
    wqkv_t = np.ascontiguousarray(
        Wqkv.reshape(32, 128, 24, 128).transpose(2, 1, 0, 3).reshape(24, 128, H))
    bqkv = np.ascontiguousarray(
        np.asarray(inputs["L0_attn_bqkv"], np.float32).reshape(1, 3072))
    qw = np.asarray(inputs["L0_ffn_qweight"])
    lut4 = np.asarray(inputs["L0_ffn_lut4"], np.float32)
    mob = mobius_matrix()
    ident = np.eye(128, dtype=BF)
    x = np.asarray(inputs["L0_attn_x"], np.float32).reshape(B * S, H)

    in_maps = []
    for c in range(NCORE := 8):
        rows = core_rows(c)
        o0 = 512 * c
        in_maps.append({
            "x": np.ascontiguousarray(x[rows]),
            "wqkv_t": wqkv_t,
            "bqkv": bqkv,
            "qw": np.ascontiguousarray(
                qw[0:4, o0:o0 + 512, :].reshape(4, 4, 128, 128)),
            "lut4T": np.ascontiguousarray(lut4[o0:o0 + 512, :].T),
            "mob": mob,
            "ident": ident,
            "identf": np.eye(128, dtype=np.float32),
        })
    return in_maps




def build_launch3():
    nc = bass.Bass("TRN2", target_bir_lowering=False)
    x1_in = nc.dram_tensor("x1", [NTOK, H], F32, kind="ExternalInput").ap()
    g1_in = nc.dram_tensor("g1", [NTOK, H], BF16, kind="ExternalInput").ap()
    q1T_in = nc.dram_tensor("q1T", [32, 128, NTOK], BF16, kind="ExternalInput").ap()
    k1_in = nc.dram_tensor("k1", [8, 128, S], BF16, kind="ExternalInput").ap()
    v1_in = nc.dram_tensor("v1", [128, 32768], BF16, kind="ExternalInput").ap()
    wo1_in = nc.dram_tensor("wo1", [8, 32, 128, 512], F32, kind="ExternalInput").ap()
    mask_in = nc.dram_tensor("mask", [128, 4096], BF16, kind="ExternalInput").ap()

    out_out = nc.dram_tensor("out", [NTOK, H], F32, kind="ExternalOutput").ap()

    SC1 = 0.08838834764831843

    from contextlib import ExitStack
    with ExitStack() as ctx:
        def sb(name, shape, dt):
            return ctx.enter_context(nc.sbuf_tensor(name, shape, dt))

        def psum(name, shape, dt=F32):
            return ctx.enter_context(nc.psum_tensor(name, shape, dt))

        AR0 = sb("V1", [128, 32768], BF16)
        AR1 = sb("OT", [128, 32768], BF16)
        k1b = sb("k1b", [128, 2 * 4096], BF16)
        q1h = sb("q1h", [128, 2 * 1024], BF16)
        exps = sb("exps", [128, 4 * 1024], BF16)
        maskb = sb("maskb", [128, 4096], BF16)
        rvec = sb("rvec3", [1, NTOK], F32)
        rrow = sb("rrow3", [1, NTOK], F32)
        onesf = sb("onesf3", [1, 128], F32)
        ones_col = sb("ones_col3", [128, 1], BF16)
        consts = sb("consts3", [128, 2], F32)
        x1t = sb("x1t", [128, 2 * 1024], F32)
        g1t = sb("g1t", [128, 2 * 1024], BF16)
        ot1 = sb("ot1", [128, 2 * 1024], F32)
        ps_s0 = psum("ps3_s0", [128, 1024], F32)
        ps_s1 = psum("ps3_s1", [128, 1024], F32)
        ps_o = psum("ps3_o", [128, 1024], F32)
        ps_r1 = psum("ps3_r1", [128, 512], F32)
        ps_r2 = psum("ps3_r2", [128, 512], F32)
        ld = Sem(ctx.enter_context(nc.semaphore("ld3")))
        gw = Sem(ctx.enter_context(nc.semaphore("gw3")))
        mm = Sem(ctx.enter_context(nc.semaphore("mm3")))
        dv = Sem(ctx.enter_context(nc.semaphore("dv3")))
        ac = Sem(ctx.enter_context(nc.semaphore("ac3")))
        st = Sem(ctx.enter_context(nc.semaphore("st3")))
        blk = ctx.enter_context(nc.Block())
        pss = [ps_s0, ps_s1]

        def c0(e, _dpld=ld.v):
            _cld = _dpld
            e.wait_ge(ld.h, _cld)
            e.dma_start(out=AR0[:], in_=v1_in[:]).then_inc(ld.h, 16)
            _cld += 16
            e.wait_ge(ld.h, _cld)
            e.dma_start(out=maskb[:], in_=mask_in[:]).then_inc(ld.h, 16)
            _cld += 16
        blk.sync(c0)
        LD_C = ld.inc(32)

        def v_c(e):
            e.memset(ones_col[:], 1.0)
            e.memset(onesf[:], 1.0)
            e.memset(consts[:, 0:1], -SHIFT)
            e.memset(consts[:, 1:2], 0.0).then_inc(dv.h, 1)
        blk.vector(v_c)
        DV_C = dv.inc()

        def nq_of(ka):
            return (8 - ka // 4) * 128

        NP = 16

        def pe_ops_of_head(h):
            ops = [("smm", 0)]
            if h > 0:
                ops += [("hrbP", -1)]
            for p in range(1, NP):
                ops += [("smm", p), ("sum", p - 1)]
            ops += [("sum", NP - 1)]
            return ops

        plan = {}
        p_mm, p_ac, p_dv = mm.v, ac.v, dv.v
        for h in range(NH):
            for op, p in pe_ops_of_head(h):
                p_mm += 1
                if op == "hrbP":
                    plan[("hrb", h - 1)] = p_mm
                else:
                    plan[(op, h, p)] = p_mm
            for p in range(NP):
                p_ac += 1
                plan[("exp", h, p)] = p_ac
            plan[("svexp", h)] = p_ac + 3
            plan[("otev", h)] = p_ac + 4
            p_ac += 4
            for p in range(NP):
                p_dv += 1
                plan[("mask", h, p)] = p_dv
            p_dv += 1
            plan[("omult", h)] = p_dv
        plan[("hrb", NH - 1)] = p_mm + 1

        k_ld = [0] * 8
        q_ld = [0] * 32
        for h in range(NH):
            kv = h // 4
            if h % 4 == 0:
                def kld(e, kv=kv,
                        prev=(plan[("smm", 4 * (kv - 2) + 3, NP - 1)]
                              if kv >= 2 else 0), _dpld=ld.v):
                    _cld = _dpld
                    if prev:
                        e.wait_ge(mm.h, prev)
                    e.wait_ge(ld.h, _cld)
                    e.dma_start(out=k1b[:, (kv % 2) * 4096:(kv % 2 + 1) * 4096],
                                in_=k1_in[kv, :, :]).then_inc(ld.h, 16)
                    _cld += 16
                blk.sync(kld)
                k_ld[kv] = ld.inc(16)

            def qld(e, h=h,
                    prev=(plan[("smm", h - 2, NP - 1)] if h >= 2 else 0),
                    _dpgw=gw.v):
                _cgw = _dpgw
                if prev:
                    e.wait_ge(mm.h, prev)
                e.wait_ge(gw.h, _cgw)
                e.dma_start(out=q1h[:, (h % 2) * 1024:(h % 2 + 1) * 1024],
                            in_=q1T_in[h, :, :]).then_inc(gw.h, 16)
                _cgw += 16
            blk.gpsimd(qld)
            q_ld[h] = gw.inc(16)

            def pe_head(e, h=h, kv=kv, kl=k_ld[kv], ql=q_ld[h]):
                e.wait_ge(ld.h, kl)
                e.wait_ge(gw.h, ql)
                for op, p in pe_ops_of_head(h):
                    if op == "smm":
                        if p >= 1:
                            e.wait_ge(ac.h, plan[("exp", h, p - 1)])
                        elif h > 0:
                            e.wait_ge(ac.h, plan[("exp", h - 1, NP - 1)])
                        for ka in (2 * p, 2 * p + 1):
                            nq = nq_of(ka)
                            lhsT = k1b[:, (kv % 2) * 4096 + ka * 128:
                                       (kv % 2) * 4096 + (ka + 1) * 128]
                            rhs = q1h[:, (h % 2) * 1024:(h % 2) * 1024 + nq]
                            r = mm_split(e, pss[ka % 2], lhsT, rhs, nq,
                                         start=True, stop=True)
                        r.then_inc(mm.h, 1)
                    elif op == "sum":
                        e.wait_ge(dv.h, plan[("mask", h, p)])
                        for ka in (2 * p, 2 * p + 1):
                            nq = nq_of(ka)
                            sl = (ka % 4) * 1024
                            st0 = (ka == 0)
                            n1 = min(nq, 512)
                            lhsT = AR0[:, (kv * 32 + ka) * 128:
                                       (kv * 32 + ka + 1) * 128]
                            r = e.matmul(ps_r1[0:1, 0:n1], ones_col[:],
                                         exps[:, sl:sl + n1],
                                         start=st0, stop=(ka == 31))
                            r = e.matmul(ps_o[:, 0:n1], lhsT, exps[:, sl:sl + n1],
                                         start=st0, stop=(ka == 31))
                            if nq > 512:
                                r = e.matmul(ps_r2[0:1, 0:nq - 512], ones_col[:],
                                             exps[:, sl + 512:sl + nq],
                                             start=st0, stop=(ka == 15))
                                r = e.matmul(ps_o[:, 512:nq], lhsT,
                                             exps[:, sl + 512:sl + nq],
                                             start=st0, stop=(ka == 15))
                        r.then_inc(mm.h, 1)
                    else:
                        e.wait_ge(ac.h, plan[("otev", h - 1)])
                        e.matmul(ps_o[:, 0:512], onesf[:], rrow[:, 0:512],
                                 start=True, stop=True)
                        e.matmul(ps_o[:, 512:1024], onesf[:],
                                 rrow[:, 512:1024],
                                 start=True, stop=True).then_inc(mm.h, 1)
            for op, p in pe_ops_of_head(h):
                v = mm.inc()
                key = ("hrb", h - 1) if op == "hrbP" else (op, h, p)
                assert v == plan[key], (h, op, p)
            blk.tensor(pe_head)

            def act_head(e, h=h, dc=DV_C):
                if h == 0:
                    e.wait_ge(dv.h, dc)
                for p in range(NP):
                    e.wait_ge(mm.h, plan[("smm", h, p)])
                    r = None
                    for ka in (2 * p, 2 * p + 1):
                        nq = nq_of(ka)
                        sl = (ka % 4) * 1024
                        r = e.activation(exps[:, sl:sl + nq],
                                         pss[ka % 2][:, 0:nq], AF.Exp,
                                         bias=consts[:, 0:1], scale=SC1)
                    r.then_inc(ac.h, 1)
                e.wait_ge(mm.h, plan[("sum", h, NP - 1)])
                e.activation(rvec[:, 0:512], ps_r1[0:1, 0:512], AF.Ln,
                             bias=consts[0:1, 1:2], scale=1.0).then_inc(ac.h, 1)
                e.activation(rvec[:, 512:1024], ps_r2[0:1, 0:512], AF.Ln,
                             bias=consts[0:1, 1:2], scale=1.0).then_inc(ac.h, 1)
                e.wait_ge(ac.h, plan[("svexp", h)] - 1)
                e.activation(rrow[:, 0:1024], rvec[:, 0:1024], AF.Exp,
                             bias=consts[0:1, 1:2], scale=-1.0).then_inc(ac.h, 1)
                e.copy(AR1[:, h * 1024:(h + 1) * 1024], ps_o[:]) \
                    .then_inc(ac.h, 1)
            for p in range(NP):
                v = ac.inc()
                assert v == plan[("exp", h, p)], (h, p)
            assert ac.inc(4) == plan[("otev", h)], h
            blk.scalar(act_head)

            def dve_head(e, h=h):
                for p in range(NP):
                    e.wait_ge(ac.h, plan[("exp", h, p)])
                    nq = nq_of(2 * p)
                    sl = (2 * p % 4) * 1024
                    e.tensor_tensor(
                        exps[:, sl:sl + 2048]
                        .rearrange("q (k n) -> q k n", n=1024)[:, :, nq - 128:nq],
                        exps[:, sl:sl + 2048]
                        .rearrange("q (k n) -> q k n", n=1024)[:, :, nq - 128:nq],
                        maskb[:, 2 * p * 128:(2 * p + 2) * 128]
                        .rearrange("q (k n) -> q k n", n=128),
                        ALU.mult).then_inc(dv.h, 1)
                e.wait_ge(mm.h, plan[("hrb", h)])
                e.wait_ge(ac.h, plan[("otev", h)])
                e.tensor_tensor(AR1[:, h * 1024:(h + 1) * 1024],
                                AR1[:, h * 1024:(h + 1) * 1024],
                                ps_o[:], ALU.mult).then_inc(dv.h, 1)
            for p in range(NP):
                v = dv.inc()
                assert v == plan[("mask", h, p)], (h, p)
            assert dv.inc() == plan[("omult", h)], h
            blk.vector(dve_head)

        def pe_fin(e, hl=NH - 1):
            e.wait_ge(ac.h, plan[("otev", hl)])
            e.matmul(ps_o[:, 0:512], onesf[:], rrow[:, 0:512],
                     start=True, stop=True)
            e.matmul(ps_o[:, 512:1024], onesf[:], rrow[:, 512:1024],
                     start=True, stop=True).then_inc(mm.h, 1)
        assert mm.inc() == plan[("hrb", NH - 1)]
        blk.tensor(pe_fin)
        ATT_DV = dv.v
        ATT_MM = mm.v
        last_exp_slot = [plan[("exp", NH - 1, NP - 1)],
                         plan[("exp", NH - 1, NP - 1)]]

        psw = [ps_o, ps_s0]
        wo_win = []
        wo_mm = []
        gate_dv = []
        out_st = []
        for og in range(8):
            def wwin(e, og=og, att=ATT_MM,
                     lastmm=(wo_mm[8 * (og - 2) + 7] if og >= 2 else 0),
                     _dpgw=gw.v):
                _cgw = _dpgw
                e.wait_ge(mm.h, max(att, lastmm))
                e.wait_ge(gw.h, _cgw)
                e.dma_start(out=AR0[:, (og % 2) * 16384:(og % 2 + 1) * 16384]
                            .rearrange("p (dc f) -> p dc f", f=512),
                            in_=wo1_in[og, :, :, :].rearrange("dc p f -> p dc f")) \
                    .then_inc(gw.h, 16)
                _cgw += 16
            blk.gpsimd(wwin)
            wo_win.append(gw.inc(16))

            for tok in range(8):
                sl = ((og * 8 + tok) % 4) * 512

                def gld(e, og=og, tok=tok, sl=sl,
                        gp=(gate_dv[-4] if len(gate_dv) >= 4 else 0), _dpld=ld.v):
                    _cld = _dpld
                    if gp:
                        e.wait_ge(dv.h, gp)
                    e.wait_ge(ld.h, _cld)
                    e.dma_start(out=x1t[:, sl:sl + 512],
                                in_=x1_in[tok * 128:(tok + 1) * 128,
                                          og * 512:(og + 1) * 512]).then_inc(ld.h, 16)
                    _cld += 16
                    e.wait_ge(ld.h, _cld)
                    e.dma_start(out=g1t[:, sl:sl + 512],
                                in_=g1_in[tok * 128:(tok + 1) * 128,
                                          og * 512:(og + 1) * 512]).then_inc(ld.h, 16)
                    _cld += 16
                blk.sync(gld)
                gv = ld.inc(32)

                def wmm(e, og=og, tok=tok, t=ATT_DV, wv=wo_win[og],
                        ae=(last_exp_slot[0] if tok % 2 == 1 else 0),
                        gp=(gate_dv[-2] if len(gate_dv) >= 2 else 0)):
                    e.wait_ge(dv.h, t)
                    e.wait_ge(gw.h, wv)
                    if ae:
                        e.wait_ge(ac.h, ae)
                    if gp:
                        e.wait_ge(dv.h, gp)
                    r = None
                    for dc in range(32):
                        lhsT = AR1[:, dc * 1024 + tok * 128:dc * 1024 + (tok + 1) * 128]
                        rhs = AR0[:, (og % 2) * 16384 + dc * 512:
                                  (og % 2) * 16384 + (dc + 1) * 512]
                        r = e.matmul(psw[tok % 2][:, 0:512], lhsT, rhs,
                                     start=(dc == 0), stop=(dc == 31))
                    r.then_inc(mm.h, 1)
                wo_mm.append(mm.inc())
                blk.tensor(wmm)

                def gate(e, og=og, tok=tok, sl=sl, t=wo_mm[-1], gv=gv, base=dv.v,
                         sp=(out_st[-4] if len(out_st) >= 4 else 0)):
                    e.wait_ge(mm.h, t)
                    e.wait_ge(ld.h, gv)
                    if sp:
                        e.wait_ge(st.h, sp)
                    e.tensor_tensor(ot1[:, sl:sl + 512], psw[tok % 2][:, 0:512],
                                    g1t[:, sl:sl + 512], ALU.mult).then_inc(dv.h, 1)
                    e.wait_ge(dv.h, base + 1)
                    e.tensor_tensor(ot1[:, sl:sl + 512], ot1[:, sl:sl + 512],
                                    x1t[:, sl:sl + 512], ALU.add).then_inc(dv.h, 1)
                gate_dv.append(dv.inc(2))
                blk.vector(gate)

                def ost(e, og=og, tok=tok, sl=sl, t=gate_dv[-1], _dpst=st.v):
                    _cst = _dpst
                    e.wait_ge(dv.h, t)
                    e.wait_ge(st.h, _cst)
                    e.dma_start(out=out_out[tok * 128:(tok + 1) * 128,
                                            og * 512:(og + 1) * 512],
                                in_=ot1[:, sl:sl + 512]).then_inc(st.h, 16)
                    _cst += 16
                blk.sync(ost)
                out_st.append(st.inc(16))

        def fin(e, t=st.v):
            e.wait_ge(st.h, t)
        blk.gpsimd(fin)

    return nc


def _prep_launch3(inputs, r2):
    perm = batch_perm()
    Wo1T = np.asarray(inputs["L1_attn_Wo"], np.float32).T
    wo1t = np.ascontiguousarray(
        Wo1T.reshape(32, 128, 8, 512).transpose(2, 0, 1, 3))
    k1b_b, v1_b = [], []
    for b in range(B):
        K1_loc = np.concatenate(
            [np.asarray(r2[4 * b + cc]["k1T"]).reshape(1024, NTOK)
             for cc in range(4)], axis=1)
        K1_full = np.ascontiguousarray(K1_loc[:, perm])
        k1b_b.append(np.ascontiguousarray(K1_full.reshape(8, 128, S)))
        V1_loc = np.concatenate(
            [np.asarray(r2[4 * b + cc]["v1T"]).reshape(1024, NTOK)
             for cc in range(4)], axis=1)
        V1_full = np.ascontiguousarray(V1_loc[:, perm].T)
        v1_b.append(np.ascontiguousarray(
            V1_full.reshape(32, 128, 8, 128).transpose(1, 2, 0, 3).reshape(128, 32768)))
    in_maps = []
    for c in range(8):
        b, cc = divmod(c, 4)
        masks = causal_masks(cc)
        maskb = np.ascontiguousarray(masks.transpose(1, 0, 2).reshape(128, 4096))
        in_maps.append({
            "x1": np.asarray(r2[c]["x1"]),
            "g1": np.asarray(r2[c]["g1"]),
            "q1T": np.asarray(r2[c]["q1T"]),
            "k1": k1b_b[b],
            "v1": v1_b[b],
            "wo1": wo1t,
            "mask": maskb,
        })
    return in_maps


def _prep_launch2(inputs, r1):
    perm = batch_perm()
    x = np.asarray(inputs["L0_attn_x"], np.float32).reshape(B * S, H)
    Wo = np.asarray(inputs["L0_attn_Wo"], np.float32)
    wo2 = np.ascontiguousarray(Wo.reshape(8, 128, 4, 1024).transpose(2, 0, 1, 3))
    bo = np.ascontiguousarray(np.asarray(inputs["L0_attn_bo"], np.float32).reshape(1, H))
    bffn = np.ascontiguousarray(np.asarray(inputs["L0_ffn_bias"], np.float32).reshape(1, H))
    Wq = np.asarray(inputs["L1_attn_Wq"], np.float32)
    Wk = np.asarray(inputs["L1_attn_Wk"], np.float32)
    Wv = np.asarray(inputs["L1_attn_Wv"], np.float32)
    W1T = np.concatenate([Wq.T, Wk.T, Wv.T], axis=1)
    w1t = np.ascontiguousarray(
        W1T.reshape(32, 128, 48, 128).transpose(2, 1, 0, 3).reshape(48, 128, H))
    Wr = np.asarray(inputs["L1_ffn_W_router"], np.float32)
    wr2 = np.ascontiguousarray(Wr.T.reshape(32, 128, 8).transpose(1, 0, 2).reshape(128, 256))
    A_stack = np.asarray(inputs["L1_ffn_A_stack"], np.float32)
    AT = A_stack.transpose(2, 0, 1).reshape(H, 128)
    at2 = np.ascontiguousarray(AT.reshape(32, 128, 128).transpose(1, 0, 2).reshape(128, H))
    B_stack = np.asarray(inputs["L1_ffn_B_stack"], np.float32)
    brr = np.ascontiguousarray(B_stack.transpose(0, 2, 1).reshape(128, H))
    WbT = np.asarray(inputs["L1_ffn_W_base"], np.float32).T
    wb2 = np.ascontiguousarray(
        WbT.reshape(32, 128, 8, 512).transpose(2, 1, 0, 3).reshape(8, 128, 16384))
    bb = np.ascontiguousarray(np.asarray(inputs["L1_ffn_b_base"], np.float32).reshape(1, H))
    sel = np.zeros((8, 128), np.float32)
    for er in range(128):
        sel[er // 16, er] = 2.0
    ident = np.eye(128, dtype=BF)

    kt2b, v2b = [], []
    for b in range(B):
        KT_loc = np.concatenate(
            [np.asarray(r1[4 * b + cc]["qkvT"])[8:16].reshape(1024, NTOK)
             for cc in range(4)], axis=1)
        KT_full = KT_loc[:, perm]
        kt2b.append(np.ascontiguousarray(
            KT_full.reshape(8, 128, 32, 128).transpose(2, 1, 0, 3).reshape(32, 128, 1024)))
        VT_loc = np.concatenate(
            [np.asarray(r1[4 * b + cc]["qkvT"])[16:24].reshape(1024, NTOK)
             for cc in range(4)], axis=1)
        V_full = np.ascontiguousarray(VT_loc[:, perm].T)
        v2b.append(np.ascontiguousarray(
            V_full.reshape(32, 128, 8, 128).transpose(1, 0, 2, 3).reshape(128, 32768)))

    wl_full = np.concatenate(
        [np.asarray(r1[c]["wlT"]).reshape(H, 512) for c in range(8)], axis=1)
    wl2 = np.ascontiguousarray(
        wl_full.reshape(32, 128, 8, 512).transpose(2, 1, 0, 3).reshape(8, 128, 16384))

    in_maps = []
    for c in range(8):
        b = c // 4
        rows = core_rows(c)
        cosT, sinT = rope_tables(rows % S)
        in_maps.append({
            "x": np.ascontiguousarray(x[rows]),
            "xnT": np.asarray(r1[c]["xnT"]),
            "qT": np.ascontiguousarray(np.asarray(r1[c]["qkvT"])[0:8]),
            "kt": kt2b[b],
            "v": v2b[b],
            "wl": wl2,
            "wo": wo2,
            "bo": bo,
            "bffn": bffn,
            "w1": w1t,
            "cosT": cosT,
            "sinT": sinT,
            "wrT": wr2,
            "at2": at2,
            "brr": brr,
            "wb": wb2,
            "bb": bb,
            "sel": sel,
            "ident": ident,
        })
    return in_maps

def kernel(**inputs):
    if "nc1" not in _CACHE:
        _CACHE["nc1"] = build_launch1()
        _CACHE["nc2"] = build_launch2()
        _CACHE["nc3"] = build_launch3()
    im1 = _prep_launch1(inputs)
    r1 = run_bass_kernel_spmd(_CACHE["nc1"], im1, core_ids=list(range(8))).results
    im2 = _prep_launch2(inputs, r1)
    r2 = run_bass_kernel_spmd(_CACHE["nc2"], im2, core_ids=list(range(8))).results
    im3 = _prep_launch3(inputs, r2)
    r3 = run_bass_kernel_spmd(_CACHE["nc3"], im3, core_ids=list(range(8))).results
    out = np.empty((B * S, H), np.float32)
    for c in range(8):
        out[core_rows(c)] = np.asarray(r3[c]["out"], np.float32)
    return out.reshape(B, S, H)
